# version 10
# baseline (speedup 1.0000x reference)
import numpy as np
import ml_dtypes
from contextlib import ExitStack
import concourse.bass as bass
import concourse.mybir as mybir
from concourse.bass_utils import run_bass_kernel_spmd

F32 = mybir.dt.float32
BF16 = mybir.dt.bfloat16
I32 = mybir.dt.int32
AF = mybir.ActivationFunctionType
ALU = mybir.AluOpType
AX = mybir.AxisListType

D = 2048
T = 2048
NT = 512
NTAU = T // NT
KC = D // 128
I0 = 5504
MC0 = I0 // 128
E = 8
LN_EPS = 1e-5
RMS_EPS = 1e-6
DENOM1 = float(np.sqrt(np.float32(2048.0)) / np.float32(2048.0))
ROOT = float(np.float32(2048.0) ** np.float32(-0.5))
BIG = 1e30


def to_slabs(w, nk, nm):
    K, M = w.shape
    assert K == nk * 128 and M == nm * 128
    return np.ascontiguousarray(
        w.reshape(nk, 128, nm, 128).transpose(2, 1, 0, 3).reshape(nm, 128, nk * 128))


def col_pp(v, n):
    return np.ascontiguousarray(np.asarray(v, np.float32).reshape(n, 128).T)


class Sem:
    def __init__(self, handle):
        self.h = handle
        self.n = 0


class Prog:
    def __init__(self):
        self.ops = {k: [] for k in ("sync", "scalar", "vector", "tensor", "gpsimd")}
        self.auto = {}

    def _add(self, eng, fn, inc=None):
        asem = self.auto.get(eng)
        if asem is not None:
            if asem.n > 0:
                self.ops[eng].append(lambda e, s=asem.h, v=asem.n: e.wait_ge(s, v))
            if inc is None:
                inc = (asem, 1)
        if inc is None:
            self.ops[eng].append(fn)
        else:
            sem, k = inc
            sem.n += k
            self.ops[eng].append(lambda e, fn=fn, s=sem.h, k=k: fn(e).then_inc(s, k))

    def vec(self, fn, inc=None):
        self._add("vector", fn, inc)

    def act(self, fn, inc=None):
        self._add("scalar", fn, inc)

    def pe(self, fn, inc=None):
        self._add("tensor", fn, inc)

    def dma(self, q, sem, fn):
        self._add(q, fn, (sem, 16))

    def wait(self, eng, sem, val=None):
        v = sem.n if val is None else val
        if v > 0:
            self.ops[eng].append(lambda e, s=sem.h, v=v: e.wait_ge(s, v))


def build_program(layers=(0, 1), x1_out=False):
    nc = bass.Bass("TRN2", target_bir_lowering=False)
    dtn = {}

    def dram(name, shape, dtype=F32, kind="ExternalInput"):
        dtn[name] = nc.dram_tensor(name, list(shape), dtype, kind=kind).ap()
        return dtn[name]

    x0 = dram("x0", [D, T])
    if 0 in layers:
        dram("win_sl", [48, 128, KC * 128])
        dram("bin_pp", [128, 48])
        dram("lnw_pp", [128, KC]); dram("lnb_pp", [128, KC])
        dram("wout_sl", [KC, 128, KC * 128])
        dram("bout_pp", [128, KC])
        dram("lut_row", [1, 16])
        dram("gu_my", [11, 128, KC * 128], I32)
        dram("d_my", [2, 128, MC0 * 128], I32)
        dram("cs_pp", [128, 8, T])
        dram("win_b", [48, 128, KC * 128], BF16, kind="Internal")
        dram("wout_b", [KC, 128, KC * 128], BF16, kind="Internal")
        dram("gu_loc", [11 * 128, KC * 128], BF16, kind="Internal")
        dram("d_loc", [2 * 128, MC0 * 128], BF16, kind="Internal")
        dram("gu_all", [88 * 128, KC * 128], BF16, kind="Internal")
        dram("d_all", [16 * 128, MC0 * 128], BF16, kind="Internal")
    if 1 in layers:
        dram("wq_sl", [KC, 128, KC * 128]); dram("wk_sl", [KC, 128, KC * 128])
        dram("wv_sl", [KC, 128, KC * 128]); dram("wo_sl", [KC, 128, KC * 128])
        dram("pre1_pp", [128, KC]); dram("pre2_pp", [128, KC])
        dram("post1_pp", [128, KC]); dram("post2_pp", [128, KC]); dram("post_pp", [128, KC])
        dram("rsc_pp", [128, KC])
        dram("rw_sl", [1, 128, KC * 8])
        dram("pes_row", [1, E])
        dram("wg1_sl", [64, 128, KC * 128]); dram("wu1_sl", [64, 128, KC * 128])
        dram("wd1_sl", [KC, 128, 64 * 128])
        dram("weg_sl", [E, 8, 128, KC * 128]); dram("weu_sl", [E, 8, 128, KC * 128])
        dram("wed_sl", [KC, 128, 64 * 128])
        dram("wq_b", [KC, 128, KC * 128], BF16, kind="Internal")
        dram("wk_b", [KC, 128, KC * 128], BF16, kind="Internal")
        dram("wv_b", [KC, 128, KC * 128], BF16, kind="Internal")
        dram("wo_b", [KC, 128, KC * 128], BF16, kind="Internal")
        dram("wg1_b", [64, 128, KC * 128], BF16, kind="Internal")
        dram("wu1_b", [64, 128, KC * 128], BF16, kind="Internal")
        dram("wd1_b", [KC, 128, 64 * 128], BF16, kind="Internal")
        dram("weg_b", [E, 8, 128, KC * 128], BF16, kind="Internal")
        dram("weu_b", [E, 8, 128, KC * 128], BF16, kind="Internal")
        dram("wed_b", [KC, 128, 64 * 128], BF16, kind="Internal")
        dram("h1_scr", [64 * 128, T], BF16, kind="Internal")
        dram("hew_scr", [64 * 128, T], BF16, kind="Internal")
        dram("dno_scr", [D, T], kind="Internal")
        dram("mo_scr", [D, T], kind="Internal")
        dram("s_scr", [D, T], kind="Internal")
        dram("acc_scr", [D, T], kind="Internal")
    x1_kind = "ExternalOutput" if x1_out else "Internal"
    dram("x1_scr", [D, T], kind=x1_kind)
    if 1 in layers:
        dram("out", [D, T], kind="ExternalOutput")

    ones_k_c = nc.inline_tensor(np.ones((128, 1), dtype=np.float32), "ones_k").ap()
    ones_r_c = nc.inline_tensor(np.ones((1, 128), dtype=np.float32), "ones_r").ap()
    ident_c = nc.inline_tensor(np.eye(128, dtype=np.float32), "ident_c").ap()
    eps_c = nc.inline_tensor(np.array([[LN_EPS, RMS_EPS, 0.0, 0.0]], dtype=np.float32),
                             "eps_c").ap()

    P = Prog()
    with ExitStack() as octx:
        def sb(ctx, name, shape, dtp=F32):
            return ctx.enter_context(nc.sbuf_tensor(name, list(shape), dtp))

        smisc = Sem(octx.enter_context(nc.semaphore("smisc")))
        sdx = [Sem(octx.enter_context(nc.semaphore(f"sdx{i}"))) for i in range(2)]
        sdw = [Sem(octx.enter_context(nc.semaphore(f"sdw{i}"))) for i in range(2)]
        gdw = [Sem(octx.enter_context(nc.semaphore(f"gdw{i}"))) for i in range(2)]
        sdo = [Sem(octx.enter_context(nc.semaphore(f"sdo{i}"))) for i in range(2)]
        sdd = [Sem(octx.enter_context(nc.semaphore(f"sdd{i}"))) for i in range(2)]
        gdd = [Sem(octx.enter_context(nc.semaphore(f"gdd{i}"))) for i in range(2)]
        sca = [Sem(octx.enter_context(nc.semaphore(f"sca{i}"))) for i in range(2)]
        ccs = Sem(octx.enter_context(nc.semaphore("ccs")))
        scp = [Sem(octx.enter_context(nc.semaphore(f"scp{i}"))) for i in range(4)]
        sco = [Sem(octx.enter_context(nc.semaphore(f"sco{i}"))) for i in range(4)]
        sv = Sem(octx.enter_context(nc.semaphore("sv")))
        sa = Sem(octx.enter_context(nc.semaphore("sa")))
        sp = Sem(octx.enter_context(nc.semaphore("sp")))
        P.auto = {"vector": sv, "scalar": sa}

        pa = octx.enter_context(nc.psum_tensor("pa", [128, NT], F32))
        pb = octx.enter_context(nc.psum_tensor("pb", [128, NT], F32))
        pc_ = octx.enter_context(nc.psum_tensor("pc", [128, NT], F32))
        pd_ = octx.enter_context(nc.psum_tensor("pd", [128, NT], F32))
        pr1 = octx.enter_context(nc.psum_tensor("pr1", [1, NT], F32))
        pr2 = octx.enter_context(nc.psum_tensor("pr2", [1, NT], F32))
        pbc = octx.enter_context(nc.psum_tensor("pbc", [128, NT], F32))
        pt = octx.enter_context(nc.psum_tensor("pt", [128, NT], F32))

        ppar = sb(octx, "ppar", [128, 224])
        lutc = sb(octx, "lutc", [128, 16])
        acol = sb(octx, "acol", [128, KC])
        pes_b = sb(octx, "pes_b", [128, E])
        onesk_s = sb(octx, "onesk_s", [128, 1])
        onesr_s = sb(octx, "onesr_s", [1, 128])
        ident_s = sb(octx, "ident_s", [128, 128])
        sc1 = sb(octx, "sc1", [128, NT]); sc2 = sb(octx, "sc2", [128, NT])
        sc3 = sb(octx, "sc3", [128, NT]); sc4 = sb(octx, "sc4", [128, NT])
        sc5 = sb(octx, "sc5", [128, NT])
        r1 = sb(octx, "r1", [1, NT]); r2 = sb(octx, "r2", [1, NT]); r3 = sb(octx, "r3", [1, NT])
        eps_s = sb(octx, "eps_s", [1, 4])
        bcA = sb(octx, "bcA", [128, NT]); bcB = sb(octx, "bcB", [128, NT])
        rs_b = sb(octx, "rs_b", [128, T]); mr_b = sb(octx, "mr_b", [128, T])
        rrms_b = sb(octx, "rrms_b", [128, T])
        wsl = [sb(octx, f"wsl{i}", [128, KC * 128], BF16) for i in range(2)]
        rtm = sb(octx, "rtm", [128, KC * 8])
        raff = sb(octx, "raff", [128, KC * 8])
        rsc1 = sb(octx, "rsc1", [128, KC * 8])
        rcol = sb(octx, "rcol", [128, 12])

        PP = {}
        _pp_next = [0]

        def pp_alloc(name, n):
            PP[name] = _pp_next[0]
            _pp_next[0] += n

        def pcol(name, i):
            o = PP[name] + i
            return ppar[:, o:o + 1]

        if 0 in layers:
            pp_alloc("bin", 48); pp_alloc("lnw", KC); pp_alloc("lnb", KC); pp_alloc("bout", KC)
        if 1 in layers:
            pp_alloc("pre1", KC); pp_alloc("pre2", KC); pp_alloc("post1", KC)
            pp_alloc("post2", KC); pp_alloc("post", KC); pp_alloc("rsc", KC)

        def dma_s(dst, src, sem):
            P.dma("sync", sem, lambda e, d=dst, s=src: e.dma_start(out=d, in_=s))

        def dma_g(dst, src, sem):
            P.dma("gpsimd", sem, lambda e, d=dst, s=src: e.dma_start(out=d, in_=s))

        def load_slab_b(dst_tile, src_ap, b):
            P.dma("scalar", sca[b],
                  lambda e, d=dst_tile[:], s2=src_ap: e.dma_start(out=d, in_=s2))
            return sca[b].n

        l0w_rdy = [0]

        def preconvert():
            specs = []
            if 0 in layers:
                specs += [("win_sl", "win_b", 48, KC * 128), ("wout_sl", "wout_b", KC, KC * 128)]
            if 1 in layers:
                specs += [("wq_sl", "wq_b", KC, KC * 128), ("wk_sl", "wk_b", KC, KC * 128),
                          ("wv_sl", "wv_b", KC, KC * 128), ("wo_sl", "wo_b", KC, KC * 128),
                          ("wg1_sl", "wg1_b", 64, KC * 128), ("wu1_sl", "wu1_b", 64, KC * 128),
                          ("wd1_sl", "wd1_b", KC, 64 * 128), ("wed_sl", "wed_b", KC, 64 * 128),
                          ("weg_sl", "weg_b", E * 8, KC * 128), ("weu_sl", "weu_b", E * 8, KC * 128)]
            CH = 4 * 1024 * 1024
            for si, (srcn, dstn, nm, F) in enumerate(specs):
                src = dtn[srcn]
                dst = dtn[dstn]
                tot = nm * 128 * F
                for off in range(0, tot, CH):
                    w = min(CH, tot - off)
                    P.dma("gpsimd", gdw[0],
                          lambda e, d=bass.AP(dst.tensor, off, [[1, 1], [1, w]]),
                          s2=bass.AP(src.tensor, off, [[1, 1], [1, w]]):
                          e.dma_start(out=d, in_=s2))
                if srcn == "wout_sl":
                    l0w_rdy[0] = gdw[0].n

        def chunk_ap(dram_ap, c, tau, rows=128, ncols=T, w=NT):
            return bass.AP(dram_ap.tensor, c * 128 * ncols + tau * w, [[ncols, rows], [1, w]])

        def chunk4_ap(dram_ap, q, tau):
            return bass.AP(dram_ap.tensor, q * 4 * 128 * T + tau * NT,
                           [[T, 128], [128 * T, 4], [1, NT]])

        _xb_marks = {}

        def stream4(src_dram, tau, consume, xb, extra=None):
            key = id(xb[0])
            marks = _xb_marks.setdefault(key, [(0, 0, 0), (0, 0, 0)])
            for q in range(4):
                b = q % 2
                mv, ma, mp = marks[b]
                P.wait("sync", sv, mv)
                P.wait("sync", sa, ma)
                P.wait("sync", sp, mp)
                if extra is not None:
                    extra()
                dma_s(xb[b][:].rearrange("p (c j) -> p c j", j=NT),
                      chunk4_ap(src_dram, q, tau), sdx[b])
                ld = sdx[b].n
                for j in range(4):
                    consume(4 * q + j, xb[b][:, j * NT:(j + 1) * NT], b, ld)
                marks[b] = (sv.n, sa.n, sp.n)

        def cm(tile, c, tau=None, w=NT):
            if tau is None:
                return tile[:, c * w:(c + 1) * w]
            return tile[:, (c * NTAU + tau) * w:(c * NTAU + tau + 1) * w]

        def bcast(row_ap, dst):
            P.wait("tensor", sv)
            P.pe(lambda t, r=row_ap: t.matmul(pbc[:], onesr_s[:], r, start=True, stop=True),
                 inc=(sp, 1))
            P.wait("vector", sp)
            P.vec(lambda v, d=dst: v.tensor_copy(out=d, in_=pbc[:]), inc=(sv, 1))

        mm_banks = [pa, pb, pc_, pd_]
        bank_free_at = [(0, 0)] * 4
        bank_ctr = [0]

        def mm_group(rhs_views, lhsT_views, evac, bank_pair=0, pre=None):
            g = bank_ctr[0]; bank_ctr[0] += 1
            bi = bank_pair * 2 + (g % 2)
            bank = mm_banks[bi]
            P.wait("tensor", sv, bank_free_at[bi][0])
            P.wait("tensor", sa, bank_free_at[bi][1])
            if pre is not None:
                pre()
            nk = len(rhs_views)
            for i in range(nk):
                P.pe(lambda t, rv=rhs_views[i], lv=lhsT_views[i], i=i, nk=nk, bank=bank:
                     t.matmul(bank[:], lv, rv, start=(i == 0), stop=(i == nk - 1)),
                     inc=(sp, 1) if i == nk - 1 else None)
            P.wait("vector", sp)
            evac(bank)
            bank_free_at[bi] = (sv.n, sa.n)

        def _stats_chunk(c, view, sq_ping, sq_done_at, do_sum):
            if do_sum:
                P.pe(lambda t, v=view, c=c: t.matmul(pr1[:], onesk_s[:], v,
                                                     start=(c == 0), stop=(c == KC - 1)),
                     inc=(sp, 1) if c == KC - 1 else None)
            sq = sq_ping[c % 2]
            P.wait("scalar", sp, sq_done_at[c % 2])
            P.act(lambda s, v=view, sq=sq: s.activation(out=sq[:], in_=v, func=AF.Square),
                  inc=(sa, 1))
            P.wait("tensor", sa)
            P.pe(lambda t, sq=sq, c=c: t.matmul(pr2[:], onesk_s[:], sq[:],
                                                start=(c == 0), stop=(c == KC - 1)),
                 inc=(sp, 1))
            sq_done_at[c % 2] = sp.n

        def stats_pass(src_dram, do_sum, per_chunk=None, xb=None):
            sq_ping = [sc1, sc2]
            sq_done_at = [0, 0]
            P.wait("tensor", sv, pr_free[0])
            if per_chunk is None:
                def consume(c, view, b, ld):
                    P.wait("tensor", sdx[b], ld)
                    P.wait("scalar", sdx[b], ld)
                    _stats_chunk(c, view, sq_ping, sq_done_at, do_sum)
                stream4(src_dram, stats_pass.tau, consume, xb)
            else:
                for c in range(KC):
                    view, ready = per_chunk(c)
                    if ready is not None:
                        ready()
                    _stats_chunk(c, view, sq_ping, sq_done_at, do_sum)
        stats_pass.tau = 0

        pr_free = [0]

        def rowmath_musd(eps, out_rs_in_r2, with_mu=True):
            P.wait("vector", sp)
            if with_mu:
                P.vec(lambda v: v.tensor_scalar(out=r1[:], in0=pr1[:], scalar1=1.0 / D,
                                                scalar2=None, op0=ALU.mult))
            P.vec(lambda v: v.tensor_scalar(out=r2[:], in0=pr2[:], scalar1=1.0 / D,
                                            scalar2=None, op0=ALU.mult), inc=(sv, 1))
            P.wait("scalar", sv)
            if with_mu:
                P.act(lambda s: s.activation(out=r3[:], in_=r1[:], func=AF.Square), inc=(sa, 1))
                P.wait("vector", sa)
                P.vec(lambda v: v.tensor_tensor(out=r2[:], in0=r2[:], in1=r3[:],
                                                op=ALU.subtract), inc=(sv, 1))
                P.wait("scalar", sv)
            P.act(lambda s, eps=eps: s.activation(out=r3[:], in_=r2[:], func=AF.Sqrt,
                                                  bias=eps_s[0:1, eps:eps + 1]), inc=(sa, 1))
            P.wait("vector", sa)
            P.vec(lambda v: v.reciprocal(out=r2[:], in_=r3[:]), inc=(sv, 1))
            pr_free[0] = sv.n

        def load_params():
            if 0 in layers:
                dma_s(ppar[:, PP["bin"]:PP["bin"] + 48], dtn["bin_pp"], smisc)
                dma_s(ppar[:, PP["lnw"]:PP["lnw"] + KC], dtn["lnw_pp"], smisc)
                dma_s(ppar[:, PP["lnb"]:PP["lnb"] + KC], dtn["lnb_pp"], smisc)
                dma_s(ppar[:, PP["bout"]:PP["bout"] + KC], dtn["bout_pp"], smisc)
            if 1 in layers:
                dma_s(ppar[:, PP["pre1"]:PP["pre1"] + KC], dtn["pre1_pp"], smisc)
                dma_s(ppar[:, PP["pre2"]:PP["pre2"] + KC], dtn["pre2_pp"], smisc)
                dma_s(ppar[:, PP["post1"]:PP["post1"] + KC], dtn["post1_pp"], smisc)
                dma_s(ppar[:, PP["post2"]:PP["post2"] + KC], dtn["post2_pp"], smisc)
                dma_s(ppar[:, PP["post"]:PP["post"] + KC], dtn["post_pp"], smisc)
                dma_s(ppar[:, PP["rsc"]:PP["rsc"] + KC], dtn["rsc_pp"], smisc)
            dma_s(eps_s[:], eps_c, smisc)
            dma_s(onesk_s[:], ones_k_c, smisc)
            dma_s(onesr_s[:], ones_r_c, smisc)
            dma_s(ident_s[:], ident_c, smisc)
            if 0 in layers:
                dma_s(r1[0:1, 0:16], dtn["lut_row"], smisc)
                P.wait("tensor", smisc)
                P.pe(lambda t: t.matmul(pbc[:, 0:16], onesr_s[:], r1[0:1, 0:16],
                                        start=True, stop=True), inc=(sp, 1))
                P.wait("vector", sp)
                P.vec(lambda v: v.tensor_copy(out=lutc[:], in_=pbc[:, 0:16]), inc=(sv, 1))
                P.wait("vector", sv)
            if 1 in layers:
                P.wait("sync", sp)
                dma_s(r1[0:1, 0:E], dtn["pes_row"], smisc)
                P.wait("tensor", smisc)
                P.wait("tensor", sv)
                P.pe(lambda t: t.matmul(pbc[:, 0:E], onesr_s[:], r1[0:1, 0:E],
                                        start=True, stop=True), inc=(sp, 1))
                P.wait("vector", sp)
                P.vec(lambda v: v.tensor_copy(out=pes_b[:], in_=pbc[:, 0:E]), inc=(sv, 1))
                P.wait("vector", sv)
            P.wait("vector", smisc)
            P.wait("scalar", smisc)
            P.wait("tensor", smisc)

        def ghost_mat(ctx):
            HF = MC0 * 128 // 2
            idxb = [sb(ctx, f"idxb{i}", [128, HF], I32) for i in range(2)]
            idbf = sb(ctx, "idbf", [128, HF], BF16)
            gacc = sb(ctx, "gacc0", [128, HF], BF16)
            gtmp = sb(ctx, "gtmp", [128, HF], BF16)
            work = []
            for si2 in range(11):
                src = dtn["gu_my"][si2]
                dst = bass.AP(dtn["gu_loc"].tensor, si2 * 128 * KC * 128,
                              [[KC * 128, 128], [1, KC * 128]])
                work.append((src, dst, KC * 128))
            for si2 in range(2):
                for h in range(2):
                    src = bass.AP(dtn["d_my"].tensor, si2 * 128 * MC0 * 128 + h * HF,
                                  [[MC0 * 128, 128], [1, HF]])
                    dst = bass.AP(dtn["d_loc"].tensor, si2 * 128 * MC0 * 128 + h * HF,
                                  [[MC0 * 128, 128], [1, HF]])
                    work.append((src, dst, HF))
            acc_store = [0]
            conv_done = {}
            for si, (src, dst, F) in enumerate(work):
                b = si % 2
                if si >= 2:
                    P.wait("scalar", sv, conv_done[si - 2])
                P.dma("scalar", sdw[b],
                      lambda e, d=idxb[b][:, 0:F], s2=src: e.dma_start(out=d, in_=s2))
                P.wait("vector", sdw[b])
                P.vec(lambda v, b=b, F=F: v.tensor_copy(out=idbf[:, 0:F],
                                                        in_=idxb[b][:, 0:F]), inc=(sv, 1))
                conv_done[si] = sv.n
                P.wait("vector", sdo[0], acc_store[0])
                P.vec(lambda v, F=F: v.tensor_scalar(
                    out=gacc[:, 0:F], in0=idbf[:, 0:F], scalar1=0.0,
                    scalar2=lutc[:, 0:1], op0=ALU.is_equal, op1=ALU.mult))
                for k in range(1, 16):
                    P.vec(lambda v, k=k, F=F: v.tensor_scalar(
                        out=gtmp[:, 0:F], in0=idbf[:, 0:F], scalar1=float(k),
                        scalar2=lutc[:, k:k + 1], op0=ALU.is_equal, op1=ALU.mult))
                    P.vec(lambda v, F=F: v.tensor_tensor(
                        out=gacc[:, 0:F], in0=gacc[:, 0:F], in1=gtmp[:, 0:F],
                        op=ALU.add), inc=(sv, 1) if k == 15 else None)
                P.wait("scalar", sv)
                P.dma("scalar", sdo[0],
                      lambda e, d=dst, s2=gacc[:, 0:F]: e.dma_start(out=d, in_=s2))
                acc_store[0] = sdo[0].n
            P.wait("gpsimd", sdo[0])
            P._add("gpsimd", lambda e: e.collective_compute(
                "AllGather", ALU.bypass, replica_groups=[list(range(8))],
                ins=[dtn["gu_loc"]], outs=[dtn["gu_all"]]), inc=(ccs, 1))
            P._add("gpsimd", lambda e: e.collective_compute(
                "AllGather", ALU.bypass, replica_groups=[list(range(8))],
                ins=[dtn["d_loc"]], outs=[dtn["d_all"]]), inc=(ccs, 1))

        def layer0_stats_h(ctx, big_a):
            xn32 = sb(ctx, "xn32", [128, KC * NT])
            xb0 = [sb(ctx, "xb0a", [128, 4 * NT]), sb(ctx, "xb0b", [128, 4 * NT])]
            for tau in range(NTAU):
                stats_pass.tau = tau
                stats_pass(x0, do_sum=True, xb=xb0)
                rowmath_musd(0, r2)
                bcast(r2[:], rs_b[:, tau * NT:(tau + 1) * NT])
                P.vec(lambda v: v.tensor_tensor(out=r3[:], in0=r1[:], in1=r2[:], op=ALU.mult),
                      inc=(sv, 1))
                bcast(r3[:], mr_b[:, tau * NT:(tau + 1) * NT])
            for tau in range(NTAU):
                rsb = rs_b[:, tau * NT:(tau + 1) * NT]
                mrb = mr_b[:, tau * NT:(tau + 1) * NT]
                sq_ping = [sc1, sc2]
                sq_done_at = [0, 0]
                P.wait("tensor", sv, pr_free[0])

                def consume(c, view, b, ld, tau=tau, rsb=rsb, mrb=mrb,
                            sq_ping=sq_ping, sq_done_at=sq_done_at):
                    P.wait("vector", sdx[b], ld)
                    P.vec(lambda v, c=c, view=view, rsb=rsb: v.tensor_tensor(
                        out=cm(xn32, c), in0=view, in1=rsb, op=ALU.mult))
                    P.vec(lambda v, c=c, mrb=mrb: v.tensor_tensor(
                        out=cm(xn32, c), in0=cm(xn32, c), in1=mrb, op=ALU.subtract),
                        inc=(sv, 1))
                    P.wait("tensor", sv)
                    P.wait("scalar", sv)
                    _stats_chunk(c, cm(xn32, c), sq_ping, sq_done_at, True)
                stream4(x0, tau, consume, xb0)
                rowmath_musd(0, r2)
                bcast(r1[:], bcA[:])
                bcast(r2[:], bcB[:])
                for c in range(KC):
                    P.vec(lambda v, c=c: v.tensor_tensor(out=sc1[:], in0=cm(xn32, c),
                                                         in1=bcA[:], op=ALU.subtract))
                    P.vec(lambda v: v.tensor_tensor(out=sc1[:], in0=sc1[:], in1=bcB[:],
                                                    op=ALU.mult))
                    P.vec(lambda v, c=c, tau=tau: v.tensor_scalar(
                        out=cm(big_a, c, tau), in0=sc1[:], scalar1=pcol("lnw", c),
                        scalar2=pcol("lnb", c), op0=ALU.mult, op1=ALU.add),
                        inc=(sv, 1) if c == KC - 1 else None)

        def layer0_attn(ctx, big_a, big_b):
            kraw = sb(ctx, "kraw", [128, KC * NT], BF16)
            costl = sc3
            sintl = sc4
            h_rdy = sv.n
            wsl_mark = [0, 0]
            slab_ctr = [0]
            P.wait("scalar", gdw[0], l0w_rdy[0])
            P.vec(lambda v: v.memset(acol[:], 0.0), inc=(sv, 1))

            def load_slab(src_ap):
                i = slab_ctr[0]; slab_ctr[0] += 1
                b = i % 2
                P.wait("scalar", sp, wsl_mark[b])
                return b, load_slab_b(wsl[b], src_ap, b)

            def rope_pass(tile, tau, with_norm):
                for c in range(4):
                    cA, cB = c, c + 4
                    P.wait("sync", sv)
                    dma_s(costl[:], bass.AP(dtn["cs_pp"].tensor, c * T + tau * NT,
                                            [[8 * T, 128], [1, NT]]), sdx[0])
                    dma_s(sintl[:], bass.AP(dtn["cs_pp"].tensor, (4 + c) * T + tau * NT,
                                            [[8 * T, 128], [1, NT]]), sdx[1])
                    P.wait("vector", sdx[0])
                    P.wait("vector", sdx[1])
                    P.vec(lambda v, tile=tile, cA=cA, tau=tau: v.tensor_tensor(
                        out=sc1[:], in0=cm(tile, cA, tau), in1=costl[:], op=ALU.mult))
                    P.vec(lambda v, tile=tile, cB=cB, tau=tau: v.tensor_tensor(
                        out=sc2[:], in0=cm(tile, cB, tau), in1=sintl[:], op=ALU.mult))
                    P.vec(lambda v: v.tensor_tensor(out=sc1[:], in0=sc1[:], in1=sc2[:],
                                                    op=ALU.subtract))
                    P.vec(lambda v, tile=tile, cB=cB, tau=tau: v.tensor_tensor(
                        out=sc2[:], in0=cm(tile, cB, tau), in1=costl[:], op=ALU.mult))
                    P.vec(lambda v, tile=tile, cA=cA, tau=tau: v.tensor_tensor(
                        out=sc5[:], in0=cm(tile, cA, tau), in1=sintl[:], op=ALU.mult))
                    P.vec(lambda v: v.tensor_tensor(out=sc2[:], in0=sc2[:], in1=sc5[:],
                                                    op=ALU.add))
                    P.vec(lambda v, tile=tile, cA=cA, tau=tau: v.tensor_copy(
                        out=cm(tile, cA, tau), in_=sc1[:]))
                    P.vec(lambda v, tile=tile, cB=cB, tau=tau: v.tensor_copy(
                        out=cm(tile, cB, tau), in_=sc2[:]), inc=(sv, 1))
                if with_norm:
                    def per_chunk(c, tile=tile, tau=tau):
                        def ready(c=c):
                            P.wait("scalar", sv)
                            P.wait("tensor", sv)
                        return cm(tile, c, tau), ready
                    stats_pass(None, do_sum=False, per_chunk=per_chunk)
                    P.wait("vector", sp)
                    P.vec(lambda v: v.tensor_copy(out=r3[:], in_=pr2[:]), inc=(sv, 1))
                    P.wait("scalar", sv)
                    P.act(lambda s: s.activation(out=r3[:], in_=r3[:], func=AF.Sqrt),
                          inc=(sa, 1))
                    P.wait("vector", sa)
                    P.vec(lambda v: v.reciprocal(out=r2[:], in_=r3[:]), inc=(sv, 1))
                    bcast(r2[:], bcA[:])

            for tau in range(NTAU):
                for mc in range(KC):
                    b, ld = load_slab(dtn["win_b"][16 + mc])

                    def pre(ld=ld, tau=tau, mc=mc, b=b):
                        P.wait("tensor", sca[b], ld)
                        if tau == 0 and mc == 0:
                            P.wait("tensor", sv, h_rdy)

                    def evk(bank, mc=mc, tau=tau):
                        P.vec(lambda v, bank=bank, mc=mc, tau=tau: v.tensor_scalar(
                            out=cm(kraw, mc), in0=bank[:], scalar1=pcol("bin", 16 + mc),
                            scalar2=None, op0=ALU.add), inc=(sv, 1))
                    mm_group([cm(big_a, kc, tau) for kc in range(KC)],
                             [wsl[b][:, kc * 128:(kc + 1) * 128] for kc in range(KC)],
                             evk, bank_pair=0, pre=pre)
                    wsl_mark[b] = sp.n
                for c in range(4):
                    cA, cB = c, c + 4
                    P.wait("sync", sv)
                    dma_s(costl[:], bass.AP(dtn["cs_pp"].tensor, c * T + tau * NT,
                                            [[8 * T, 128], [1, NT]]), sdx[0])
                    dma_s(sintl[:], bass.AP(dtn["cs_pp"].tensor, (4 + c) * T + tau * NT,
                                            [[8 * T, 128], [1, NT]]), sdx[1])
                    P.wait("vector", sdx[0])
                    P.wait("vector", sdx[1])
                    P.vec(lambda v, cA=cA: v.tensor_tensor(
                        out=sc1[:], in0=cm(kraw, cA), in1=costl[:], op=ALU.mult))
                    P.vec(lambda v, cB=cB: v.tensor_tensor(
                        out=sc2[:], in0=cm(kraw, cB), in1=sintl[:], op=ALU.mult))
                    P.vec(lambda v: v.tensor_tensor(out=sc1[:], in0=sc1[:], in1=sc2[:],
                                                    op=ALU.subtract))
                    P.vec(lambda v, cB=cB: v.tensor_tensor(
                        out=sc2[:], in0=cm(kraw, cB), in1=costl[:], op=ALU.mult))
                    P.vec(lambda v, cA=cA: v.tensor_tensor(
                        out=sc5[:], in0=cm(kraw, cA), in1=sintl[:], op=ALU.mult))
                    P.vec(lambda v: v.tensor_tensor(out=sc2[:], in0=sc2[:], in1=sc5[:],
                                                    op=ALU.add))
                    P.vec(lambda v, cA=cA: v.tensor_copy(out=cm(kraw, cA), in_=sc1[:]))
                    P.vec(lambda v, cB=cB: v.tensor_copy(out=cm(kraw, cB), in_=sc2[:]),
                          inc=(sv, 1))
                def per_chunk_k(c):
                    def ready():
                        P.wait("scalar", sv)
                        P.wait("tensor", sv)
                    return cm(kraw, c), ready
                stats_pass(None, do_sum=False, per_chunk=per_chunk_k)
                P.wait("vector", sp)
                P.vec(lambda v: v.tensor_copy(out=r3[:], in_=pr2[:]), inc=(sv, 1))
                P.wait("scalar", sv)
                P.act(lambda s: s.activation(out=r3[:], in_=r3[:], func=AF.Sqrt), inc=(sa, 1))
                P.wait("vector", sa)
                P.vec(lambda v: v.reciprocal(out=r2[:], in_=r3[:]), inc=(sv, 1))
                bcast(r2[:], bcA[:])

                for mc in range(KC):
                    b, ld = load_slab(dtn["win_b"][32 + mc])

                    def prev(ld=ld, b=b):
                        P.wait("tensor", sca[b], ld)

                    def evv(bank, mc=mc, tau=tau):
                        P.vec(lambda v, bank=bank, mc=mc: v.tensor_scalar(
                            out=sc1[:], in0=bank[:], scalar1=pcol("bin", 32 + mc),
                            scalar2=None, op0=ALU.add))
                        P.vec(lambda v, mc=mc: v.tensor_tensor(
                            out=sc1[:], in0=sc1[:], in1=cm(kraw, mc), op=ALU.mult))
                        P.vec(lambda v: v.tensor_tensor(out=sc1[:], in0=sc1[:], in1=bcA[:],
                                                        op=ALU.mult))
                        P.vec(lambda v: v.reduce_sum(out=sc2[:, 0:1], in_=sc1[:], axis=AX.X))
                        P.vec(lambda v, mc=mc: v.tensor_tensor(
                            out=acol[:, mc:mc + 1], in0=acol[:, mc:mc + 1],
                            in1=sc2[:, 0:1], op=ALU.add), inc=(sv, 1))
                    mm_group([cm(big_a, kc, tau) for kc in range(KC)],
                             [wsl[b][:, kc * 128:(kc + 1) * 128] for kc in range(KC)],
                             evv, bank_pair=0, pre=prev)
                    wsl_mark[b] = sp.n

                for mc in range(KC):
                    b, ld = load_slab(dtn["win_b"][mc])

                    def preq(ld=ld, b=b):
                        P.wait("tensor", sca[b], ld)

                    def evq(bank, mc=mc, tau=tau):
                        P.vec(lambda v, bank=bank, mc=mc, tau=tau: v.tensor_scalar(
                            out=cm(big_b, mc, tau), in0=bank[:], scalar1=pcol("bin", mc),
                            scalar2=None, op0=ALU.add), inc=(sv, 1))
                    mm_group([cm(big_a, kc, tau) for kc in range(KC)],
                             [wsl[b][:, kc * 128:(kc + 1) * 128] for kc in range(KC)],
                             evq, bank_pair=0, pre=preq)
                    wsl_mark[b] = sp.n
                rope_pass(big_b, tau, with_norm=False)

            P.wait("vector", sv)
            for c in range(KC):
                P.vec(lambda v, c=c: v.tensor_scalar(
                    out=big_b[:, c * T:(c + 1) * T], in0=big_b[:, c * T:(c + 1) * T],
                    scalar1=acol[:, c:c + 1], scalar2=None, op0=ALU.mult),
                    inc=(sv, 1) if c == KC - 1 else None)
            y_rdy = sv.n

            for mc in range(KC):
                b, ld = load_slab(dtn["wout_b"][mc])
                for tau in range(NTAU):
                    P.wait("sync", sv)
                    dma_s(sc4[:], chunk_ap(x0, mc, tau), sdx[1])
                    xld = sdx[1].n

                    def pred(ld=ld, tau=tau, b=b):
                        if tau == 0:
                            P.wait("tensor", sca[b], ld)
                            P.wait("tensor", sv, y_rdy)

                    def evd(bank, mc=mc, tau=tau, xld=xld):
                        P.wait("vector", sdx[1], xld)
                        rsb = rs_b[:, tau * NT:(tau + 1) * NT]
                        mrb = mr_b[:, tau * NT:(tau + 1) * NT]
                        P.vec(lambda v, rsb=rsb: v.tensor_tensor(
                            out=sc1[:], in0=sc4[:], in1=rsb, op=ALU.mult))
                        P.vec(lambda v, mrb=mrb: v.tensor_tensor(
                            out=sc1[:], in0=sc1[:], in1=mrb, op=ALU.subtract))
                        P.vec(lambda v: v.tensor_tensor(out=sc1[:], in0=sc1[:], in1=sc4[:],
                                                        op=ALU.add))
                        P.vec(lambda v, bank=bank, mc=mc: v.tensor_scalar(
                            out=sc2[:], in0=bank[:], scalar1=pcol("bout", mc),
                            scalar2=None, op0=ALU.add))
                        P.wait("vector", sdo[0])
                        P.vec(lambda v: v.tensor_tensor(out=sc3[:], in0=sc1[:], in1=sc2[:],
                                                        op=ALU.add), inc=(sv, 1))
                        P.wait("sync", sv)
                        dma_s(chunk_ap(dtn["x1_scr"], mc, tau), sc3[:], sdo[0])
                    mm_group([cm(big_b, kc, tau) for kc in range(KC)],
                             [wsl[b][:, kc * 128:(kc + 1) * 128] for kc in range(KC)],
                             evd, bank_pair=0, pre=pred)
                wsl_mark[b] = sp.n

        def layer0_ffn(ctx):
            xn_bf = sb(ctx, "xn_bf", [128, KC * NT], BF16)
            xb4 = [sb(ctx, "xb4a", [128, 4 * NT]), sb(ctx, "xb4b", [128, 4 * NT])]
            hss = sb(ctx, "hss", [128, MC0 * NT], BF16)
            wg2 = [sb(ctx, f"wg2_{i}", [128, KC * 128], BF16) for i in range(2)]
            wdl = [sb(ctx, f"wdl{i}", [128, MC0 * 128], BF16) for i in range(2)]
            gu_mark = [sp.n, sp.n]
            wd_mark = [0, 0]
            P.wait("sync", sdo[0])
            P.wait("sync", sdo[1])
            P.wait("sync", ccs)
            for tau in range(NTAU):
                rsb = rs_b[:, tau * NT:(tau + 1) * NT]
                mrb = mr_b[:, tau * NT:(tau + 1) * NT]
                for c in range(KC):
                    buf = [sc3, sc4][c % 2]
                    P.wait("sync", sv)
                    P.wait("sync", sdo[0])
                    dma_s(buf[:], chunk_ap(x0, c, tau), sdx[c % 2])
                    P.wait("vector", sdx[c % 2])
                    P.vec(lambda v, buf=buf, rsb=rsb: v.tensor_tensor(
                        out=sc1[:], in0=buf[:], in1=rsb, op=ALU.mult))
                    P.vec(lambda v, c=c, mrb=mrb: v.tensor_tensor(
                        out=cm(xn_bf, c), in0=sc1[:], in1=mrb, op=ALU.subtract),
                        inc=(sv, 1))
                xn_rdy = sv.n
                for mc in range(MC0):
                    bg = mc % 2
                    P.wait("sync", sp, gu_mark[bg])
                    dma_s(wsl[bg][:], bass.AP(dtn["gu_all"].tensor, mc * 128 * KC * 128,
                                              [[KC * 128, 128], [1, KC * 128]]), sdw[bg])
                    dma_s(wg2[bg][:], bass.AP(dtn["gu_all"].tensor, (43 + mc) * 128 * KC * 128,
                                              [[KC * 128, 128], [1, KC * 128]]), sdw[bg])
                    ld = sdw[bg].n

                    def preg(ld=ld, xn_rdy=xn_rdy, tau=tau, mc=mc, bg=bg):
                        P.wait("tensor", sdw[bg], ld)
                        if mc == 0:
                            P.wait("tensor", sv, xn_rdy)

                    def evg(bank):
                        P.wait("scalar", sp)
                        P.wait("scalar", sv)
                        P.act(lambda s, bank=bank: s.activation(out=sc2[:], in_=bank[:],
                                                                func=AF.Silu))

                    def evu(bank, mc=mc):
                        P.wait("vector", sa)
                        P.vec(lambda v, bank=bank, mc=mc: v.tensor_tensor(
                            out=cm(hss, mc), in0=sc2[:], in1=bank[:], op=ALU.mult),
                            inc=(sv, 1))
                    mm_group([cm(xn_bf, kc) for kc in range(KC)],
                             [wsl[bg][:, kc * 128:(kc + 1) * 128] for kc in range(KC)],
                             evg, bank_pair=0, pre=preg)
                    mm_group([cm(xn_bf, kc) for kc in range(KC)],
                             [wg2[bg][:, kc * 128:(kc + 1) * 128] for kc in range(KC)],
                             evu, bank_pair=1)
                    gu_mark[bg] = sp.n
                hss_rdy = sv.n
                for hm in range(KC):
                    b = hm % 2
                    P.wait("sync", sp, wd_mark[b])
                    dma_s(wdl[b][:], bass.AP(dtn["d_all"].tensor, hm * 128 * MC0 * 128,
                                             [[MC0 * 128, 128], [1, MC0 * 128]]), sdd[b])
                    ld = sdd[b].n
                    P.wait("sync", sv)
                    dma_s(sc4[:], chunk_ap(dtn["x1_scr"], hm, tau), sdx[1])
                    xld = sdx[1].n

                    def prew(ld=ld, hss_rdy=hss_rdy, hm=hm, b=b):
                        P.wait("tensor", sdd[b], ld)
                        if hm == 0:
                            P.wait("tensor", sv, hss_rdy)

                    def evdn(bank, hm=hm, tau=tau, xld=xld):
                        P.wait("vector", sdx[1], xld)
                        P.wait("vector", sdo[0])
                        P.vec(lambda v, bank=bank: v.tensor_tensor(
                            out=sc3[:], in0=bank[:], in1=sc4[:], op=ALU.add), inc=(sv, 1))
                        P.wait("sync", sv)
                        dma_s(chunk_ap(dtn["x1_scr"], hm, tau), sc3[:], sdo[0])
                    mm_group([cm(hss, kc) for kc in range(MC0)],
                             [wdl[b][:, kc * 128:(kc + 1) * 128] for kc in range(MC0)],
                             evdn, bank_pair=0, pre=prew)
                    wd_mark[b] = sp.n

        def phase_barrier(exclude=()):
            allsems = [sv, sa, sp, smisc, sdx[0], sdx[1], sdw[0], sdw[1],
                       gdw[0], gdw[1], sdo[0], sdo[1], sdd[0], sdd[1], gdd[0], gdd[1],
                       sca[0], sca[1], scp[0], scp[1], scp[2], scp[3],
                       sco[0], sco[1], sco[2], sco[3], ccs]
            for eng in ("vector", "scalar", "tensor", "sync", "gpsimd"):
                for sem in allsems:
                    if sem in exclude:
                        continue
                    P.wait(eng, sem)

        x1d = dtn["x1_scr"]

        xbL1 = [None]

        def build_xnt(xnt, tau):
            rsb = rs_b[:, tau * NT:(tau + 1) * NT]
            mrb = mr_b[:, tau * NT:(tau + 1) * NT]

            def consume(c, view, b, ld, rsb=rsb, mrb=mrb):
                P.wait("vector", sdx[b], ld)
                P.vec(lambda v, view=view, rsb=rsb: v.tensor_tensor(
                    out=sc1[:], in0=view, in1=rsb, op=ALU.mult))
                P.vec(lambda v, c=c, mrb=mrb: v.tensor_tensor(
                    out=xnt[:, c * NT:(c + 1) * NT], in0=sc1[:], in1=mrb,
                    op=ALU.subtract), inc=(sv, 1))
            stream4(x1d, tau, consume, xbL1[0],
                    extra=lambda: P.wait("sync", sdo[0]))

        def layer1_stats():
            for tau in range(NTAU):
                stats_pass.tau = tau
                stats_pass(x1d, do_sum=True, xb=xbL1[0])
                rowmath_musd(0, r2)
                bcast(r2[:], rs_b[:, tau * NT:(tau + 1) * NT])
                P.vec(lambda v: v.tensor_tensor(out=r3[:], in0=r1[:], in1=r2[:], op=ALU.mult))
                bcast(r3[:], mr_b[:, tau * NT:(tau + 1) * NT])
            for tau in range(NTAU):
                rsb = rs_b[:, tau * NT:(tau + 1) * NT]
                mrb = mr_b[:, tau * NT:(tau + 1) * NT]
                sq_ping = [sc1, sc2]
                sq_done_at = [0, 0]
                P.wait("tensor", sv, pr_free[0])

                def consume(c, view, b, ld, rsb=rsb, mrb=mrb,
                            sq_ping=sq_ping, sq_done_at=sq_done_at):
                    P.wait("vector", sdx[b], ld)
                    P.wait("vector", sa)
                    P.vec(lambda v, view=view, rsb=rsb: v.tensor_tensor(
                        out=sc5[:], in0=view, in1=rsb, op=ALU.mult))
                    P.vec(lambda v, mrb=mrb: v.tensor_tensor(
                        out=sc5[:], in0=sc5[:], in1=mrb, op=ALU.subtract), inc=(sv, 1))
                    P.wait("tensor", sv)
                    P.wait("scalar", sv)
                    _stats_chunk(c, sc5[:], sq_ping, sq_done_at, False)
                stream4(x1d, tau, consume, xbL1[0])
                rowmath_musd(1, r2, with_mu=False)
                bcast(r2[:], rrms_b[:, tau * NT:(tau + 1) * NT])

        def layer1_router(ctx):
            rwt = sb(ctx, "rwt", [128, KC * 8])
            rwb = sb(ctx, "rwb", [128, KC * 8], BF16)
            lgf = sb(ctx, "lgf", [8, T])
            xnt = sb(ctx, "xnt", [128, KC * NT], BF16)
            dma_s(rwt[:], dtn["rw_sl"][0], smisc)
            P.wait("vector", smisc)
            for kc in range(KC):
                P.vec(lambda v, kc=kc: v.tensor_scalar(
                    out=rwb[:, kc * 8:(kc + 1) * 8], in0=rwt[:, kc * 8:(kc + 1) * 8],
                    scalar1=pcol("rsc", kc), scalar2=None, op0=ALU.mult))
            rwb_rdy = sv.n
            for tau in range(NTAU):
                build_xnt(xnt, tau)
                P.wait("tensor", sv)
                for kc in range(KC):
                    P.pe(lambda t, kc=kc, tau=tau: t.matmul(
                        pbc[0:8, :], rwb[:, kc * 8:(kc + 1) * 8],
                        xnt[:, kc * NT:(kc + 1) * NT],
                        start=(kc == 0), stop=(kc == KC - 1)),
                        inc=(sp, 1) if kc == KC - 1 else None)
                P.wait("vector", sp)
                P.vec(lambda v, tau=tau: v.tensor_tensor(
                    out=lgf[0:8, tau * NT:(tau + 1) * NT], in0=pbc[0:8, :],
                    in1=rrms_b[0:8, tau * NT:(tau + 1) * NT], op=ALU.mult))
                P.vec(lambda v, tau=tau: v.tensor_scalar(
                    out=lgf[0:8, tau * NT:(tau + 1) * NT],
                    in0=lgf[0:8, tau * NT:(tau + 1) * NT],
                    scalar1=ROOT, scalar2=None, op0=ALU.mult), inc=(sv, 1))
            for tc in range(KC):
                P.wait("tensor", sv)
                P.pe(lambda t, tc=tc: t.transpose(
                    pt[:, 0:8], lgf[0:8, tc * 128:(tc + 1) * 128], ident_s[0:8, 0:8]),
                    inc=(sp, 1))
                P.wait("vector", sp)
                P.vec(lambda v, tc=tc: v.tensor_copy(out=rtm[:, tc * 8:(tc + 1) * 8],
                                                     in_=pt[:, 0:8]), inc=(sv, 1))
            for tc in range(KC):
                p8 = rtm[:, tc * 8:(tc + 1) * 8]
                q8 = raff[:, tc * 8:(tc + 1) * 8]
                s8 = rsc1[:, tc * 8:(tc + 1) * 8]
                P.vec(lambda v, p8=p8: v.reduce_max(out=rcol[:, 0:1], in_=p8, axis=AX.X))
                P.vec(lambda v, p8=p8, q8=q8: v.tensor_scalar(
                    out=q8, in0=p8, scalar1=rcol[:, 0:1], scalar2=None, op0=ALU.is_ge))
                P.vec(lambda v, p8=p8, q8=q8, s8=s8: v.scalar_tensor_tensor(
                    out=s8, in0=q8, scalar=-BIG, in1=p8, op0=ALU.mult, op1=ALU.add))
                P.vec(lambda v, s8=s8: v.reduce_max(out=rcol[:, 1:2], in_=s8, axis=AX.X))
                P.vec(lambda v, s8=s8: v.tensor_scalar(
                    out=s8, in0=s8, scalar1=rcol[:, 1:2], scalar2=None, op0=ALU.is_ge))
                P.vec(lambda v: v.tensor_tensor(out=rcol[:, 2:3], in0=rcol[:, 1:2],
                                                in1=rcol[:, 0:1], op=ALU.subtract))
                P.wait("scalar", sv)
                P.act(lambda s: s.activation(out=rcol[:, 3:4], in_=rcol[:, 2:3], func=AF.Exp))
                P.wait("vector", sa)
                P.vec(lambda v: v.tensor_scalar(out=rcol[:, 4:5], in0=rcol[:, 3:4],
                                                scalar1=1.0, scalar2=None, op0=ALU.add))
                P.vec(lambda v: v.reciprocal(out=rcol[:, 5:6], in_=rcol[:, 4:5]))
                P.vec(lambda v: v.tensor_scalar(out=rcol[:, 6:7], in0=rcol[:, 5:6],
                                                scalar1=-1.0, scalar2=1.0, op0=ALU.mult,
                                                op1=ALU.add))
                P.vec(lambda v, q8=q8: v.tensor_scalar(
                    out=q8, in0=q8, scalar1=rcol[:, 5:6], scalar2=None, op0=ALU.mult))
                P.vec(lambda v, s8=s8: v.tensor_scalar(
                    out=s8, in0=s8, scalar1=rcol[:, 6:7], scalar2=None, op0=ALU.mult))
                P.vec(lambda v, q8=q8, s8=s8: v.tensor_tensor(out=q8, in0=q8, in1=s8,
                                                              op=ALU.add))
                P.vec(lambda v, q8=q8: v.tensor_tensor(out=q8, in0=q8, in1=pes_b[:, 0:8],
                                                       op=ALU.mult), inc=(sv, 1))

        def layer1_attn(ctx):
            kbuf = sb(ctx, "kbuf", [128, KC * NT])
            vhb = sb(ctx, "vhb", [128, KC * NT], BF16)
            xnt = sb(ctx, "xnt_a", [128, KC * NT], BF16)
            carry = sb(ctx, "carry", [128, KC])
            wsl_mark = [0, 0]
            slab_ctr = [0]
            P.vec(lambda v: v.memset(carry[:], 0.0))

            def load_slab(src_ap):
                i = slab_ctr[0]; slab_ctr[0] += 1
                b = i % 2
                P.wait("scalar", sp, wsl_mark[b])
                return b, load_slab_b(wsl[b], src_ap, b)

            for tau in range(NTAU):
                build_xnt(xnt, tau)
                xn_rdy = sv.n
                for mc in range(KC):
                    b, ld = load_slab(dtn["wk_b"][mc])

                    def prk(ld=ld, b=b, xn_rdy=xn_rdy):
                        P.wait("tensor", sca[b], ld)
                        P.wait("tensor", sv, xn_rdy)

                    def evkk(bank, mc=mc):
                        P.vec(lambda v, bank=bank, mc=mc: v.tensor_copy(
                            out=cm(kbuf, mc), in_=bank[:]), inc=(sv, 1))
                    mm_group([xnt[:, kc * NT:(kc + 1) * NT] for kc in range(KC)],
                             [wsl[b][:, kc * 128:(kc + 1) * 128] for kc in range(KC)],
                             evkk, bank_pair=0, pre=prk)
                    wsl_mark[b] = sp.n
                for mc in range(KC):
                    b, ld = load_slab(dtn["wv_b"][mc])

                    def prv(ld=ld, b=b):
                        P.wait("tensor", sca[b], ld)

                    def evvv(bank, mc=mc):
                        P.vec(lambda v, bank=bank, mc=mc: v.tensor_tensor(
                            out=sc1[:], in0=bank[:], in1=cm(kbuf, mc), op=ALU.mult))
                        P.vec(lambda v, mc=mc: v.tensor_tensor_scan(
                            out=cm(kbuf, mc), data0=sc1[:], data1=sc1[:],
                            initial=carry[:, mc:mc + 1], op0=ALU.add, op1=ALU.bypass))
                        P.vec(lambda v, mc=mc: v.tensor_copy(
                            out=carry[:, mc:mc + 1], in_=cm(kbuf, mc)[:, NT - 1:NT]),
                            inc=(sv, 1))
                    mm_group([xnt[:, kc * NT:(kc + 1) * NT] for kc in range(KC)],
                             [wsl[b][:, kc * 128:(kc + 1) * 128] for kc in range(KC)],
                             evvv, bank_pair=0, pre=prv)
                    wsl_mark[b] = sp.n
                for mc in range(KC):
                    b, ld = load_slab(dtn["wq_b"][mc])

                    def prq(ld=ld, b=b):
                        P.wait("tensor", sca[b], ld)

                    def evqq(bank, mc=mc):
                        P.vec(lambda v, bank=bank, mc=mc: v.tensor_tensor(
                            out=sc1[:], in0=bank[:], in1=cm(kbuf, mc), op=ALU.mult))
                        P.vec(lambda v, mc=mc: v.tensor_scalar(
                            out=cm(vhb, mc), in0=sc1[:], scalar1=DENOM1, scalar2=None,
                            op0=ALU.mult), inc=(sv, 1))
                    mm_group([xnt[:, kc * NT:(kc + 1) * NT] for kc in range(KC)],
                             [wsl[b][:, kc * 128:(kc + 1) * 128] for kc in range(KC)],
                             evqq, bank_pair=0, pre=prq)
                    wsl_mark[b] = sp.n
                for mc in range(KC):
                    b, ld = load_slab(dtn["wo_b"][mc])
                    P.wait("sync", sv)
                    dma_s(sc4[:], chunk_ap(x1d, mc, tau), sdx[1])
                    xld = sdx[1].n

                    def pro(ld=ld, b=b):
                        P.wait("tensor", sca[b], ld)
                        P.wait("tensor", sv)

                    def evo(bank, mc=mc, tau=tau, xld=xld):
                        P.wait("vector", sdx[1], xld)
                        P.wait("vector", sdo[0])
                        P.vec(lambda v, bank=bank: v.tensor_tensor(
                            out=sc3[:], in0=bank[:], in1=sc4[:], op=ALU.add), inc=(sv, 1))
                        P.wait("sync", sv)
                        dma_s(chunk_ap(dtn["acc_scr"], mc, tau), sc3[:], sdo[0])
                    mm_group([cm(vhb, kc) for kc in range(KC)],
                             [wsl[b][:, kc * 128:(kc + 1) * 128] for kc in range(KC)],
                             evo, bank_pair=0, pre=pro)
                    wsl_mark[b] = sp.n

        def layer1_dense_gu(ctx, din):
            scb = sb(ctx, "scb", [128, NT], BF16)
            wga = [sb(ctx, f"wga{i}", [128, KC * 128], BF16) for i in range(2)]
            wgb = [sb(ctx, f"wgb{i}", [128, KC * 128], BF16) for i in range(2)]
            for tau in range(NTAU):
                rsb = rs_b[:, tau * NT:(tau + 1) * NT]
                mrb = mr_b[:, tau * NT:(tau + 1) * NT]
                rrb = rrms_b[:, tau * NT:(tau + 1) * NT]

                def consume(c, view, b, ld, tau=tau, rsb=rsb, mrb=mrb, rrb=rrb):
                    P.wait("vector", sdx[b], ld)
                    P.vec(lambda v, view=view, rsb=rsb: v.tensor_tensor(
                        out=sc1[:], in0=view, in1=rsb, op=ALU.mult))
                    P.vec(lambda v, mrb=mrb: v.tensor_tensor(
                        out=sc1[:], in0=sc1[:], in1=mrb, op=ALU.subtract))
                    P.vec(lambda v, rrb=rrb: v.tensor_tensor(
                        out=sc1[:], in0=sc1[:], in1=rrb, op=ALU.mult))
                    P.vec(lambda v, c=c, tau=tau: v.tensor_scalar(
                        out=cm(din, c, tau), in0=sc1[:], scalar1=pcol("pre1", c),
                        scalar2=None, op0=ALU.mult), inc=(sv, 1))
                stream4(x1d, tau, consume, xbL1[0],
                        extra=lambda: P.wait("sync", sdo[0]))
            din_rdy = sv.n
            gu_mark = [sp.n, sp.n]
            for mc in range(64):
                bg = mc % 2
                P.wait("scalar", sp, gu_mark[bg])
                load_slab_b(wga[bg], dtn["wg1_b"][mc], bg)
                ld = load_slab_b(wgb[bg], dtn["wu1_b"][mc], bg)
                for tau in range(NTAU):
                    def prg(ld=ld, bg=bg, mc=mc, tau=tau):
                        P.wait("tensor", sca[bg], ld)
                        if mc == 0 and tau == 0:
                            P.wait("tensor", sv, din_rdy)

                    def evg1(bank):
                        P.wait("scalar", sp)
                        P.wait("scalar", sv)
                        P.act(lambda s, bank=bank: s.activation(
                            out=sc2[:], in_=bank[:], func=AF.Gelu_apprx_tanh))

                    def evu1(bank, mc=mc, tau=tau):
                        P.wait("vector", sa)
                        P.wait("vector", sdo[1])
                        P.vec(lambda v, bank=bank: v.tensor_tensor(
                            out=scb[:], in0=sc2[:], in1=bank[:], op=ALU.mult), inc=(sv, 1))
                        P.wait("sync", sv)
                        dma_s(chunk_ap(dtn["h1_scr"], mc, tau), scb[:], sdo[1])
                    mm_group([cm(din, kc, tau) for kc in range(KC)],
                             [wga[bg][:, kc * 128:(kc + 1) * 128] for kc in range(KC)],
                             evg1, bank_pair=0, pre=prg)
                    mm_group([cm(din, kc, tau) for kc in range(KC)],
                             [wgb[bg][:, kc * 128:(kc + 1) * 128] for kc in range(KC)],
                             evu1, bank_pair=1)
                gu_mark[bg] = sp.n

        def layer1_dense_down(ctx, rdacc):
            h1t = sb(ctx, "h1t", [128, 64 * NT], BF16)
            wdl = [sb(ctx, f"wdl1_{i}", [128, 64 * 128], BF16) for i in range(2)]
            wd_mark = [0, 0]
            h1_mark = [0]
            P.vec(lambda v: v.memset(rdacc[:], 0.0), inc=(sv, 1))
            for tau in range(NTAU):
                P.wait("sync", sp, h1_mark[0])
                P.wait("sync", sdo[1])
                dma_s(h1t[:].rearrange("p (c j) -> p c j", j=NT),
                      bass.AP(dtn["h1_scr"].tensor, tau * NT,
                              [[T, 128], [128 * T, 64], [1, NT]]), sdx[0])
                h1ld = sdx[0].n
                for hm in range(KC):
                    b = hm % 2
                    P.wait("scalar", sp, wd_mark[b])
                    ld = load_slab_b(wdl[b], dtn["wd1_b"][hm], b)

                    def prd(ld=ld, b=b, h1ld=h1ld):
                        P.wait("tensor", sca[b], ld)
                        P.wait("tensor", sdx[0], h1ld)

                    def evd1(bank, hm=hm, tau=tau):
                        P.wait("vector", sdo[0])
                        P.vec(lambda v, bank=bank: v.tensor_copy(out=sc3[:], in_=bank[:]),
                              inc=(sv, 1))
                        P.wait("sync", sv)
                        dma_s(chunk_ap(dtn["dno_scr"], hm, tau), sc3[:], sdo[0])
                        P.wait("scalar", sv)
                        P.act(lambda s: s.activation(out=sc2[:], in_=sc3[:], func=AF.Square))
                        P.wait("tensor", sa)
                        P.wait("tensor", sv)
                        P.pe(lambda t: t.matmul(pr1[:], onesk_s[:], sc2[:],
                                                start=True, stop=True), inc=(sp, 1))
                        P.wait("vector", sp)
                        P.vec(lambda v, tau=tau: v.tensor_tensor(
                            out=rdacc[0:1, tau * NT:(tau + 1) * NT],
                            in0=rdacc[0:1, tau * NT:(tau + 1) * NT],
                            in1=pr1[:], op=ALU.add), inc=(sv, 1))
                    mm_group([h1t[:, kc * NT:(kc + 1) * NT] for kc in range(64)],
                             [wdl[b][:, kc * 128:(kc + 1) * 128] for kc in range(64)],
                             evd1, bank_pair=0, pre=prd)
                    wd_mark[b] = sp.n
                h1_mark[0] = sp.n

        def build_affb(e, affb):
            for tau in range(NTAU):
                for j in range(4):
                    tc = tau * 4 + j
                    P.wait("tensor", sv)
                    P.pe(lambda t, tc=tc, j=j, e=e: t.transpose(
                        pt[0:1, j * 128:(j + 1) * 128],
                        raff[:, tc * 8 + e:tc * 8 + e + 1], ident_s[:]),
                        inc=(sp, 1))
                P.wait("vector", sp)
                P.vec(lambda v: v.tensor_copy(out=r3[:], in_=pt[0:1, 0:NT]), inc=(sv, 1))
                bcast(r3[:], affb[:, tau * NT:(tau + 1) * NT])

        def layer1_moe_he(ctx, moein):
            scb = sb(ctx, "scb2", [128, NT], BF16)
            affb = sb(ctx, "affb", [128, NTAU * NT])
            wga = [sb(ctx, f"wea{i}", [128, KC * 128], BF16) for i in range(2)]
            wgb = [sb(ctx, f"web{i}", [128, KC * 128], BF16) for i in range(2)]
            for tau in range(NTAU):
                rsb = rs_b[:, tau * NT:(tau + 1) * NT]
                mrb = mr_b[:, tau * NT:(tau + 1) * NT]
                rrb = rrms_b[:, tau * NT:(tau + 1) * NT]

                def consume(c, view, b, ld, tau=tau, rsb=rsb, mrb=mrb, rrb=rrb):
                    P.wait("vector", sdx[b], ld)
                    P.vec(lambda v, view=view, rsb=rsb: v.tensor_tensor(
                        out=sc1[:], in0=view, in1=rsb, op=ALU.mult))
                    P.vec(lambda v, mrb=mrb: v.tensor_tensor(
                        out=sc1[:], in0=sc1[:], in1=mrb, op=ALU.subtract))
                    P.vec(lambda v, rrb=rrb: v.tensor_tensor(
                        out=sc1[:], in0=sc1[:], in1=rrb, op=ALU.mult))
                    P.vec(lambda v, c=c, tau=tau: v.tensor_scalar(
                        out=cm(moein, c, tau), in0=sc1[:], scalar1=pcol("pre2", c),
                        scalar2=None, op0=ALU.mult), inc=(sv, 1))
                stream4(x1d, tau, consume, xbL1[0],
                        extra=lambda: P.wait("sync", sdo[0]))
            mo_rdy = sv.n
            gu_mark = [sp.n, sp.n]
            for e in range(E):
                build_affb(e, affb)
                for mc in range(8):
                    bg = mc % 2
                    P.wait("scalar", sp, gu_mark[bg])
                    load_slab_b(wga[bg], dtn["weg_b"][e, mc], bg)
                    ld = load_slab_b(wgb[bg], dtn["weu_b"][e, mc], bg)
                    for tau in range(NTAU):
                        def prg(ld=ld, bg=bg, e=e, mc=mc, tau=tau):
                            P.wait("tensor", sca[bg], ld)
                            if e == 0 and mc == 0 and tau == 0:
                                P.wait("tensor", sv, mo_rdy)

                        def evg2(bank):
                            P.wait("scalar", sp)
                            P.wait("scalar", sv)
                            P.act(lambda s, bank=bank: s.activation(
                                out=sc2[:], in_=bank[:], func=AF.Gelu_apprx_tanh))

                        def evu2(bank, e=e, mc=mc, tau=tau):
                            P.wait("vector", sa)
                            P.wait("vector", sdo[1])
                            P.vec(lambda v, bank=bank: v.tensor_tensor(
                                out=sc1[:], in0=sc2[:], in1=bank[:], op=ALU.mult))
                            P.vec(lambda v, tau=tau: v.tensor_tensor(
                                out=scb[:], in0=sc1[:],
                                in1=affb[:, tau * NT:(tau + 1) * NT], op=ALU.mult),
                                inc=(sv, 1))
                            P.wait("sync", sv)
                            dma_s(chunk_ap(dtn["hew_scr"], e * 8 + mc, tau), scb[:], sdo[1])
                        mm_group([cm(moein, kc, tau) for kc in range(KC)],
                                 [wga[bg][:, kc * 128:(kc + 1) * 128] for kc in range(KC)],
                                 evg2, bank_pair=0, pre=prg)
                        mm_group([cm(moein, kc, tau) for kc in range(KC)],
                                 [wgb[bg][:, kc * 128:(kc + 1) * 128] for kc in range(KC)],
                                 evu2, bank_pair=1)
                    gu_mark[bg] = sp.n

        def layer1_moe_down(ctx, rmacc):
            hewt = sb(ctx, "hewt", [128, 64 * NT], BF16)
            wdl = [sb(ctx, f"wdl2_{i}", [128, 64 * 128], BF16) for i in range(2)]
            wd_mark = [0, 0]
            hw_mark = [0]
            P.vec(lambda v: v.memset(rmacc[:], 0.0), inc=(sv, 1))
            for tau in range(NTAU):
                P.wait("sync", sp, hw_mark[0])
                P.wait("sync", sdo[1])
                dma_s(hewt[:].rearrange("p (c j) -> p c j", j=NT),
                      bass.AP(dtn["hew_scr"].tensor, tau * NT,
                              [[T, 128], [128 * T, 64], [1, NT]]), sdx[0])
                hld = sdx[0].n
                for hm in range(KC):
                    b = hm % 2
                    P.wait("scalar", sp, wd_mark[b])
                    ld = load_slab_b(wdl[b], dtn["wed_b"][hm], b)

                    def prd(ld=ld, b=b, hld=hld):
                        P.wait("tensor", sca[b], ld)
                        P.wait("tensor", sdx[0], hld)

                    def evd2(bank, hm=hm, tau=tau):
                        P.wait("vector", sdo[0])
                        P.vec(lambda v, bank=bank: v.tensor_copy(out=sc3[:], in_=bank[:]),
                              inc=(sv, 1))
                        P.wait("sync", sv)
                        dma_s(chunk_ap(dtn["mo_scr"], hm, tau), sc3[:], sdo[0])
                        P.wait("scalar", sv)
                        P.act(lambda s: s.activation(out=sc2[:], in_=sc3[:], func=AF.Square))
                        P.wait("tensor", sa)
                        P.wait("tensor", sv)
                        P.pe(lambda t: t.matmul(pr1[:], onesk_s[:], sc2[:],
                                                start=True, stop=True), inc=(sp, 1))
                        P.wait("vector", sp)
                        P.vec(lambda v, tau=tau: v.tensor_tensor(
                            out=rmacc[0:1, tau * NT:(tau + 1) * NT],
                            in0=rmacc[0:1, tau * NT:(tau + 1) * NT],
                            in1=pr1[:], op=ALU.add), inc=(sv, 1))
                    mm_group([hewt[:, kc * NT:(kc + 1) * NT] for kc in range(64)],
                             [wdl[b][:, kc * 128:(kc + 1) * 128] for kc in range(64)],
                             evd2, bank_pair=0, pre=prd)
                    wd_mark[b] = sp.n
                hw_mark[0] = sp.n

        def layer1_final(rdacc, rmacc, rsacc):
            def rms_row_bcast(accrow, tau, dst):
                P.vec(lambda v, accrow=accrow, tau=tau: v.tensor_scalar(
                    out=r2[:], in0=accrow[0:1, tau * NT:(tau + 1) * NT],
                    scalar1=1.0 / D, scalar2=None, op0=ALU.mult), inc=(sv, 1))
                P.wait("scalar", sv)
                P.act(lambda s: s.activation(out=r3[:], in_=r2[:], func=AF.Sqrt,
                                             bias=eps_s[0:1, 1:2]), inc=(sa, 1))
                P.wait("vector", sa)
                P.vec(lambda v: v.reciprocal(out=r2[:], in_=r3[:]), inc=(sv, 1))
                bcast(r2[:], dst)

            P.vec(lambda v: v.memset(rsacc[:], 0.0), inc=(sv, 1))
            xba, xbb = xbL1[0]
            for tau in range(NTAU):
                rms_row_bcast(rdacc, tau, bcA[:])
                rms_row_bcast(rmacc, tau, bcB[:])
                for q in range(4):
                    P.wait("sync", sv)
                    P.wait("sync", sa)
                    P.wait("sync", sp)
                    P.wait("sync", sdo[0])
                    dma_s(xba[:].rearrange("p (c j) -> p c j", j=NT),
                          chunk4_ap(dtn["dno_scr"], q, tau), sdx[0])
                    lda = sdx[0].n
                    dma_s(xbb[:].rearrange("p (c j) -> p c j", j=NT),
                          chunk4_ap(dtn["mo_scr"], q, tau), sdx[1])
                    ldb = sdx[1].n
                    for j in range(4):
                        c = 4 * q + j
                        va = xba[:, j * NT:(j + 1) * NT]
                        vb = xbb[:, j * NT:(j + 1) * NT]
                        P.wait("vector", sdx[0], lda)
                        P.wait("vector", sdx[1], ldb)
                        P.wait("vector", sa)
                        P.vec(lambda v, va=va: v.tensor_tensor(out=sc1[:], in0=va, in1=bcA[:],
                                                               op=ALU.mult))
                        P.vec(lambda v, c=c: v.tensor_scalar(
                            out=sc1[:], in0=sc1[:], scalar1=pcol("post1", c), scalar2=None,
                            op0=ALU.mult))
                        P.vec(lambda v, vb=vb: v.tensor_tensor(out=sc2[:], in0=vb, in1=bcB[:],
                                                               op=ALU.mult))
                        P.vec(lambda v, c=c: v.tensor_scalar(
                            out=sc2[:], in0=sc2[:], scalar1=pcol("post2", c), scalar2=None,
                            op0=ALU.mult))
                        P.wait("vector", sdo[0])
                        P.vec(lambda v: v.tensor_tensor(out=sc5[:], in0=sc1[:], in1=sc2[:],
                                                        op=ALU.add), inc=(sv, 1))
                        P.wait("sync", sv)
                        dma_s(chunk_ap(dtn["s_scr"], c, tau), sc5[:], sdo[0])
                        P.wait("scalar", sv)
                        P.act(lambda s: s.activation(out=sc2[:], in_=sc5[:], func=AF.Square))
                        P.wait("tensor", sa)
                        P.wait("tensor", sv)
                        P.pe(lambda t: t.matmul(pr1[:], onesk_s[:], sc2[:],
                                                start=True, stop=True), inc=(sp, 1))
                        P.wait("vector", sp)
                        P.vec(lambda v, tau=tau: v.tensor_tensor(
                            out=rsacc[0:1, tau * NT:(tau + 1) * NT],
                            in0=rsacc[0:1, tau * NT:(tau + 1) * NT],
                            in1=pr1[:], op=ALU.add), inc=(sv, 1))
            for tau in range(NTAU):
                rms_row_bcast(rsacc, tau, bcA[:])
                for q in range(4):
                    P.wait("sync", sv)
                    P.wait("sync", sp)
                    P.wait("sync", sdo[0])
                    dma_s(xba[:].rearrange("p (c j) -> p c j", j=NT),
                          chunk4_ap(dtn["s_scr"], q, tau), sdx[0])
                    lda = sdx[0].n
                    dma_s(xbb[:].rearrange("p (c j) -> p c j", j=NT),
                          chunk4_ap(dtn["acc_scr"], q, tau), sdx[1])
                    ldb = sdx[1].n
                    for j in range(4):
                        c = 4 * q + j
                        va = xba[:, j * NT:(j + 1) * NT]
                        vb = xbb[:, j * NT:(j + 1) * NT]
                        P.wait("vector", sdx[0], lda)
                        P.wait("vector", sdx[1], ldb)
                        P.vec(lambda v, va=va: v.tensor_tensor(out=sc1[:], in0=va, in1=bcA[:],
                                                               op=ALU.mult))
                        P.vec(lambda v, c=c: v.tensor_scalar(
                            out=sc1[:], in0=sc1[:], scalar1=pcol("post", c), scalar2=None,
                            op0=ALU.mult))
                        P.wait("vector", sdo[0])
                        P.vec(lambda v, vb=vb: v.tensor_tensor(out=sc5[:], in0=sc1[:], in1=vb,
                                                               op=ALU.add), inc=(sv, 1))
                        P.wait("sync", sv)
                        dma_s(chunk_ap(dtn["out"], c, tau), sc5[:], sdo[0])

        load_params()
        preconvert()
        if 0 in layers:
            with ExitStack() as c0:
                big_a = sb(c0, "big_a", [128, KC * T], BF16)
                with ExitStack() as mctx:
                    ghost_mat(mctx)
                    with ExitStack() as c1:
                        layer0_stats_h(c1, big_a)
                    for _s2 in (sv, sdw[0], sdw[1], sdo[0], ccs):
                        for _e in ("vector", "sync", "scalar", "tensor"):
                            P.wait(_e, _s2)
                with ExitStack() as c2:
                    big_b = sb(c2, "big_b", [128, KC * T], BF16)
                    with ExitStack() as c3:
                        layer0_attn(c3, big_a, big_b)
            with ExitStack() as c4:
                layer0_ffn(c4)
        elif 1 in layers:
            phase_barrier(exclude=(gdw[0],))
        if 1 in layers:
            phase_barrier()
            with ExitStack() as l1a:
                rdacc = sb(l1a, "rdacc", [1, T])
                rmacc = sb(l1a, "rmacc", [1, T])
                xbL1[0] = [sb(l1a, "xb1a", [128, 4 * NT]), sb(l1a, "xb1b", [128, 4 * NT])]
                layer1_stats()
                with ExitStack() as l1b:
                    layer1_router(l1b)
                phase_barrier()
                with ExitStack() as l1c:
                    layer1_attn(l1c)
                phase_barrier()
                with ExitStack() as l1d:
                    din = sb(l1d, "din", [128, KC * T], BF16)
                    layer1_dense_gu(l1d, din)
                phase_barrier()
                with ExitStack() as l1e:
                    layer1_dense_down(l1e, rdacc)
                phase_barrier()
                with ExitStack() as l1f:
                    moein = sb(l1f, "moein", [128, KC * T], BF16)
                    layer1_moe_he(l1f, moein)
                phase_barrier()
                with ExitStack() as l1g:
                    layer1_moe_down(l1g, rmacc)
                phase_barrier()
                with ExitStack() as l1h:
                    rsacc = sb(l1h, "rsacc", [1, T])
                    layer1_final(rdacc, rmacc, rsacc)

        for _sem in [smisc, sdx[0], sdx[1], sdw[0], sdw[1], sdo[0], sdo[1], sdd[0], sdd[1]]:
            P.wait("sync", _sem)
        for _sem in [sca[0], sca[1], scp[0], scp[1], scp[2], scp[3],
                     sco[0], sco[1], sco[2], sco[3]]:
            P.wait("scalar", _sem)
        for _sem in [gdw[0], gdw[1], gdd[0], gdd[1], ccs]:
            P.wait("gpsimd", _sem)

        block = octx.enter_context(nc.Block())

        @block.sync
        def _(eng):
            for f in P.ops["sync"]:
                f(eng)

        @block.gpsimd
        def _(eng):
            for f in P.ops["gpsimd"]:
                f(eng)

        @block.vector
        def _(eng):
            for f in P.ops["vector"]:
                f(eng)

        @block.scalar
        def _(eng):
            for f in P.ops["scalar"]:
                f(eng)

        @block.tensor
        def _(eng):
            for f in P.ops["tensor"]:
                f(eng)

    return nc



def prep_inputs_l0(inputs):
    perm = np.concatenate([np.arange(0, 1024, 2), np.arange(1, 1024, 2),
                           np.arange(1024, 2048)])
    qkvperm = np.concatenate([perm, 2048 + perm, 4096 + perm])
    W_in = np.asarray(inputs["L0_attn_W_in"], dtype=np.float32)
    b_in = np.asarray(inputs["L0_attn_b_in"], dtype=np.float32)
    W_out = np.asarray(inputs["L0_attn_W_out"], dtype=np.float32)
    d = {}
    d["win_sl"] = to_slabs(np.ascontiguousarray(W_in[:, qkvperm]), KC, 48)
    d["bin_pp"] = col_pp(b_in[qkvperm], 48)
    d["lnw_pp"] = col_pp(inputs["L0_attn_ln_w"], KC)
    d["lnb_pp"] = col_pp(inputs["L0_attn_ln_b"], KC)
    d["wout_sl"] = to_slabs(np.ascontiguousarray(W_out[perm, :]), KC, KC)
    d["bout_pp"] = col_pp(inputs["L0_attn_b_out"], KC)
    d["lut_row"] = np.asarray(inputs["L0_ffn_lut"], np.float32).reshape(1, 16)
    gT = np.ascontiguousarray(np.asarray(inputs["L0_ffn_gate_idx"], np.int32).T)
    uT = np.ascontiguousarray(np.asarray(inputs["L0_ffn_up_idx"], np.int32).T)
    dT = np.ascontiguousarray(np.asarray(inputs["L0_ffn_down_idx"], np.int32).T)
    gu = np.concatenate([to_slabs(gT, KC, MC0), to_slabs(uT, KC, MC0)], axis=0)
    d["_gu88"] = np.concatenate([gu, gu[0:2]], axis=0)
    d["_d16"] = to_slabs(dT, MC0, KC)
    i = np.arange(0, 1024, 2, dtype=np.float64) / 1024.0
    inv_freq = 1.0 / (10000.0 ** i)
    s = np.arange(T, dtype=np.float64)
    ang = s[None, :] * inv_freq[:, None]
    cosT = np.cos(ang).astype(np.float32).reshape(4, 128, T)
    sinT = np.sin(ang).astype(np.float32).reshape(4, 128, T)
    cs = np.concatenate([cosT, sinT], axis=0)
    d["cs_pp"] = np.ascontiguousarray(cs.transpose(1, 0, 2))
    return d


def prep_inputs_l1(inputs):
    d = {}
    for nm, key in [("wq", "L1_attn_Wq"), ("wk", "L1_attn_Wk"), ("wv", "L1_attn_Wv"),
                    ("wo", "L1_attn_Wo")]:
        W = np.asarray(inputs[key], np.float32)
        d[nm + "_sl"] = to_slabs(np.ascontiguousarray(W.T), KC, KC)
    for nm, key in [("pre1", "L1_ffn_rn_pre1"), ("pre2", "L1_ffn_rn_pre2"),
                    ("post1", "L1_ffn_rn_post1"), ("post2", "L1_ffn_rn_post2"),
                    ("post", "L1_ffn_rn_post"), ("rsc", "L1_ffn_router_scale")]:
        d[nm + "_pp"] = col_pp(inputs[key], KC)
    rw = np.asarray(inputs["L1_ffn_router_w"], np.float32)
    d["rw_sl"] = np.ascontiguousarray(
        rw.reshape(KC, 128, 8).transpose(1, 0, 2).reshape(1, 128, KC * 8))
    d["pes_row"] = np.asarray(inputs["L1_ffn_per_expert_scale"], np.float32).reshape(1, E)
    d["wg1_sl"] = to_slabs(np.asarray(inputs["L1_ffn_w_gate"], np.float32), KC, 64)
    d["wu1_sl"] = to_slabs(np.asarray(inputs["L1_ffn_w_up"], np.float32), KC, 64)
    d["wd1_sl"] = to_slabs(np.asarray(inputs["L1_ffn_w_down"], np.float32), 64, KC)
    weg = np.asarray(inputs["L1_ffn_we_gate"], np.float32)
    weu = np.asarray(inputs["L1_ffn_we_up"], np.float32)
    wed = np.asarray(inputs["L1_ffn_we_down"], np.float32)
    d["weg_sl"] = np.stack([to_slabs(weg[e], KC, 8) for e in range(E)])
    d["weu_sl"] = np.stack([to_slabs(weu[e], KC, 8) for e in range(E)])
    wed_cat = np.concatenate([wed[e] for e in range(E)], axis=0)
    d["wed_sl"] = to_slabs(np.ascontiguousarray(wed_cat), 64, KC)
    return d


def kernel(**inputs):
    X = np.asarray(inputs["L0_attn_X"], np.float32)
    common = prep_inputs_l0(inputs)
    common.update(prep_inputs_l1(inputs))
    nc = build_program(layers=(0, 1))
    gu88 = common.pop("_gu88")
    d16 = common.pop("_d16")
    in_maps = []
    for b in range(8):
        m = dict(common)
        m["x0"] = np.ascontiguousarray(X[b].T)
        m["gu_my"] = np.ascontiguousarray(gu88[11 * b:11 * b + 11])
        m["d_my"] = np.ascontiguousarray(d16[2 * b:2 * b + 2])
        in_maps.append(m)
    res = run_bass_kernel_spmd(nc, in_maps, core_ids=list(range(8)))
    out = np.stack([np.ascontiguousarray(r["out"].T) for r in res.results])
    return out


# version 12
# speedup vs baseline: 1.0056x; 1.0056x over previous
import numpy as np
import ml_dtypes
from contextlib import ExitStack
import concourse.bass as bass
import concourse.mybir as mybir
from concourse.bass_utils import run_bass_kernel_spmd

F32 = mybir.dt.float32
BF16 = mybir.dt.bfloat16
I32 = mybir.dt.int32
AF = mybir.ActivationFunctionType
ALU = mybir.AluOpType
AX = mybir.AxisListType

D = 2048
T = 2048
NT = 512
NTAU = T // NT
KC = D // 128
I0 = 5504
MC0 = I0 // 128
E = 8
LN_EPS = 1e-5
RMS_EPS = 1e-6
DENOM1 = float(np.sqrt(np.float32(2048.0)) / np.float32(2048.0))
ROOT = float(np.float32(2048.0) ** np.float32(-0.5))
BIG = 1e30


def to_slabs(w, nk, nm):
    K, M = w.shape
    assert K == nk * 128 and M == nm * 128
    return np.ascontiguousarray(
        w.reshape(nk, 128, nm, 128).transpose(2, 1, 0, 3).reshape(nm, 128, nk * 128))


def col_pp(v, n):
    return np.ascontiguousarray(np.asarray(v, np.float32).reshape(n, 128).T)


class Sem:
    def __init__(self, handle):
        self.h = handle
        self.n = 0


class Prog:
    def __init__(self):
        self.ops = {k: [] for k in ("sync", "scalar", "vector", "tensor", "gpsimd")}
        self.auto = {}

    def _add(self, eng, fn, inc=None):
        asem = self.auto.get(eng)
        if asem is not None:
            if asem.n > 0:
                self.ops[eng].append(lambda e, s=asem.h, v=asem.n: e.wait_ge(s, v))
            if inc is None:
                inc = (asem, 1)
        if inc is None:
            self.ops[eng].append(fn)
        else:
            sem, k = inc
            sem.n += k
            self.ops[eng].append(lambda e, fn=fn, s=sem.h, k=k: fn(e).then_inc(s, k))

    def vec(self, fn, inc=None):
        self._add("vector", fn, inc)

    def act(self, fn, inc=None):
        self._add("scalar", fn, inc)

    def pe(self, fn, inc=None):
        self._add("tensor", fn, inc)

    def dma(self, q, sem, fn):
        self._add(q, fn, (sem, 16))

    def wait(self, eng, sem, val=None):
        v = sem.n if val is None else val
        if v > 0:
            self.ops[eng].append(lambda e, s=sem.h, v=v: e.wait_ge(s, v))


def build_program(layers=(0, 1), x1_out=False):
    nc = bass.Bass("TRN2", target_bir_lowering=False)
    dtn = {}

    def dram(name, shape, dtype=F32, kind="ExternalInput"):
        dtn[name] = nc.dram_tensor(name, list(shape), dtype, kind=kind).ap()
        return dtn[name]

    x0 = dram("x0", [D, T])
    if 0 in layers:
        dram("win_sl", [48, 128, KC * 128])
        dram("bin_pp", [128, 48])
        dram("lnw_pp", [128, KC]); dram("lnb_pp", [128, KC])
        dram("wout_sl", [KC, 128, KC * 128])
        dram("bout_pp", [128, KC])
        dram("lut_row", [1, 16])
        dram("gu_my", [11, 128, KC * 128], I32)
        dram("d_my", [2, 128, MC0 * 128], I32)
        dram("cs_pp", [128, 8, T])
        dram("win_b", [48, 128, KC * 128], BF16, kind="Internal")
        dram("wout_b", [KC, 128, KC * 128], BF16, kind="Internal")
        dram("gu_loc", [11 * 128, KC * 128], BF16, kind="Internal")
        dram("d_loc", [2 * 128, MC0 * 128], BF16, kind="Internal")
        dram("gu_all", [88 * 128, KC * 128], BF16, kind="Internal")
        dram("d_all", [16 * 128, MC0 * 128], BF16, kind="Internal")
    if 1 in layers:
        dram("wq_sl", [KC, 128, KC * 128]); dram("wk_sl", [KC, 128, KC * 128])
        dram("wv_sl", [KC, 128, KC * 128]); dram("wo_sl", [KC, 128, KC * 128])
        dram("pre1_pp", [128, KC]); dram("pre2_pp", [128, KC])
        dram("post1_pp", [128, KC]); dram("post2_pp", [128, KC]); dram("post_pp", [128, KC])
        dram("rsc_pp", [128, KC])
        dram("rw_sl", [1, 128, KC * 8])
        dram("pes_row", [1, E])
        dram("wg1_sl", [64, 128, KC * 128]); dram("wu1_sl", [64, 128, KC * 128])
        dram("wd1_sl", [KC, 128, 64 * 128])
        dram("weg_sl", [E, 8, 128, KC * 128]); dram("weu_sl", [E, 8, 128, KC * 128])
        dram("wed_sl", [KC, 128, 64 * 128])
        dram("wq_b", [KC, 128, KC * 128], BF16, kind="Internal")
        dram("wk_b", [KC, 128, KC * 128], BF16, kind="Internal")
        dram("wv_b", [KC, 128, KC * 128], BF16, kind="Internal")
        dram("wo_b", [KC, 128, KC * 128], BF16, kind="Internal")
        dram("wg1_b", [64, 128, KC * 128], BF16, kind="Internal")
        dram("wu1_b", [64, 128, KC * 128], BF16, kind="Internal")
        dram("wd1_b", [KC, 128, 64 * 128], BF16, kind="Internal")
        dram("weg_b", [E, 8, 128, KC * 128], BF16, kind="Internal")
        dram("weu_b", [E, 8, 128, KC * 128], BF16, kind="Internal")
        dram("wed_b", [KC, 128, 64 * 128], BF16, kind="Internal")
        dram("h1_scr", [64 * 128, T], BF16, kind="Internal")
        dram("hew_scr", [64 * 128, T], BF16, kind="Internal")
        dram("dno_scr", [D, T], kind="Internal")
        dram("mo_scr", [D, T], kind="Internal")
        dram("s_scr", [D, T], kind="Internal")
        dram("acc_scr", [D, T], kind="Internal")
    x1_kind = "ExternalOutput" if x1_out else "Internal"
    dram("x1_scr", [D, T], kind=x1_kind)
    if 1 in layers:
        dram("out", [D, T], kind="ExternalOutput")

    ones_k_c = nc.inline_tensor(np.ones((128, 1), dtype=np.float32), "ones_k").ap()
    ones_r_c = nc.inline_tensor(np.ones((1, 128), dtype=np.float32), "ones_r").ap()
    ident_c = nc.inline_tensor(np.eye(128, dtype=np.float32), "ident_c").ap()
    eps_c = nc.inline_tensor(np.array([[LN_EPS, RMS_EPS, 0.0, 0.0]], dtype=np.float32),
                             "eps_c").ap()

    P = Prog()
    with ExitStack() as octx:
        def sb(ctx, name, shape, dtp=F32):
            return ctx.enter_context(nc.sbuf_tensor(name, list(shape), dtp))

        smisc = Sem(octx.enter_context(nc.semaphore("smisc")))
        sdx = [Sem(octx.enter_context(nc.semaphore(f"sdx{i}"))) for i in range(2)]
        sdw = [Sem(octx.enter_context(nc.semaphore(f"sdw{i}"))) for i in range(2)]
        gdw = [Sem(octx.enter_context(nc.semaphore(f"gdw{i}"))) for i in range(2)]
        sdo = [Sem(octx.enter_context(nc.semaphore(f"sdo{i}"))) for i in range(2)]
        sdd = [Sem(octx.enter_context(nc.semaphore(f"sdd{i}"))) for i in range(2)]
        gdd = [Sem(octx.enter_context(nc.semaphore(f"gdd{i}"))) for i in range(2)]
        sca = [Sem(octx.enter_context(nc.semaphore(f"sca{i}"))) for i in range(2)]
        ccs = Sem(octx.enter_context(nc.semaphore("ccs")))
        scp = [Sem(octx.enter_context(nc.semaphore(f"scp{i}"))) for i in range(4)]
        sco = [Sem(octx.enter_context(nc.semaphore(f"sco{i}"))) for i in range(4)]
        sv = Sem(octx.enter_context(nc.semaphore("sv")))
        sa = Sem(octx.enter_context(nc.semaphore("sa")))
        sp = Sem(octx.enter_context(nc.semaphore("sp")))
        P.auto = {"vector": sv, "scalar": sa}

        pa = octx.enter_context(nc.psum_tensor("pa", [128, NT], F32))
        pb = octx.enter_context(nc.psum_tensor("pb", [128, NT], F32))
        pc_ = octx.enter_context(nc.psum_tensor("pc", [128, NT], F32))
        pd_ = octx.enter_context(nc.psum_tensor("pd", [128, NT], F32))
        pr1 = octx.enter_context(nc.psum_tensor("pr1", [1, NT], F32))
        pr2 = octx.enter_context(nc.psum_tensor("pr2", [1, NT], F32))
        pbc = octx.enter_context(nc.psum_tensor("pbc", [128, NT], F32))
        pt = octx.enter_context(nc.psum_tensor("pt", [128, NT], F32))

        ppar = sb(octx, "ppar", [128, 224])
        lutc = sb(octx, "lutc", [128, 16])
        acol = sb(octx, "acol", [128, KC])
        pes_b = sb(octx, "pes_b", [128, E])
        onesk_s = sb(octx, "onesk_s", [128, 1])
        onesr_s = sb(octx, "onesr_s", [1, 128])
        ident_s = sb(octx, "ident_s", [128, 128])
        sc1 = sb(octx, "sc1", [128, NT]); sc2 = sb(octx, "sc2", [128, NT])
        sc3 = sb(octx, "sc3", [128, NT]); sc4 = sb(octx, "sc4", [128, NT])
        sc5 = sb(octx, "sc5", [128, NT])
        r1 = sb(octx, "r1", [1, NT]); r2 = sb(octx, "r2", [1, NT]); r3 = sb(octx, "r3", [1, NT])
        eps_s = sb(octx, "eps_s", [1, 4])
        bcA = sb(octx, "bcA", [128, NT]); bcB = sb(octx, "bcB", [128, NT])
        rs_b = sb(octx, "rs_b", [128, T]); mr_b = sb(octx, "mr_b", [128, T])
        rrms_b = sb(octx, "rrms_b", [128, T])
        wsl = [sb(octx, f"wsl{i}", [128, KC * 128], BF16) for i in range(2)]
        rtm = sb(octx, "rtm", [128, KC * 8])
        raff = sb(octx, "raff", [128, KC * 8])
        rsc1 = sb(octx, "rsc1", [128, KC * 8])
        rcol = sb(octx, "rcol", [128, 12])

        PP = {}
        _pp_next = [0]

        def pp_alloc(name, n):
            PP[name] = _pp_next[0]
            _pp_next[0] += n

        def pcol(name, i):
            o = PP[name] + i
            return ppar[:, o:o + 1]

        if 0 in layers:
            pp_alloc("bin", 48); pp_alloc("lnw", KC); pp_alloc("lnb", KC); pp_alloc("bout", KC)
        if 1 in layers:
            pp_alloc("pre1", KC); pp_alloc("pre2", KC); pp_alloc("post1", KC)
            pp_alloc("post2", KC); pp_alloc("post", KC); pp_alloc("rsc", KC)

        def dma_s(dst, src, sem):
            P.dma("sync", sem, lambda e, d=dst, s=src: e.dma_start(out=d, in_=s))

        def dma_g(dst, src, sem):
            P.dma("gpsimd", sem, lambda e, d=dst, s=src: e.dma_start(out=d, in_=s))

        def load_slab_b(dst_tile, src_ap, b):
            P.dma("scalar", sca[b],
                  lambda e, d=dst_tile[:], s2=src_ap: e.dma_start(out=d, in_=s2))
            return sca[b].n

        l0w_rdy = [0]

        def preconvert():
            specs = []
            if 0 in layers:
                specs += [("win_sl", "win_b", 48, KC * 128), ("wout_sl", "wout_b", KC, KC * 128)]
            if 1 in layers:
                specs += [("wq_sl", "wq_b", KC, KC * 128), ("wk_sl", "wk_b", KC, KC * 128),
                          ("wv_sl", "wv_b", KC, KC * 128), ("wo_sl", "wo_b", KC, KC * 128),
                          ("wg1_sl", "wg1_b", 64, KC * 128), ("wu1_sl", "wu1_b", 64, KC * 128),
                          ("wd1_sl", "wd1_b", KC, 64 * 128), ("wed_sl", "wed_b", KC, 64 * 128),
                          ("weg_sl", "weg_b", E * 8, KC * 128), ("weu_sl", "weu_b", E * 8, KC * 128)]
            CH = 4 * 1024 * 1024
            for si, (srcn, dstn, nm, F) in enumerate(specs):
                src = dtn[srcn]
                dst = dtn[dstn]
                tot = nm * 128 * F
                for off in range(0, tot, CH):
                    w = min(CH, tot - off)
                    P.dma("gpsimd", gdw[0],
                          lambda e, d=bass.AP(dst.tensor, off, [[1, 1], [1, w]]),
                          s2=bass.AP(src.tensor, off, [[1, 1], [1, w]]):
                          e.dma_start(out=d, in_=s2))
                if srcn == "wout_sl":
                    l0w_rdy[0] = gdw[0].n

        def chunk_ap(dram_ap, c, tau, rows=128, ncols=T, w=NT):
            return bass.AP(dram_ap.tensor, c * 128 * ncols + tau * w, [[ncols, rows], [1, w]])

        def chunk4_ap(dram_ap, q, tau):
            return bass.AP(dram_ap.tensor, q * 4 * 128 * T + tau * NT,
                           [[T, 128], [128 * T, 4], [1, NT]])

        _xb_marks = {}

        def stream4(src_dram, tau, consume, xb, extra=None):
            key = id(xb[0])
            marks = _xb_marks.setdefault(key, [(0, 0, 0), (0, 0, 0)])
            for q in range(4):
                b = q % 2
                mv, ma, mp = marks[b]
                P.wait("sync", sv, mv)
                P.wait("sync", sa, ma)
                P.wait("sync", sp, mp)
                if extra is not None:
                    extra()
                dma_s(xb[b][:].rearrange("p (c j) -> p c j", j=NT),
                      chunk4_ap(src_dram, q, tau), sdx[b])
                ld = sdx[b].n
                for j in range(4):
                    consume(4 * q + j, xb[b][:, j * NT:(j + 1) * NT], b, ld)
                marks[b] = (sv.n, sa.n, sp.n)

        def cm(tile, c, tau=None, w=NT):
            if tau is None:
                return tile[:, c * w:(c + 1) * w]
            return tile[:, (c * NTAU + tau) * w:(c * NTAU + tau + 1) * w]

        def bcast(row_ap, dst):
            P.wait("tensor", sv)
            P.pe(lambda t, r=row_ap: t.matmul(pbc[:], onesr_s[:], r, start=True, stop=True),
                 inc=(sp, 1))
            P.wait("vector", sp)
            P.vec(lambda v, d=dst: v.tensor_copy(out=d, in_=pbc[:]), inc=(sv, 1))

        mm_banks = [pa, pb, pc_, pd_]
        bank_free_at = [(0, 0)] * 4
        bank_ctr = [0]

        def mm_group(rhs_views, lhsT_views, evac, bank_pair=0, pre=None):
            g = bank_ctr[0]; bank_ctr[0] += 1
            bi = bank_pair * 2 + (g % 2)
            bank = mm_banks[bi]
            P.wait("tensor", sv, bank_free_at[bi][0])
            P.wait("tensor", sa, bank_free_at[bi][1])
            if pre is not None:
                pre()
            nk = len(rhs_views)
            for i in range(nk):
                P.pe(lambda t, rv=rhs_views[i], lv=lhsT_views[i], i=i, nk=nk, bank=bank:
                     t.matmul(bank[:], lv, rv, start=(i == 0), stop=(i == nk - 1)),
                     inc=(sp, 1) if i == nk - 1 else None)
            P.wait("vector", sp)
            evac(bank)
            bank_free_at[bi] = (sv.n, sa.n)

        def _stats_chunk(c, view, sq_ping, sq_done_at, do_sum):
            if do_sum:
                P.pe(lambda t, v=view, c=c: t.matmul(pr1[:], onesk_s[:], v,
                                                     start=(c == 0), stop=(c == KC - 1)),
                     inc=(sp, 1) if c == KC - 1 else None)
            sq = sq_ping[c % 2]
            P.wait("scalar", sp, sq_done_at[c % 2])
            P.act(lambda s, v=view, sq=sq: s.activation(out=sq[:], in_=v, func=AF.Square),
                  inc=(sa, 1))
            P.wait("tensor", sa)
            P.pe(lambda t, sq=sq, c=c: t.matmul(pr2[:], onesk_s[:], sq[:],
                                                start=(c == 0), stop=(c == KC - 1)),
                 inc=(sp, 1))
            sq_done_at[c % 2] = sp.n

        def stats_pass(src_dram, do_sum, per_chunk=None, xb=None):
            sq_ping = [sc1, sc2]
            sq_done_at = [0, 0]
            P.wait("tensor", sv, pr_free[0])
            if per_chunk is None:
                def consume(c, view, b, ld):
                    P.wait("tensor", sdx[b], ld)
                    P.wait("scalar", sdx[b], ld)
                    _stats_chunk(c, view, sq_ping, sq_done_at, do_sum)
                stream4(src_dram, stats_pass.tau, consume, xb)
            else:
                for c in range(KC):
                    view, ready = per_chunk(c)
                    if ready is not None:
                        ready()
                    _stats_chunk(c, view, sq_ping, sq_done_at, do_sum)
        stats_pass.tau = 0

        pr_free = [0]

        def rowmath_musd(eps, out_rs_in_r2, with_mu=True):
            P.wait("vector", sp)
            if with_mu:
                P.vec(lambda v: v.tensor_scalar(out=r1[:], in0=pr1[:], scalar1=1.0 / D,
                                                scalar2=None, op0=ALU.mult))
            P.vec(lambda v: v.tensor_scalar(out=r2[:], in0=pr2[:], scalar1=1.0 / D,
                                            scalar2=None, op0=ALU.mult), inc=(sv, 1))
            P.wait("scalar", sv)
            if with_mu:
                P.act(lambda s: s.activation(out=r3[:], in_=r1[:], func=AF.Square), inc=(sa, 1))
                P.wait("vector", sa)
                P.vec(lambda v: v.tensor_tensor(out=r2[:], in0=r2[:], in1=r3[:],
                                                op=ALU.subtract), inc=(sv, 1))
                P.wait("scalar", sv)
            P.act(lambda s, eps=eps: s.activation(out=r3[:], in_=r2[:], func=AF.Sqrt,
                                                  bias=eps_s[0:1, eps:eps + 1]), inc=(sa, 1))
            P.wait("vector", sa)
            P.vec(lambda v: v.reciprocal(out=r2[:], in_=r3[:]), inc=(sv, 1))
            pr_free[0] = sv.n

        def load_params():
            if 0 in layers:
                dma_s(ppar[:, PP["bin"]:PP["bin"] + 48], dtn["bin_pp"], smisc)
                dma_s(ppar[:, PP["lnw"]:PP["lnw"] + KC], dtn["lnw_pp"], smisc)
                dma_s(ppar[:, PP["lnb"]:PP["lnb"] + KC], dtn["lnb_pp"], smisc)
                dma_s(ppar[:, PP["bout"]:PP["bout"] + KC], dtn["bout_pp"], smisc)
            if 1 in layers:
                dma_s(ppar[:, PP["pre1"]:PP["pre1"] + KC], dtn["pre1_pp"], smisc)
                dma_s(ppar[:, PP["pre2"]:PP["pre2"] + KC], dtn["pre2_pp"], smisc)
                dma_s(ppar[:, PP["post1"]:PP["post1"] + KC], dtn["post1_pp"], smisc)
                dma_s(ppar[:, PP["post2"]:PP["post2"] + KC], dtn["post2_pp"], smisc)
                dma_s(ppar[:, PP["post"]:PP["post"] + KC], dtn["post_pp"], smisc)
                dma_s(ppar[:, PP["rsc"]:PP["rsc"] + KC], dtn["rsc_pp"], smisc)
            dma_s(eps_s[:], eps_c, smisc)
            dma_s(onesk_s[:], ones_k_c, smisc)
            dma_s(onesr_s[:], ones_r_c, smisc)
            dma_s(ident_s[:], ident_c, smisc)
            if 0 in layers:
                dma_s(r1[0:1, 0:16], dtn["lut_row"], smisc)
                P.wait("tensor", smisc)
                P.pe(lambda t: t.matmul(pbc[:, 0:16], onesr_s[:], r1[0:1, 0:16],
                                        start=True, stop=True), inc=(sp, 1))
                P.wait("vector", sp)
                P.vec(lambda v: v.tensor_copy(out=lutc[:], in_=pbc[:, 0:16]), inc=(sv, 1))
                P.wait("vector", sv)
            if 1 in layers:
                P.wait("sync", sp)
                dma_s(r1[0:1, 0:E], dtn["pes_row"], smisc)
                P.wait("tensor", smisc)
                P.wait("tensor", sv)
                P.pe(lambda t: t.matmul(pbc[:, 0:E], onesr_s[:], r1[0:1, 0:E],
                                        start=True, stop=True), inc=(sp, 1))
                P.wait("vector", sp)
                P.vec(lambda v: v.tensor_copy(out=pes_b[:], in_=pbc[:, 0:E]), inc=(sv, 1))
                P.wait("vector", sv)
            P.wait("vector", smisc)
            P.wait("scalar", smisc)
            P.wait("tensor", smisc)

        def ghost_mat(ctx):
            HF = MC0 * 128 // 2
            idxb = [sb(ctx, f"idxb{i}", [128, HF], I32) for i in range(2)]
            idbf = sb(ctx, "idbf", [128, HF], BF16)
            gacc = sb(ctx, "gacc0", [128, HF], BF16)
            gtmp = sb(ctx, "gtmp", [128, HF], BF16)
            work = []
            for si2 in range(11):
                src = dtn["gu_my"][si2]
                dst = bass.AP(dtn["gu_loc"].tensor, si2 * 128 * KC * 128,
                              [[KC * 128, 128], [1, KC * 128]])
                work.append((src, dst, KC * 128))
            for si2 in range(2):
                for h in range(2):
                    src = bass.AP(dtn["d_my"].tensor, si2 * 128 * MC0 * 128 + h * HF,
                                  [[MC0 * 128, 128], [1, HF]])
                    dst = bass.AP(dtn["d_loc"].tensor, si2 * 128 * MC0 * 128 + h * HF,
                                  [[MC0 * 128, 128], [1, HF]])
                    work.append((src, dst, HF))
            acc_store = [0]
            conv_done = {}
            for si, (src, dst, F) in enumerate(work):
                b = si % 2
                if si >= 2:
                    P.wait("scalar", sv, conv_done[si - 2])
                P.dma("scalar", sdw[b],
                      lambda e, d=idxb[b][:, 0:F], s2=src: e.dma_start(out=d, in_=s2))
                P.wait("vector", sdw[b])
                P.vec(lambda v, b=b, F=F: v.tensor_copy(out=idbf[:, 0:F],
                                                        in_=idxb[b][:, 0:F]), inc=(sv, 1))
                conv_done[si] = sv.n
                P.wait("vector", sdo[0], acc_store[0])
                P.vec(lambda v, F=F: v.tensor_scalar(
                    out=gacc[:, 0:F], in0=idbf[:, 0:F], scalar1=0.0,
                    scalar2=lutc[:, 0:1], op0=ALU.is_equal, op1=ALU.mult))
                for k in range(1, 16):
                    P.vec(lambda v, k=k, F=F: v.tensor_scalar(
                        out=gtmp[:, 0:F], in0=idbf[:, 0:F], scalar1=float(k),
                        scalar2=lutc[:, k:k + 1], op0=ALU.is_equal, op1=ALU.mult))
                    P.vec(lambda v, F=F: v.tensor_tensor(
                        out=gacc[:, 0:F], in0=gacc[:, 0:F], in1=gtmp[:, 0:F],
                        op=ALU.add), inc=(sv, 1) if k == 15 else None)
                P.wait("scalar", sv)
                P.dma("scalar", sdo[0],
                      lambda e, d=dst, s2=gacc[:, 0:F]: e.dma_start(out=d, in_=s2))
                acc_store[0] = sdo[0].n
            P.wait("gpsimd", sdo[0])
            P._add("gpsimd", lambda e: e.collective_compute(
                "AllGather", ALU.bypass, replica_groups=[list(range(8))],
                ins=[dtn["gu_loc"]], outs=[dtn["gu_all"]]), inc=(ccs, 1))
            P._add("gpsimd", lambda e: e.collective_compute(
                "AllGather", ALU.bypass, replica_groups=[list(range(8))],
                ins=[dtn["d_loc"]], outs=[dtn["d_all"]]), inc=(ccs, 1))

        def layer0_stats_h(ctx, big_a):
            xn32 = sb(ctx, "xn32", [128, KC * NT])
            xb0 = [sb(ctx, "xb0a", [128, 4 * NT]), sb(ctx, "xb0b", [128, 4 * NT])]
            for tau in range(NTAU):
                stats_pass.tau = tau
                stats_pass(x0, do_sum=True, xb=xb0)
                rowmath_musd(0, r2)
                bcast(r2[:], rs_b[:, tau * NT:(tau + 1) * NT])
                P.vec(lambda v: v.tensor_tensor(out=r3[:], in0=r1[:], in1=r2[:], op=ALU.mult),
                      inc=(sv, 1))
                bcast(r3[:], mr_b[:, tau * NT:(tau + 1) * NT])
            for tau in range(NTAU):
                rsb = rs_b[:, tau * NT:(tau + 1) * NT]
                mrb = mr_b[:, tau * NT:(tau + 1) * NT]
                sq_ping = [sc1, sc2]
                sq_done_at = [0, 0]
                P.wait("tensor", sv, pr_free[0])

                def consume(c, view, b, ld, tau=tau, rsb=rsb, mrb=mrb,
                            sq_ping=sq_ping, sq_done_at=sq_done_at):
                    P.wait("vector", sdx[b], ld)
                    P.vec(lambda v, c=c, view=view, rsb=rsb: v.tensor_tensor(
                        out=cm(xn32, c), in0=view, in1=rsb, op=ALU.mult))
                    P.vec(lambda v, c=c, mrb=mrb: v.tensor_tensor(
                        out=cm(xn32, c), in0=cm(xn32, c), in1=mrb, op=ALU.subtract),
                        inc=(sv, 1))
                    P.wait("tensor", sv)
                    P.wait("scalar", sv)
                    _stats_chunk(c, cm(xn32, c), sq_ping, sq_done_at, True)
                stream4(x0, tau, consume, xb0)
                rowmath_musd(0, r2)
                bcast(r1[:], bcA[:])
                bcast(r2[:], bcB[:])
                for c in range(KC):
                    P.vec(lambda v, c=c: v.tensor_tensor(out=sc1[:], in0=cm(xn32, c),
                                                         in1=bcA[:], op=ALU.subtract))
                    P.vec(lambda v: v.tensor_tensor(out=sc1[:], in0=sc1[:], in1=bcB[:],
                                                    op=ALU.mult))
                    P.vec(lambda v, c=c, tau=tau: v.tensor_scalar(
                        out=cm(big_a, c, tau), in0=sc1[:], scalar1=pcol("lnw", c),
                        scalar2=pcol("lnb", c), op0=ALU.mult, op1=ALU.add),
                        inc=(sv, 1) if c == KC - 1 else None)

        def layer0_attn(ctx, big_a, big_b):
            kraw = sb(ctx, "kraw", [128, KC * NT], BF16)
            costl = sc3
            sintl = sc4
            h_rdy = sv.n
            wsl_mark = [0, 0]
            slab_ctr = [0]
            P.wait("scalar", gdw[0], l0w_rdy[0])
            P.vec(lambda v: v.memset(acol[:], 0.0), inc=(sv, 1))

            def load_slab(src_ap):
                i = slab_ctr[0]; slab_ctr[0] += 1
                b = i % 2
                P.wait("scalar", sp, wsl_mark[b])
                return b, load_slab_b(wsl[b], src_ap, b)

            def rope_pass(tile, tau, with_norm):
                for c in range(4):
                    cA, cB = c, c + 4
                    P.wait("sync", sv)
                    dma_s(costl[:], bass.AP(dtn["cs_pp"].tensor, c * T + tau * NT,
                                            [[8 * T, 128], [1, NT]]), sdx[0])
                    dma_s(sintl[:], bass.AP(dtn["cs_pp"].tensor, (4 + c) * T + tau * NT,
                                            [[8 * T, 128], [1, NT]]), sdx[1])
                    P.wait("vector", sdx[0])
                    P.wait("vector", sdx[1])
                    P.vec(lambda v, tile=tile, cA=cA, tau=tau: v.tensor_tensor(
                        out=sc1[:], in0=cm(tile, cA, tau), in1=costl[:], op=ALU.mult))
                    P.vec(lambda v, tile=tile, cB=cB, tau=tau: v.tensor_tensor(
                        out=sc2[:], in0=cm(tile, cB, tau), in1=sintl[:], op=ALU.mult))
                    P.vec(lambda v: v.tensor_tensor(out=sc1[:], in0=sc1[:], in1=sc2[:],
                                                    op=ALU.subtract))
                    P.vec(lambda v, tile=tile, cB=cB, tau=tau: v.tensor_tensor(
                        out=sc2[:], in0=cm(tile, cB, tau), in1=costl[:], op=ALU.mult))
                    P.vec(lambda v, tile=tile, cA=cA, tau=tau: v.tensor_tensor(
                        out=sc5[:], in0=cm(tile, cA, tau), in1=sintl[:], op=ALU.mult))
                    P.vec(lambda v: v.tensor_tensor(out=sc2[:], in0=sc2[:], in1=sc5[:],
                                                    op=ALU.add))
                    P.vec(lambda v, tile=tile, cA=cA, tau=tau: v.tensor_copy(
                        out=cm(tile, cA, tau), in_=sc1[:]))
                    P.vec(lambda v, tile=tile, cB=cB, tau=tau: v.tensor_copy(
                        out=cm(tile, cB, tau), in_=sc2[:]), inc=(sv, 1))
                if with_norm:
                    def per_chunk(c, tile=tile, tau=tau):
                        def ready(c=c):
                            P.wait("scalar", sv)
                            P.wait("tensor", sv)
                        return cm(tile, c, tau), ready
                    stats_pass(None, do_sum=False, per_chunk=per_chunk)
                    P.wait("vector", sp)
                    P.vec(lambda v: v.tensor_copy(out=r3[:], in_=pr2[:]), inc=(sv, 1))
                    P.wait("scalar", sv)
                    P.act(lambda s: s.activation(out=r3[:], in_=r3[:], func=AF.Sqrt),
                          inc=(sa, 1))
                    P.wait("vector", sa)
                    P.vec(lambda v: v.reciprocal(out=r2[:], in_=r3[:]), inc=(sv, 1))
                    bcast(r2[:], bcA[:])

            for tau in range(NTAU):
                for mc in range(KC):
                    b, ld = load_slab(dtn["win_b"][16 + mc])

                    def pre(ld=ld, tau=tau, mc=mc, b=b):
                        P.wait("tensor", sca[b], ld)
                        if tau == 0 and mc == 0:
                            P.wait("tensor", sv, h_rdy)

                    def evk(bank, mc=mc, tau=tau):
                        P.vec(lambda v, bank=bank, mc=mc, tau=tau: v.tensor_scalar(
                            out=cm(kraw, mc), in0=bank[:], scalar1=pcol("bin", 16 + mc),
                            scalar2=None, op0=ALU.add), inc=(sv, 1))
                    mm_group([cm(big_a, kc, tau) for kc in range(KC)],
                             [wsl[b][:, kc * 128:(kc + 1) * 128] for kc in range(KC)],
                             evk, bank_pair=0, pre=pre)
                    wsl_mark[b] = sp.n
                for c in range(4):
                    cA, cB = c, c + 4
                    P.wait("sync", sv)
                    dma_s(costl[:], bass.AP(dtn["cs_pp"].tensor, c * T + tau * NT,
                                            [[8 * T, 128], [1, NT]]), sdx[0])
                    dma_s(sintl[:], bass.AP(dtn["cs_pp"].tensor, (4 + c) * T + tau * NT,
                                            [[8 * T, 128], [1, NT]]), sdx[1])
                    P.wait("vector", sdx[0])
                    P.wait("vector", sdx[1])
                    P.vec(lambda v, cA=cA: v.tensor_tensor(
                        out=sc1[:], in0=cm(kraw, cA), in1=costl[:], op=ALU.mult))
                    P.vec(lambda v, cB=cB: v.tensor_tensor(
                        out=sc2[:], in0=cm(kraw, cB), in1=sintl[:], op=ALU.mult))
                    P.vec(lambda v: v.tensor_tensor(out=sc1[:], in0=sc1[:], in1=sc2[:],
                                                    op=ALU.subtract))
                    P.vec(lambda v, cB=cB: v.tensor_tensor(
                        out=sc2[:], in0=cm(kraw, cB), in1=costl[:], op=ALU.mult))
                    P.vec(lambda v, cA=cA: v.tensor_tensor(
                        out=sc5[:], in0=cm(kraw, cA), in1=sintl[:], op=ALU.mult))
                    P.vec(lambda v: v.tensor_tensor(out=sc2[:], in0=sc2[:], in1=sc5[:],
                                                    op=ALU.add))
                    P.vec(lambda v, cA=cA: v.tensor_copy(out=cm(kraw, cA), in_=sc1[:]))
                    P.vec(lambda v, cB=cB: v.tensor_copy(out=cm(kraw, cB), in_=sc2[:]),
                          inc=(sv, 1))
                def per_chunk_k(c):
                    def ready():
                        P.wait("scalar", sv)
                        P.wait("tensor", sv)
                    return cm(kraw, c), ready
                stats_pass(None, do_sum=False, per_chunk=per_chunk_k)
                P.wait("vector", sp)
                P.vec(lambda v: v.tensor_copy(out=r3[:], in_=pr2[:]), inc=(sv, 1))
                P.wait("scalar", sv)
                P.act(lambda s: s.activation(out=r3[:], in_=r3[:], func=AF.Sqrt), inc=(sa, 1))
                P.wait("vector", sa)
                P.vec(lambda v: v.reciprocal(out=r2[:], in_=r3[:]), inc=(sv, 1))
                bcast(r2[:], bcA[:])

                for mc in range(KC):
                    b, ld = load_slab(dtn["win_b"][32 + mc])

                    def prev(ld=ld, b=b):
                        P.wait("tensor", sca[b], ld)

                    def evv(bank, mc=mc, tau=tau):
                        P.vec(lambda v, bank=bank, mc=mc: v.tensor_scalar(
                            out=sc1[:], in0=bank[:], scalar1=pcol("bin", 32 + mc),
                            scalar2=None, op0=ALU.add))
                        P.vec(lambda v, mc=mc: v.tensor_tensor(
                            out=sc1[:], in0=sc1[:], in1=cm(kraw, mc), op=ALU.mult))
                        P.vec(lambda v: v.tensor_tensor(out=sc1[:], in0=sc1[:], in1=bcA[:],
                                                        op=ALU.mult))
                        P.vec(lambda v: v.reduce_sum(out=sc2[:, 0:1], in_=sc1[:], axis=AX.X))
                        P.vec(lambda v, mc=mc: v.tensor_tensor(
                            out=acol[:, mc:mc + 1], in0=acol[:, mc:mc + 1],
                            in1=sc2[:, 0:1], op=ALU.add), inc=(sv, 1))
                    mm_group([cm(big_a, kc, tau) for kc in range(KC)],
                             [wsl[b][:, kc * 128:(kc + 1) * 128] for kc in range(KC)],
                             evv, bank_pair=0, pre=prev)
                    wsl_mark[b] = sp.n

                for mc in range(KC):
                    b, ld = load_slab(dtn["win_b"][mc])

                    def preq(ld=ld, b=b):
                        P.wait("tensor", sca[b], ld)

                    def evq(bank, mc=mc, tau=tau):
                        P.vec(lambda v, bank=bank, mc=mc, tau=tau: v.tensor_scalar(
                            out=cm(big_b, mc, tau), in0=bank[:], scalar1=pcol("bin", mc),
                            scalar2=None, op0=ALU.add), inc=(sv, 1))
                    mm_group([cm(big_a, kc, tau) for kc in range(KC)],
                             [wsl[b][:, kc * 128:(kc + 1) * 128] for kc in range(KC)],
                             evq, bank_pair=0, pre=preq)
                    wsl_mark[b] = sp.n
                rope_pass(big_b, tau, with_norm=False)

            P.wait("vector", sv)
            for c in range(KC):
                P.vec(lambda v, c=c: v.tensor_scalar(
                    out=big_b[:, c * T:(c + 1) * T], in0=big_b[:, c * T:(c + 1) * T],
                    scalar1=acol[:, c:c + 1], scalar2=None, op0=ALU.mult),
                    inc=(sv, 1) if c == KC - 1 else None)
            y_rdy = sv.n

            for mc in range(KC):
                b, ld = load_slab(dtn["wout_b"][mc])
                for tau in range(NTAU):
                    P.wait("sync", sv)
                    dma_s(sc4[:], chunk_ap(x0, mc, tau), sdx[1])
                    xld = sdx[1].n

                    def pred(ld=ld, tau=tau, b=b):
                        if tau == 0:
                            P.wait("tensor", sca[b], ld)
                            P.wait("tensor", sv, y_rdy)

                    def evd(bank, mc=mc, tau=tau, xld=xld):
                        P.wait("vector", sdx[1], xld)
                        rsb = rs_b[:, tau * NT:(tau + 1) * NT]
                        mrb = mr_b[:, tau * NT:(tau + 1) * NT]
                        P.vec(lambda v, rsb=rsb: v.tensor_tensor(
                            out=sc1[:], in0=sc4[:], in1=rsb, op=ALU.mult))
                        P.vec(lambda v, mrb=mrb: v.tensor_tensor(
                            out=sc1[:], in0=sc1[:], in1=mrb, op=ALU.subtract))
                        P.vec(lambda v: v.tensor_tensor(out=sc1[:], in0=sc1[:], in1=sc4[:],
                                                        op=ALU.add))
                        P.vec(lambda v, bank=bank, mc=mc: v.tensor_scalar(
                            out=sc2[:], in0=bank[:], scalar1=pcol("bout", mc),
                            scalar2=None, op0=ALU.add))
                        P.wait("vector", sdo[0])
                        P.vec(lambda v: v.tensor_tensor(out=sc3[:], in0=sc1[:], in1=sc2[:],
                                                        op=ALU.add), inc=(sv, 1))
                        P.wait("sync", sv)
                        dma_s(chunk_ap(dtn["x1_scr"], mc, tau), sc3[:], sdo[0])
                    mm_group([cm(big_b, kc, tau) for kc in range(KC)],
                             [wsl[b][:, kc * 128:(kc + 1) * 128] for kc in range(KC)],
                             evd, bank_pair=0, pre=pred)
                wsl_mark[b] = sp.n

        def layer0_ffn(ctx):
            xn_bf = sb(ctx, "xn_bf", [128, KC * NT], BF16)
            xb4 = [sb(ctx, "xb4a", [128, 4 * NT]), sb(ctx, "xb4b", [128, 4 * NT])]
            hss = sb(ctx, "hss", [128, MC0 * NT], BF16)
            wg2 = [sb(ctx, f"wg2_{i}", [128, KC * 128], BF16) for i in range(2)]
            wdl = [sb(ctx, f"wdl{i}", [128, MC0 * 128], BF16) for i in range(2)]
            gu_mark = [sp.n, sp.n]
            wd_mark = [0, 0]
            P.wait("sync", sdo[0])
            P.wait("sync", sdo[1])
            P.wait("sync", ccs)
            for tau in range(NTAU):
                rsb = rs_b[:, tau * NT:(tau + 1) * NT]
                mrb = mr_b[:, tau * NT:(tau + 1) * NT]
                for c in range(KC):
                    buf = [sc3, sc4][c % 2]
                    P.wait("sync", sv)
                    P.wait("sync", sdo[0])
                    dma_s(buf[:], chunk_ap(x0, c, tau), sdx[c % 2])
                    P.wait("vector", sdx[c % 2])
                    P.vec(lambda v, buf=buf, rsb=rsb: v.tensor_tensor(
                        out=sc1[:], in0=buf[:], in1=rsb, op=ALU.mult))
                    P.vec(lambda v, c=c, mrb=mrb: v.tensor_tensor(
                        out=cm(xn_bf, c), in0=sc1[:], in1=mrb, op=ALU.subtract),
                        inc=(sv, 1))
                xn_rdy = sv.n
                for mc in range(MC0):
                    bg = mc % 2
                    P.wait("sync", sp, gu_mark[bg])
                    dma_s(wsl[bg][:], bass.AP(dtn["gu_all"].tensor, mc * 128 * KC * 128,
                                              [[KC * 128, 128], [1, KC * 128]]), sdw[bg])
                    dma_s(wg2[bg][:], bass.AP(dtn["gu_all"].tensor, (43 + mc) * 128 * KC * 128,
                                              [[KC * 128, 128], [1, KC * 128]]), sdw[bg])
                    ld = sdw[bg].n

                    def preg(ld=ld, xn_rdy=xn_rdy, tau=tau, mc=mc, bg=bg):
                        P.wait("tensor", sdw[bg], ld)
                        if mc == 0:
                            P.wait("tensor", sv, xn_rdy)

                    def evg(bank):
                        P.wait("scalar", sp)
                        P.wait("scalar", sv)
                        P.act(lambda s, bank=bank: s.activation(out=sc2[:], in_=bank[:],
                                                                func=AF.Silu))

                    def evu(bank, mc=mc):
                        P.wait("vector", sa)
                        P.vec(lambda v, bank=bank, mc=mc: v.tensor_tensor(
                            out=cm(hss, mc), in0=sc2[:], in1=bank[:], op=ALU.mult),
                            inc=(sv, 1))
                    mm_group([cm(xn_bf, kc) for kc in range(KC)],
                             [wsl[bg][:, kc * 128:(kc + 1) * 128] for kc in range(KC)],
                             evg, bank_pair=0, pre=preg)
                    mm_group([cm(xn_bf, kc) for kc in range(KC)],
                             [wg2[bg][:, kc * 128:(kc + 1) * 128] for kc in range(KC)],
                             evu, bank_pair=1)
                    gu_mark[bg] = sp.n
                hss_rdy = sv.n
                for hm in range(KC):
                    b = hm % 2
                    P.wait("sync", sp, wd_mark[b])
                    dma_s(wdl[b][:], bass.AP(dtn["d_all"].tensor, hm * 128 * MC0 * 128,
                                             [[MC0 * 128, 128], [1, MC0 * 128]]), sdd[b])
                    ld = sdd[b].n
                    P.wait("sync", sv)
                    dma_s(sc4[:], chunk_ap(dtn["x1_scr"], hm, tau), sdx[1])
                    xld = sdx[1].n

                    def prew(ld=ld, hss_rdy=hss_rdy, hm=hm, b=b):
                        P.wait("tensor", sdd[b], ld)
                        if hm == 0:
                            P.wait("tensor", sv, hss_rdy)

                    def evdn(bank, hm=hm, tau=tau, xld=xld):
                        P.wait("vector", sdx[1], xld)
                        P.wait("vector", sdo[0])
                        P.vec(lambda v, bank=bank: v.tensor_tensor(
                            out=sc3[:], in0=bank[:], in1=sc4[:], op=ALU.add), inc=(sv, 1))
                        P.wait("sync", sv)
                        dma_s(chunk_ap(dtn["x1_scr"], hm, tau), sc3[:], sdo[0])
                    mm_group([cm(hss, kc) for kc in range(MC0)],
                             [wdl[b][:, kc * 128:(kc + 1) * 128] for kc in range(MC0)],
                             evdn, bank_pair=0, pre=prew)
                    wd_mark[b] = sp.n

        def phase_barrier(exclude=()):
            allsems = [sv, sa, sp, smisc, sdx[0], sdx[1], sdw[0], sdw[1],
                       gdw[0], gdw[1], sdo[0], sdo[1], sdd[0], sdd[1], gdd[0], gdd[1],
                       sca[0], sca[1], scp[0], scp[1], scp[2], scp[3],
                       sco[0], sco[1], sco[2], sco[3], ccs]
            for eng in ("vector", "scalar", "tensor", "sync", "gpsimd"):
                for sem in allsems:
                    if sem in exclude:
                        continue
                    P.wait(eng, sem)

        x1d = dtn["x1_scr"]

        xbL1 = [None]

        def build_xnt(xnt, tau):
            rsb = rs_b[:, tau * NT:(tau + 1) * NT]
            mrb = mr_b[:, tau * NT:(tau + 1) * NT]

            def consume(c, view, b, ld, rsb=rsb, mrb=mrb):
                P.wait("vector", sdx[b], ld)
                P.vec(lambda v, view=view, rsb=rsb: v.tensor_tensor(
                    out=sc1[:], in0=view, in1=rsb, op=ALU.mult))
                P.vec(lambda v, c=c, mrb=mrb: v.tensor_tensor(
                    out=xnt[:, c * NT:(c + 1) * NT], in0=sc1[:], in1=mrb,
                    op=ALU.subtract), inc=(sv, 1))
            stream4(x1d, tau, consume, xbL1[0],
                    extra=lambda: P.wait("sync", sdo[0]))

        def layer1_stats():
            for tau in range(NTAU):
                stats_pass.tau = tau
                stats_pass(x1d, do_sum=True, xb=xbL1[0])
                rowmath_musd(0, r2)
                bcast(r2[:], rs_b[:, tau * NT:(tau + 1) * NT])
                P.vec(lambda v: v.tensor_tensor(out=r3[:], in0=r1[:], in1=r2[:], op=ALU.mult))
                bcast(r3[:], mr_b[:, tau * NT:(tau + 1) * NT])
            for tau in range(NTAU):
                rsb = rs_b[:, tau * NT:(tau + 1) * NT]
                mrb = mr_b[:, tau * NT:(tau + 1) * NT]
                sq_ping = [sc1, sc2]
                sq_done_at = [0, 0]
                P.wait("tensor", sv, pr_free[0])

                def consume(c, view, b, ld, rsb=rsb, mrb=mrb,
                            sq_ping=sq_ping, sq_done_at=sq_done_at):
                    P.wait("vector", sdx[b], ld)
                    P.wait("vector", sa)
                    P.vec(lambda v, view=view, rsb=rsb: v.tensor_tensor(
                        out=sc5[:], in0=view, in1=rsb, op=ALU.mult))
                    P.vec(lambda v, mrb=mrb: v.tensor_tensor(
                        out=sc5[:], in0=sc5[:], in1=mrb, op=ALU.subtract), inc=(sv, 1))
                    P.wait("tensor", sv)
                    P.wait("scalar", sv)
                    _stats_chunk(c, sc5[:], sq_ping, sq_done_at, False)
                stream4(x1d, tau, consume, xbL1[0])
                rowmath_musd(1, r2, with_mu=False)
                bcast(r2[:], rrms_b[:, tau * NT:(tau + 1) * NT])

        def layer1_router(ctx):
            rwt = sb(ctx, "rwt", [128, KC * 8])
            rwb = sb(ctx, "rwb", [128, KC * 8], BF16)
            lgf = sb(ctx, "lgf", [8, T])
            xnt = sb(ctx, "xnt", [128, KC * NT], BF16)
            dma_s(rwt[:], dtn["rw_sl"][0], smisc)
            P.wait("vector", smisc)
            for kc in range(KC):
                P.vec(lambda v, kc=kc: v.tensor_scalar(
                    out=rwb[:, kc * 8:(kc + 1) * 8], in0=rwt[:, kc * 8:(kc + 1) * 8],
                    scalar1=pcol("rsc", kc), scalar2=None, op0=ALU.mult))
            rwb_rdy = sv.n
            for tau in range(NTAU):
                build_xnt(xnt, tau)
                P.wait("tensor", sv)
                for kc in range(KC):
                    P.pe(lambda t, kc=kc, tau=tau: t.matmul(
                        pbc[0:8, :], rwb[:, kc * 8:(kc + 1) * 8],
                        xnt[:, kc * NT:(kc + 1) * NT],
                        start=(kc == 0), stop=(kc == KC - 1)),
                        inc=(sp, 1) if kc == KC - 1 else None)
                P.wait("vector", sp)
                P.vec(lambda v, tau=tau: v.tensor_tensor(
                    out=lgf[0:8, tau * NT:(tau + 1) * NT], in0=pbc[0:8, :],
                    in1=rrms_b[0:8, tau * NT:(tau + 1) * NT], op=ALU.mult))
                P.vec(lambda v, tau=tau: v.tensor_scalar(
                    out=lgf[0:8, tau * NT:(tau + 1) * NT],
                    in0=lgf[0:8, tau * NT:(tau + 1) * NT],
                    scalar1=ROOT, scalar2=None, op0=ALU.mult), inc=(sv, 1))
            for tc in range(KC):
                P.wait("tensor", sv)
                P.pe(lambda t, tc=tc: t.transpose(
                    pt[:, 0:8], lgf[0:8, tc * 128:(tc + 1) * 128], ident_s[0:8, 0:8]),
                    inc=(sp, 1))
                P.wait("vector", sp)
                P.vec(lambda v, tc=tc: v.tensor_copy(out=rtm[:, tc * 8:(tc + 1) * 8],
                                                     in_=pt[:, 0:8]), inc=(sv, 1))
            for tc in range(KC):
                p8 = rtm[:, tc * 8:(tc + 1) * 8]
                q8 = raff[:, tc * 8:(tc + 1) * 8]
                s8 = rsc1[:, tc * 8:(tc + 1) * 8]
                P.vec(lambda v, p8=p8: v.reduce_max(out=rcol[:, 0:1], in_=p8, axis=AX.X))
                P.vec(lambda v, p8=p8, q8=q8: v.tensor_scalar(
                    out=q8, in0=p8, scalar1=rcol[:, 0:1], scalar2=None, op0=ALU.is_ge))
                P.vec(lambda v, p8=p8, q8=q8, s8=s8: v.scalar_tensor_tensor(
                    out=s8, in0=q8, scalar=-BIG, in1=p8, op0=ALU.mult, op1=ALU.add))
                P.vec(lambda v, s8=s8: v.reduce_max(out=rcol[:, 1:2], in_=s8, axis=AX.X))
                P.vec(lambda v, s8=s8: v.tensor_scalar(
                    out=s8, in0=s8, scalar1=rcol[:, 1:2], scalar2=None, op0=ALU.is_ge))
                P.vec(lambda v: v.tensor_tensor(out=rcol[:, 2:3], in0=rcol[:, 1:2],
                                                in1=rcol[:, 0:1], op=ALU.subtract))
                P.wait("scalar", sv)
                P.act(lambda s: s.activation(out=rcol[:, 3:4], in_=rcol[:, 2:3], func=AF.Exp))
                P.wait("vector", sa)
                P.vec(lambda v: v.tensor_scalar(out=rcol[:, 4:5], in0=rcol[:, 3:4],
                                                scalar1=1.0, scalar2=None, op0=ALU.add))
                P.vec(lambda v: v.reciprocal(out=rcol[:, 5:6], in_=rcol[:, 4:5]))
                P.vec(lambda v: v.tensor_scalar(out=rcol[:, 6:7], in0=rcol[:, 5:6],
                                                scalar1=-1.0, scalar2=1.0, op0=ALU.mult,
                                                op1=ALU.add))
                P.vec(lambda v, q8=q8: v.tensor_scalar(
                    out=q8, in0=q8, scalar1=rcol[:, 5:6], scalar2=None, op0=ALU.mult))
                P.vec(lambda v, s8=s8: v.tensor_scalar(
                    out=s8, in0=s8, scalar1=rcol[:, 6:7], scalar2=None, op0=ALU.mult))
                P.vec(lambda v, q8=q8, s8=s8: v.tensor_tensor(out=q8, in0=q8, in1=s8,
                                                              op=ALU.add))
                P.vec(lambda v, q8=q8: v.tensor_tensor(out=q8, in0=q8, in1=pes_b[:, 0:8],
                                                       op=ALU.mult), inc=(sv, 1))

        def layer1_attn(ctx):
            kbuf = sb(ctx, "kbuf", [128, KC * NT])
            vhb = sb(ctx, "vhb", [128, KC * NT], BF16)
            xnt = sb(ctx, "xnt_a", [128, KC * NT], BF16)
            carry = sb(ctx, "carry", [128, KC])
            wsl_mark = [0, 0]
            slab_ctr = [0]
            P.vec(lambda v: v.memset(carry[:], 0.0))

            def load_slab(src_ap):
                i = slab_ctr[0]; slab_ctr[0] += 1
                b = i % 2
                P.wait("scalar", sp, wsl_mark[b])
                return b, load_slab_b(wsl[b], src_ap, b)

            for tau in range(NTAU):
                build_xnt(xnt, tau)
                xn_rdy = sv.n
                for mc in range(KC):
                    b, ld = load_slab(dtn["wk_b"][mc])

                    def prk(ld=ld, b=b, xn_rdy=xn_rdy):
                        P.wait("tensor", sca[b], ld)
                        P.wait("tensor", sv, xn_rdy)

                    def evkk(bank, mc=mc):
                        P.vec(lambda v, bank=bank, mc=mc: v.tensor_copy(
                            out=cm(kbuf, mc), in_=bank[:]), inc=(sv, 1))
                    mm_group([xnt[:, kc * NT:(kc + 1) * NT] for kc in range(KC)],
                             [wsl[b][:, kc * 128:(kc + 1) * 128] for kc in range(KC)],
                             evkk, bank_pair=0, pre=prk)
                    wsl_mark[b] = sp.n
                for mc in range(KC):
                    b, ld = load_slab(dtn["wv_b"][mc])

                    def prv(ld=ld, b=b):
                        P.wait("tensor", sca[b], ld)

                    def evvv(bank, mc=mc):
                        P.vec(lambda v, bank=bank, mc=mc: v.tensor_tensor(
                            out=sc1[:], in0=bank[:], in1=cm(kbuf, mc), op=ALU.mult))
                        P.vec(lambda v, mc=mc: v.tensor_tensor_scan(
                            out=cm(kbuf, mc), data0=sc1[:], data1=sc1[:],
                            initial=carry[:, mc:mc + 1], op0=ALU.add, op1=ALU.bypass))
                        P.vec(lambda v, mc=mc: v.tensor_copy(
                            out=carry[:, mc:mc + 1], in_=cm(kbuf, mc)[:, NT - 1:NT]),
                            inc=(sv, 1))
                    mm_group([xnt[:, kc * NT:(kc + 1) * NT] for kc in range(KC)],
                             [wsl[b][:, kc * 128:(kc + 1) * 128] for kc in range(KC)],
                             evvv, bank_pair=0, pre=prv)
                    wsl_mark[b] = sp.n
                for mc in range(KC):
                    b, ld = load_slab(dtn["wq_b"][mc])

                    def prq(ld=ld, b=b):
                        P.wait("tensor", sca[b], ld)

                    def evqq(bank, mc=mc):
                        P.vec(lambda v, bank=bank, mc=mc: v.tensor_tensor(
                            out=sc1[:], in0=bank[:], in1=cm(kbuf, mc), op=ALU.mult))
                        P.vec(lambda v, mc=mc: v.tensor_scalar(
                            out=cm(vhb, mc), in0=sc1[:], scalar1=DENOM1, scalar2=None,
                            op0=ALU.mult), inc=(sv, 1))
                    mm_group([xnt[:, kc * NT:(kc + 1) * NT] for kc in range(KC)],
                             [wsl[b][:, kc * 128:(kc + 1) * 128] for kc in range(KC)],
                             evqq, bank_pair=0, pre=prq)
                    wsl_mark[b] = sp.n
                for mc in range(KC):
                    b, ld = load_slab(dtn["wo_b"][mc])
                    P.wait("sync", sv)
                    dma_s(sc4[:], chunk_ap(x1d, mc, tau), sdx[1])
                    xld = sdx[1].n

                    def pro(ld=ld, b=b):
                        P.wait("tensor", sca[b], ld)
                        P.wait("tensor", sv)

                    def evo(bank, mc=mc, tau=tau, xld=xld):
                        P.wait("vector", sdx[1], xld)
                        P.wait("vector", sdo[0])
                        P.vec(lambda v, bank=bank: v.tensor_tensor(
                            out=sc3[:], in0=bank[:], in1=sc4[:], op=ALU.add), inc=(sv, 1))
                        P.wait("sync", sv)
                        dma_s(chunk_ap(dtn["acc_scr"], mc, tau), sc3[:], sdo[0])
                    mm_group([cm(vhb, kc) for kc in range(KC)],
                             [wsl[b][:, kc * 128:(kc + 1) * 128] for kc in range(KC)],
                             evo, bank_pair=0, pre=pro)
                    wsl_mark[b] = sp.n

        def layer1_dense_gu(ctx, din):
            scb = sb(ctx, "scb", [128, NT], BF16)
            wga = [sb(ctx, f"wga{i}", [128, KC * 128], BF16) for i in range(2)]
            wgb = [sb(ctx, f"wgb{i}", [128, KC * 128], BF16) for i in range(2)]
            for tau in range(NTAU):
                rsb = rs_b[:, tau * NT:(tau + 1) * NT]
                mrb = mr_b[:, tau * NT:(tau + 1) * NT]
                rrb = rrms_b[:, tau * NT:(tau + 1) * NT]

                def consume(c, view, b, ld, tau=tau, rsb=rsb, mrb=mrb, rrb=rrb):
                    P.wait("vector", sdx[b], ld)
                    P.vec(lambda v, view=view, rsb=rsb: v.tensor_tensor(
                        out=sc1[:], in0=view, in1=rsb, op=ALU.mult))
                    P.vec(lambda v, mrb=mrb: v.tensor_tensor(
                        out=sc1[:], in0=sc1[:], in1=mrb, op=ALU.subtract))
                    P.vec(lambda v, rrb=rrb: v.tensor_tensor(
                        out=sc1[:], in0=sc1[:], in1=rrb, op=ALU.mult))
                    P.vec(lambda v, c=c, tau=tau: v.tensor_scalar(
                        out=cm(din, c, tau), in0=sc1[:], scalar1=pcol("pre1", c),
                        scalar2=None, op0=ALU.mult), inc=(sv, 1))
                stream4(x1d, tau, consume, xbL1[0],
                        extra=lambda: P.wait("sync", sdo[0]))
            din_rdy = sv.n
            gu_mark = [sp.n, sp.n]
            for mc in range(64):
                bg = mc % 2
                P.wait("scalar", sp, gu_mark[bg])
                load_slab_b(wga[bg], dtn["wg1_b"][mc], bg)
                ld = load_slab_b(wgb[bg], dtn["wu1_b"][mc], bg)
                for tau in range(NTAU):
                    def prg(ld=ld, bg=bg, mc=mc, tau=tau):
                        P.wait("tensor", sca[bg], ld)
                        if mc == 0 and tau == 0:
                            P.wait("tensor", sv, din_rdy)

                    def evg1(bank):
                        P.wait("scalar", sp)
                        P.wait("scalar", sv)
                        P.act(lambda s, bank=bank: s.activation(
                            out=sc2[:], in_=bank[:], func=AF.Gelu_apprx_tanh))

                    def evu1(bank, mc=mc, tau=tau):
                        P.wait("vector", sa)
                        P.wait("vector", sdo[1])
                        P.vec(lambda v, bank=bank: v.tensor_tensor(
                            out=scb[:], in0=sc2[:], in1=bank[:], op=ALU.mult), inc=(sv, 1))
                        P.wait("sync", sv)
                        dma_s(chunk_ap(dtn["h1_scr"], mc, tau), scb[:], sdo[1])
                    mm_group([cm(din, kc, tau) for kc in range(KC)],
                             [wga[bg][:, kc * 128:(kc + 1) * 128] for kc in range(KC)],
                             evg1, bank_pair=0, pre=prg)
                    mm_group([cm(din, kc, tau) for kc in range(KC)],
                             [wgb[bg][:, kc * 128:(kc + 1) * 128] for kc in range(KC)],
                             evu1, bank_pair=1)
                gu_mark[bg] = sp.n

        def layer1_dense_down(ctx, rdacc):
            h1t = sb(ctx, "h1t", [128, 64 * NT], BF16)
            wdl = [sb(ctx, f"wdl1_{i}", [128, 64 * 128], BF16) for i in range(2)]
            wd_mark = [0, 0]
            h1_mark = [0]
            P.vec(lambda v: v.memset(rdacc[:], 0.0), inc=(sv, 1))
            for tau in range(NTAU):
                P.wait("sync", sp, h1_mark[0])
                P.wait("sync", sdo[1])
                dma_s(h1t[:].rearrange("p (c j) -> p c j", j=NT),
                      bass.AP(dtn["h1_scr"].tensor, tau * NT,
                              [[T, 128], [128 * T, 64], [1, NT]]), sdx[0])
                h1ld = sdx[0].n
                for hm in range(KC):
                    b = hm % 2
                    P.wait("scalar", sp, wd_mark[b])
                    ld = load_slab_b(wdl[b], dtn["wd1_b"][hm], b)

                    def prd(ld=ld, b=b, h1ld=h1ld):
                        P.wait("tensor", sca[b], ld)
                        P.wait("tensor", sdx[0], h1ld)

                    def evd1(bank, hm=hm, tau=tau):
                        P.wait("vector", sdo[0])
                        P.vec(lambda v, bank=bank: v.tensor_copy(out=sc3[:], in_=bank[:]),
                              inc=(sv, 1))
                        P.wait("sync", sv)
                        dma_s(chunk_ap(dtn["dno_scr"], hm, tau), sc3[:], sdo[0])
                        P.wait("scalar", sv)
                        P.act(lambda s: s.activation(out=sc2[:], in_=sc3[:], func=AF.Square))
                        P.wait("tensor", sa)
                        P.wait("tensor", sv)
                        P.pe(lambda t: t.matmul(pr1[:], onesk_s[:], sc2[:],
                                                start=True, stop=True), inc=(sp, 1))
                        P.wait("vector", sp)
                        P.vec(lambda v, tau=tau: v.tensor_tensor(
                            out=rdacc[0:1, tau * NT:(tau + 1) * NT],
                            in0=rdacc[0:1, tau * NT:(tau + 1) * NT],
                            in1=pr1[:], op=ALU.add), inc=(sv, 1))
                    mm_group([h1t[:, kc * NT:(kc + 1) * NT] for kc in range(64)],
                             [wdl[b][:, kc * 128:(kc + 1) * 128] for kc in range(64)],
                             evd1, bank_pair=0, pre=prd)
                    wd_mark[b] = sp.n
                h1_mark[0] = sp.n

        def build_affb(e, affb):
            for tau in range(NTAU):
                for j in range(4):
                    tc = tau * 4 + j
                    P.wait("tensor", sv)
                    P.pe(lambda t, tc=tc, j=j, e=e: t.transpose(
                        pt[0:1, j * 128:(j + 1) * 128],
                        raff[:, tc * 8 + e:tc * 8 + e + 1], ident_s[:]),
                        inc=(sp, 1))
                P.wait("vector", sp)
                P.vec(lambda v: v.tensor_copy(out=r3[:], in_=pt[0:1, 0:NT]), inc=(sv, 1))
                bcast(r3[:], affb[:, tau * NT:(tau + 1) * NT])

        def layer1_moe_he(ctx, moein):
            scb = sb(ctx, "scb2", [128, NT], BF16)
            affb = sb(ctx, "affb", [128, NTAU * NT])
            wga = [sb(ctx, f"wea{i}", [128, KC * 128], BF16) for i in range(2)]
            wgb = [sb(ctx, f"web{i}", [128, KC * 128], BF16) for i in range(2)]
            for tau in range(NTAU):
                rsb = rs_b[:, tau * NT:(tau + 1) * NT]
                mrb = mr_b[:, tau * NT:(tau + 1) * NT]
                rrb = rrms_b[:, tau * NT:(tau + 1) * NT]

                def consume(c, view, b, ld, tau=tau, rsb=rsb, mrb=mrb, rrb=rrb):
                    P.wait("vector", sdx[b], ld)
                    P.vec(lambda v, view=view, rsb=rsb: v.tensor_tensor(
                        out=sc1[:], in0=view, in1=rsb, op=ALU.mult))
                    P.vec(lambda v, mrb=mrb: v.tensor_tensor(
                        out=sc1[:], in0=sc1[:], in1=mrb, op=ALU.subtract))
                    P.vec(lambda v, rrb=rrb: v.tensor_tensor(
                        out=sc1[:], in0=sc1[:], in1=rrb, op=ALU.mult))
                    P.vec(lambda v, c=c, tau=tau: v.tensor_scalar(
                        out=cm(moein, c, tau), in0=sc1[:], scalar1=pcol("pre2", c),
                        scalar2=None, op0=ALU.mult), inc=(sv, 1))
                stream4(x1d, tau, consume, xbL1[0],
                        extra=lambda: P.wait("sync", sdo[0]))
            mo_rdy = sv.n
            gu_mark = [sp.n, sp.n]
            for e in range(E):
                build_affb(e, affb)
                for mc in range(8):
                    bg = mc % 2
                    P.wait("scalar", sp, gu_mark[bg])
                    load_slab_b(wga[bg], dtn["weg_b"][e, mc], bg)
                    ld = load_slab_b(wgb[bg], dtn["weu_b"][e, mc], bg)
                    for tau in range(NTAU):
                        def prg(ld=ld, bg=bg, e=e, mc=mc, tau=tau):
                            P.wait("tensor", sca[bg], ld)
                            if e == 0 and mc == 0 and tau == 0:
                                P.wait("tensor", sv, mo_rdy)

                        def evg2(bank):
                            P.wait("scalar", sp)
                            P.wait("scalar", sv)
                            P.act(lambda s, bank=bank: s.activation(
                                out=sc2[:], in_=bank[:], func=AF.Gelu_apprx_tanh))

                        def evu2(bank, e=e, mc=mc, tau=tau):
                            P.wait("vector", sa)
                            P.wait("vector", sdo[1])
                            P.vec(lambda v, bank=bank: v.tensor_tensor(
                                out=sc1[:], in0=sc2[:], in1=bank[:], op=ALU.mult))
                            P.vec(lambda v, tau=tau: v.tensor_tensor(
                                out=scb[:], in0=sc1[:],
                                in1=affb[:, tau * NT:(tau + 1) * NT], op=ALU.mult),
                                inc=(sv, 1))
                            P.wait("sync", sv)
                            dma_s(chunk_ap(dtn["hew_scr"], e * 8 + mc, tau), scb[:], sdo[1])
                        mm_group([cm(moein, kc, tau) for kc in range(KC)],
                                 [wga[bg][:, kc * 128:(kc + 1) * 128] for kc in range(KC)],
                                 evg2, bank_pair=0, pre=prg)
                        mm_group([cm(moein, kc, tau) for kc in range(KC)],
                                 [wgb[bg][:, kc * 128:(kc + 1) * 128] for kc in range(KC)],
                                 evu2, bank_pair=1)
                    gu_mark[bg] = sp.n

        def layer1_moe_down(ctx, rdacc, rmacc, rsacc):
            hewt = sb(ctx, "hewt", [128, 64 * NT], BF16)
            wdl = [sb(ctx, f"wdl2_{i}", [128, 64 * 128], BF16) for i in range(2)]
            wd_mark = [0, 0]
            hw_mark = [0]
            P.vec(lambda v: v.memset(rmacc[:], 0.0))
            P.vec(lambda v: v.memset(rsacc[:], 0.0), inc=(sv, 1))
            for tau in range(NTAU):
                P.wait("sync", sp, hw_mark[0])
                P.wait("sync", sdo[1])
                dma_s(hewt[:].rearrange("p (c j) -> p c j", j=NT),
                      bass.AP(dtn["hew_scr"].tensor, tau * NT,
                              [[T, 128], [128 * T, 64], [1, NT]]), sdx[0])
                hld = sdx[0].n
                for hm in range(KC):
                    b = hm % 2
                    P.wait("scalar", sp, wd_mark[b])
                    ld = load_slab_b(wdl[b], dtn["wed_b"][hm], b)

                    def prd(ld=ld, b=b, hld=hld):
                        P.wait("tensor", sca[b], ld)
                        P.wait("tensor", sdx[0], hld)

                    def evd2(bank, hm=hm, tau=tau):
                        P.wait("vector", sdo[0])
                        P.vec(lambda v, bank=bank: v.tensor_copy(out=sc3[:], in_=bank[:]),
                              inc=(sv, 1))
                        P.wait("sync", sv)
                        dma_s(chunk_ap(dtn["mo_scr"], hm, tau), sc3[:], sdo[0])
                        P.wait("scalar", sv)
                        P.act(lambda s: s.activation(out=sc2[:], in_=sc3[:], func=AF.Square))
                        P.wait("tensor", sa)
                        P.wait("tensor", sv)
                        P.pe(lambda t: t.matmul(pr1[:], onesk_s[:], sc2[:],
                                                start=True, stop=True), inc=(sp, 1))
                        P.wait("vector", sp)
                        P.vec(lambda v, tau=tau: v.tensor_tensor(
                            out=rmacc[0:1, tau * NT:(tau + 1) * NT],
                            in0=rmacc[0:1, tau * NT:(tau + 1) * NT],
                            in1=pr1[:], op=ALU.add), inc=(sv, 1))
                    mm_group([hewt[:, kc * NT:(kc + 1) * NT] for kc in range(64)],
                             [wdl[b][:, kc * 128:(kc + 1) * 128] for kc in range(64)],
                             evd2, bank_pair=0, pre=prd)
                    wd_mark[b] = sp.n
                hw_mark[0] = sp.n
                if tau > 0:
                    layer1_final_tau(rdacc, rmacc, rsacc, tau - 1)
            layer1_final_tau(rdacc, rmacc, rsacc, NTAU - 1)

        def rms_row_bcast_def():
            def rms_row_bcast(accrow, tau, dst):
                P.vec(lambda v, accrow=accrow, tau=tau: v.tensor_scalar(
                    out=r2[:], in0=accrow[0:1, tau * NT:(tau + 1) * NT],
                    scalar1=1.0 / D, scalar2=None, op0=ALU.mult), inc=(sv, 1))
                P.wait("scalar", sv)
                P.act(lambda s: s.activation(out=r3[:], in_=r2[:], func=AF.Sqrt,
                                             bias=eps_s[0:1, 1:2]), inc=(sa, 1))
                P.wait("vector", sa)
                P.vec(lambda v: v.reciprocal(out=r2[:], in_=r3[:]), inc=(sv, 1))
                bcast(r2[:], dst)
            return rms_row_bcast

        def layer1_final_tau(rdacc, rmacc, rsacc, tau):
            rms_row_bcast = rms_row_bcast_def()
            xba, xbb = xbL1[0]
            if True:
                rms_row_bcast(rdacc, tau, bcA[:])
                rms_row_bcast(rmacc, tau, bcB[:])
                for q in range(4):
                    P.wait("sync", sv)
                    P.wait("sync", sa)
                    P.wait("sync", sp)
                    P.wait("sync", sdo[0])
                    dma_s(xba[:].rearrange("p (c j) -> p c j", j=NT),
                          chunk4_ap(dtn["dno_scr"], q, tau), sdx[0])
                    lda = sdx[0].n
                    dma_s(xbb[:].rearrange("p (c j) -> p c j", j=NT),
                          chunk4_ap(dtn["mo_scr"], q, tau), sdx[1])
                    ldb = sdx[1].n
                    for j in range(4):
                        c = 4 * q + j
                        va = xba[:, j * NT:(j + 1) * NT]
                        vb = xbb[:, j * NT:(j + 1) * NT]
                        P.wait("vector", sdx[0], lda)
                        P.wait("vector", sdx[1], ldb)
                        P.wait("vector", sa)
                        P.vec(lambda v, va=va: v.tensor_tensor(out=sc1[:], in0=va, in1=bcA[:],
                                                               op=ALU.mult))
                        P.vec(lambda v, c=c: v.tensor_scalar(
                            out=sc1[:], in0=sc1[:], scalar1=pcol("post1", c), scalar2=None,
                            op0=ALU.mult))
                        P.vec(lambda v, vb=vb: v.tensor_tensor(out=sc2[:], in0=vb, in1=bcB[:],
                                                               op=ALU.mult))
                        P.vec(lambda v, c=c: v.tensor_scalar(
                            out=sc2[:], in0=sc2[:], scalar1=pcol("post2", c), scalar2=None,
                            op0=ALU.mult))
                        P.wait("vector", sdo[0])
                        P.vec(lambda v: v.tensor_tensor(out=sc5[:], in0=sc1[:], in1=sc2[:],
                                                        op=ALU.add), inc=(sv, 1))
                        P.wait("sync", sv)
                        dma_s(chunk_ap(dtn["s_scr"], c, tau), sc5[:], sdo[0])
                        P.wait("scalar", sv)
                        P.act(lambda s: s.activation(out=sc2[:], in_=sc5[:], func=AF.Square))
                        P.wait("tensor", sa)
                        P.wait("tensor", sv)
                        P.pe(lambda t: t.matmul(pr1[:], onesk_s[:], sc2[:],
                                                start=True, stop=True), inc=(sp, 1))
                        P.wait("vector", sp)
                        P.vec(lambda v, tau=tau: v.tensor_tensor(
                            out=rsacc[0:1, tau * NT:(tau + 1) * NT],
                            in0=rsacc[0:1, tau * NT:(tau + 1) * NT],
                            in1=pr1[:], op=ALU.add), inc=(sv, 1))
            if True:
                rms_row_bcast(rsacc, tau, bcA[:])
                for q in range(4):
                    P.wait("sync", sv)
                    P.wait("sync", sp)
                    P.wait("sync", sdo[0])
                    dma_s(xba[:].rearrange("p (c j) -> p c j", j=NT),
                          chunk4_ap(dtn["s_scr"], q, tau), sdx[0])
                    lda = sdx[0].n
                    dma_s(xbb[:].rearrange("p (c j) -> p c j", j=NT),
                          chunk4_ap(dtn["acc_scr"], q, tau), sdx[1])
                    ldb = sdx[1].n
                    for j in range(4):
                        c = 4 * q + j
                        va = xba[:, j * NT:(j + 1) * NT]
                        vb = xbb[:, j * NT:(j + 1) * NT]
                        P.wait("vector", sdx[0], lda)
                        P.wait("vector", sdx[1], ldb)
                        P.vec(lambda v, va=va: v.tensor_tensor(out=sc1[:], in0=va, in1=bcA[:],
                                                               op=ALU.mult))
                        P.vec(lambda v, c=c: v.tensor_scalar(
                            out=sc1[:], in0=sc1[:], scalar1=pcol("post", c), scalar2=None,
                            op0=ALU.mult))
                        P.wait("vector", sdo[0])
                        P.vec(lambda v, vb=vb: v.tensor_tensor(out=sc5[:], in0=sc1[:], in1=vb,
                                                               op=ALU.add), inc=(sv, 1))
                        P.wait("sync", sv)
                        dma_s(chunk_ap(dtn["out"], c, tau), sc5[:], sdo[0])

        load_params()
        preconvert()
        if 0 in layers:
            with ExitStack() as c0:
                big_a = sb(c0, "big_a", [128, KC * T], BF16)
                with ExitStack() as mctx:
                    ghost_mat(mctx)
                    with ExitStack() as c1:
                        layer0_stats_h(c1, big_a)
                    for _s2 in (sv, sdw[0], sdw[1], sdo[0], ccs):
                        for _e in ("vector", "sync", "scalar", "tensor"):
                            P.wait(_e, _s2)
                with ExitStack() as c2:
                    big_b = sb(c2, "big_b", [128, KC * T], BF16)
                    with ExitStack() as c3:
                        layer0_attn(c3, big_a, big_b)
            with ExitStack() as c4:
                layer0_ffn(c4)
        elif 1 in layers:
            phase_barrier(exclude=(gdw[0],))
        if 1 in layers:
            phase_barrier()
            with ExitStack() as l1a:
                rdacc = sb(l1a, "rdacc", [1, T])
                rmacc = sb(l1a, "rmacc", [1, T])
                xbL1[0] = [sb(l1a, "xb1a", [128, 4 * NT]), sb(l1a, "xb1b", [128, 4 * NT])]
                layer1_stats()
                with ExitStack() as l1b:
                    layer1_router(l1b)
                phase_barrier()
                with ExitStack() as l1c:
                    layer1_attn(l1c)
                phase_barrier()
                with ExitStack() as l1d:
                    din = sb(l1d, "din", [128, KC * T], BF16)
                    layer1_dense_gu(l1d, din)
                phase_barrier()
                with ExitStack() as l1e:
                    layer1_dense_down(l1e, rdacc)
                phase_barrier()
                with ExitStack() as l1f:
                    moein = sb(l1f, "moein", [128, KC * T], BF16)
                    layer1_moe_he(l1f, moein)
                phase_barrier()
                with ExitStack() as l1g:
                    rsacc = sb(l1g, "rsacc", [1, T])
                    layer1_moe_down(l1g, rdacc, rmacc, rsacc)

        for _sem in [smisc, sdx[0], sdx[1], sdw[0], sdw[1], sdo[0], sdo[1], sdd[0], sdd[1]]:
            P.wait("sync", _sem)
        for _sem in [sca[0], sca[1], scp[0], scp[1], scp[2], scp[3],
                     sco[0], sco[1], sco[2], sco[3]]:
            P.wait("scalar", _sem)
        for _sem in [gdw[0], gdw[1], gdd[0], gdd[1], ccs]:
            P.wait("gpsimd", _sem)

        block = octx.enter_context(nc.Block())

        @block.sync
        def _(eng):
            for f in P.ops["sync"]:
                f(eng)

        @block.gpsimd
        def _(eng):
            for f in P.ops["gpsimd"]:
                f(eng)

        @block.vector
        def _(eng):
            for f in P.ops["vector"]:
                f(eng)

        @block.scalar
        def _(eng):
            for f in P.ops["scalar"]:
                f(eng)

        @block.tensor
        def _(eng):
            for f in P.ops["tensor"]:
                f(eng)

    return nc



def prep_inputs_l0(inputs):
    perm = np.concatenate([np.arange(0, 1024, 2), np.arange(1, 1024, 2),
                           np.arange(1024, 2048)])
    qkvperm = np.concatenate([perm, 2048 + perm, 4096 + perm])
    W_in = np.asarray(inputs["L0_attn_W_in"], dtype=np.float32)
    b_in = np.asarray(inputs["L0_attn_b_in"], dtype=np.float32)
    W_out = np.asarray(inputs["L0_attn_W_out"], dtype=np.float32)
    d = {}
    d["win_sl"] = to_slabs(np.ascontiguousarray(W_in[:, qkvperm]), KC, 48)
    d["bin_pp"] = col_pp(b_in[qkvperm], 48)
    d["lnw_pp"] = col_pp(inputs["L0_attn_ln_w"], KC)
    d["lnb_pp"] = col_pp(inputs["L0_attn_ln_b"], KC)
    d["wout_sl"] = to_slabs(np.ascontiguousarray(W_out[perm, :]), KC, KC)
    d["bout_pp"] = col_pp(inputs["L0_attn_b_out"], KC)
    d["lut_row"] = np.asarray(inputs["L0_ffn_lut"], np.float32).reshape(1, 16)
    gT = np.ascontiguousarray(np.asarray(inputs["L0_ffn_gate_idx"], np.int32).T)
    uT = np.ascontiguousarray(np.asarray(inputs["L0_ffn_up_idx"], np.int32).T)
    dT = np.ascontiguousarray(np.asarray(inputs["L0_ffn_down_idx"], np.int32).T)
    gu = np.concatenate([to_slabs(gT, KC, MC0), to_slabs(uT, KC, MC0)], axis=0)
    d["_gu88"] = np.concatenate([gu, gu[0:2]], axis=0)
    d["_d16"] = to_slabs(dT, MC0, KC)
    i = np.arange(0, 1024, 2, dtype=np.float64) / 1024.0
    inv_freq = 1.0 / (10000.0 ** i)
    s = np.arange(T, dtype=np.float64)
    ang = s[None, :] * inv_freq[:, None]
    cosT = np.cos(ang).astype(np.float32).reshape(4, 128, T)
    sinT = np.sin(ang).astype(np.float32).reshape(4, 128, T)
    cs = np.concatenate([cosT, sinT], axis=0)
    d["cs_pp"] = np.ascontiguousarray(cs.transpose(1, 0, 2))
    return d


def prep_inputs_l1(inputs):
    d = {}
    for nm, key in [("wq", "L1_attn_Wq"), ("wk", "L1_attn_Wk"), ("wv", "L1_attn_Wv"),
                    ("wo", "L1_attn_Wo")]:
        W = np.asarray(inputs[key], np.float32)
        d[nm + "_sl"] = to_slabs(np.ascontiguousarray(W.T), KC, KC)
    for nm, key in [("pre1", "L1_ffn_rn_pre1"), ("pre2", "L1_ffn_rn_pre2"),
                    ("post1", "L1_ffn_rn_post1"), ("post2", "L1_ffn_rn_post2"),
                    ("post", "L1_ffn_rn_post"), ("rsc", "L1_ffn_router_scale")]:
        d[nm + "_pp"] = col_pp(inputs[key], KC)
    rw = np.asarray(inputs["L1_ffn_router_w"], np.float32)
    d["rw_sl"] = np.ascontiguousarray(
        rw.reshape(KC, 128, 8).transpose(1, 0, 2).reshape(1, 128, KC * 8))
    d["pes_row"] = np.asarray(inputs["L1_ffn_per_expert_scale"], np.float32).reshape(1, E)
    d["wg1_sl"] = to_slabs(np.asarray(inputs["L1_ffn_w_gate"], np.float32), KC, 64)
    d["wu1_sl"] = to_slabs(np.asarray(inputs["L1_ffn_w_up"], np.float32), KC, 64)
    d["wd1_sl"] = to_slabs(np.asarray(inputs["L1_ffn_w_down"], np.float32), 64, KC)
    weg = np.asarray(inputs["L1_ffn_we_gate"], np.float32)
    weu = np.asarray(inputs["L1_ffn_we_up"], np.float32)
    wed = np.asarray(inputs["L1_ffn_we_down"], np.float32)
    d["weg_sl"] = np.stack([to_slabs(weg[e], KC, 8) for e in range(E)])
    d["weu_sl"] = np.stack([to_slabs(weu[e], KC, 8) for e in range(E)])
    wed_cat = np.concatenate([wed[e] for e in range(E)], axis=0)
    d["wed_sl"] = to_slabs(np.ascontiguousarray(wed_cat), 64, KC)
    return d


def kernel(**inputs):
    X = np.asarray(inputs["L0_attn_X"], np.float32)
    common = prep_inputs_l0(inputs)
    common.update(prep_inputs_l1(inputs))
    nc = build_program(layers=(0, 1))
    gu88 = common.pop("_gu88")
    d16 = common.pop("_d16")
    in_maps = []
    for b in range(8):
        m = dict(common)
        m["x0"] = np.ascontiguousarray(X[b].T)
        m["gu_my"] = np.ascontiguousarray(gu88[11 * b:11 * b + 11])
        m["d_my"] = np.ascontiguousarray(d16[2 * b:2 * b + 2])
        in_maps.append(m)
    res = run_bass_kernel_spmd(nc, in_maps, core_ids=list(range(8)))
    out = np.stack([np.ascontiguousarray(r["out"].T) for r in res.results])
    return out


# version 13 (final)
# speedup vs baseline: 1.0101x; 1.0044x over previous
import numpy as np
import ml_dtypes
from contextlib import ExitStack
import concourse.bass as bass
import concourse.mybir as mybir
from concourse.bass_utils import run_bass_kernel_spmd

F32 = mybir.dt.float32
BF16 = mybir.dt.bfloat16
I32 = mybir.dt.int32
AF = mybir.ActivationFunctionType
ALU = mybir.AluOpType
AX = mybir.AxisListType

D = 2048
T = 2048
NT = 512
NTAU = T // NT
KC = D // 128
I0 = 5504
MC0 = I0 // 128
E = 8
LN_EPS = 1e-5
RMS_EPS = 1e-6
DENOM1 = float(np.sqrt(np.float32(2048.0)) / np.float32(2048.0))
ROOT = float(np.float32(2048.0) ** np.float32(-0.5))
BIG = 1e30


def to_slabs(w, nk, nm):
    K, M = w.shape
    assert K == nk * 128 and M == nm * 128
    return np.ascontiguousarray(
        w.reshape(nk, 128, nm, 128).transpose(2, 1, 0, 3).reshape(nm, 128, nk * 128))


def col_pp(v, n):
    return np.ascontiguousarray(np.asarray(v, np.float32).reshape(n, 128).T)


class Sem:
    def __init__(self, handle):
        self.h = handle
        self.n = 0


class Prog:
    def __init__(self):
        self.ops = {k: [] for k in ("sync", "scalar", "vector", "tensor", "gpsimd")}
        self.auto = {}

    def _add(self, eng, fn, inc=None):
        asem = self.auto.get(eng)
        if asem is not None:
            if asem.n > 0:
                self.ops[eng].append(lambda e, s=asem.h, v=asem.n: e.wait_ge(s, v))
            if inc is None:
                inc = (asem, 1)
        if inc is None:
            self.ops[eng].append(fn)
        else:
            sem, k = inc
            sem.n += k
            self.ops[eng].append(lambda e, fn=fn, s=sem.h, k=k: fn(e).then_inc(s, k))

    def vec(self, fn, inc=None):
        self._add("vector", fn, inc)

    def act(self, fn, inc=None):
        self._add("scalar", fn, inc)

    def pe(self, fn, inc=None):
        self._add("tensor", fn, inc)

    def dma(self, q, sem, fn):
        self._add(q, fn, (sem, 16))

    def wait(self, eng, sem, val=None):
        v = sem.n if val is None else val
        if v > 0:
            self.ops[eng].append(lambda e, s=sem.h, v=v: e.wait_ge(s, v))


def build_program(layers=(0, 1), x1_out=False):
    nc = bass.Bass("TRN2", target_bir_lowering=False)
    dtn = {}

    def dram(name, shape, dtype=F32, kind="ExternalInput"):
        dtn[name] = nc.dram_tensor(name, list(shape), dtype, kind=kind).ap()
        return dtn[name]

    x0 = dram("x0", [D, T])
    if 0 in layers:
        dram("win_sl", [48, 128, KC * 128])
        dram("bin_pp", [128, 48])
        dram("lnw_pp", [128, KC]); dram("lnb_pp", [128, KC])
        dram("wout_sl", [KC, 128, KC * 128])
        dram("bout_pp", [128, KC])
        dram("lut_row", [1, 16])
        dram("gu_my", [11, 128, KC * 128], I32)
        dram("d_my", [2, 128, MC0 * 128], I32)
        dram("cs_pp", [128, 8, T])
        dram("win_b", [48, 128, KC * 128], BF16, kind="Internal")
        dram("wout_b", [KC, 128, KC * 128], BF16, kind="Internal")
        dram("gu_loc", [11 * 128, KC * 128], BF16, kind="Internal")
        dram("d_loc", [2 * 128, MC0 * 128], BF16, kind="Internal")
        dram("gu_all", [88 * 128, KC * 128], BF16, kind="Internal")
        dram("d_all", [16 * 128, MC0 * 128], BF16, kind="Internal")
    if 1 in layers:
        dram("wq_sl", [KC, 128, KC * 128]); dram("wk_sl", [KC, 128, KC * 128])
        dram("wv_sl", [KC, 128, KC * 128]); dram("wo_sl", [KC, 128, KC * 128])
        dram("pre1_pp", [128, KC]); dram("pre2_pp", [128, KC])
        dram("post1_pp", [128, KC]); dram("post2_pp", [128, KC]); dram("post_pp", [128, KC])
        dram("rsc_pp", [128, KC])
        dram("rw_sl", [1, 128, KC * 8])
        dram("pes_row", [1, E])
        dram("wg1_sl", [64, 128, KC * 128]); dram("wu1_sl", [64, 128, KC * 128])
        dram("wd1_sl", [KC, 128, 64 * 128])
        dram("weg_sl", [E, 8, 128, KC * 128]); dram("weu_sl", [E, 8, 128, KC * 128])
        dram("wed_sl", [KC, 128, 64 * 128])
        dram("wq_b", [KC, 128, KC * 128], BF16, kind="Internal")
        dram("wk_b", [KC, 128, KC * 128], BF16, kind="Internal")
        dram("wv_b", [KC, 128, KC * 128], BF16, kind="Internal")
        dram("wo_b", [KC, 128, KC * 128], BF16, kind="Internal")
        dram("wg1_b", [64, 128, KC * 128], BF16, kind="Internal")
        dram("wu1_b", [64, 128, KC * 128], BF16, kind="Internal")
        dram("wd1_b", [KC, 128, 64 * 128], BF16, kind="Internal")
        dram("weg_b", [E, 8, 128, KC * 128], BF16, kind="Internal")
        dram("weu_b", [E, 8, 128, KC * 128], BF16, kind="Internal")
        dram("wed_b", [KC, 128, 64 * 128], BF16, kind="Internal")
        dram("h1_scr", [64 * 128, T], BF16, kind="Internal")
        dram("hew_scr", [64 * 128, T], BF16, kind="Internal")
        dram("dno_scr", [D, T], kind="Internal")
        dram("mo_scr", [D, T], kind="Internal")
        dram("s_scr", [D, T], kind="Internal")
        dram("acc_scr", [D, T], kind="Internal")
    x1_kind = "ExternalOutput" if x1_out else "Internal"
    dram("x1_scr", [D, T], kind=x1_kind)
    if 1 in layers:
        dram("out", [D, T], kind="ExternalOutput")

    ones_k_c = nc.inline_tensor(np.ones((128, 1), dtype=np.float32), "ones_k").ap()
    ones_r_c = nc.inline_tensor(np.ones((1, 128), dtype=np.float32), "ones_r").ap()
    ident_c = nc.inline_tensor(np.eye(128, dtype=np.float32), "ident_c").ap()
    eps_c = nc.inline_tensor(np.array([[LN_EPS, RMS_EPS, 0.0, 0.0]], dtype=np.float32),
                             "eps_c").ap()

    P = Prog()
    with ExitStack() as octx:
        def sb(ctx, name, shape, dtp=F32):
            return ctx.enter_context(nc.sbuf_tensor(name, list(shape), dtp))

        smisc = Sem(octx.enter_context(nc.semaphore("smisc")))
        sdx = [Sem(octx.enter_context(nc.semaphore(f"sdx{i}"))) for i in range(2)]
        sdw = [Sem(octx.enter_context(nc.semaphore(f"sdw{i}"))) for i in range(2)]
        gdw = [Sem(octx.enter_context(nc.semaphore(f"gdw{i}"))) for i in range(2)]
        sdo = [Sem(octx.enter_context(nc.semaphore(f"sdo{i}"))) for i in range(2)]
        sdd = [Sem(octx.enter_context(nc.semaphore(f"sdd{i}"))) for i in range(2)]
        gdd = [Sem(octx.enter_context(nc.semaphore(f"gdd{i}"))) for i in range(2)]
        sca = [Sem(octx.enter_context(nc.semaphore(f"sca{i}"))) for i in range(2)]
        ccs = Sem(octx.enter_context(nc.semaphore("ccs")))
        scp = [Sem(octx.enter_context(nc.semaphore(f"scp{i}"))) for i in range(4)]
        sco = [Sem(octx.enter_context(nc.semaphore(f"sco{i}"))) for i in range(4)]
        sv = Sem(octx.enter_context(nc.semaphore("sv")))
        sa = Sem(octx.enter_context(nc.semaphore("sa")))
        sp = Sem(octx.enter_context(nc.semaphore("sp")))
        P.auto = {"vector": sv, "scalar": sa}

        pa = octx.enter_context(nc.psum_tensor("pa", [128, NT], F32))
        pb = octx.enter_context(nc.psum_tensor("pb", [128, NT], F32))
        pc_ = octx.enter_context(nc.psum_tensor("pc", [128, NT], F32))
        pd_ = octx.enter_context(nc.psum_tensor("pd", [128, NT], F32))
        pr1 = octx.enter_context(nc.psum_tensor("pr1", [1, NT], F32))
        pr2 = octx.enter_context(nc.psum_tensor("pr2", [1, NT], F32))
        pbc = octx.enter_context(nc.psum_tensor("pbc", [128, NT], F32))
        pt = octx.enter_context(nc.psum_tensor("pt", [128, NT], F32))

        ppar = sb(octx, "ppar", [128, 224])
        lutc = sb(octx, "lutc", [128, 16])
        acol = sb(octx, "acol", [128, KC])
        pes_b = sb(octx, "pes_b", [128, E])
        onesk_s = sb(octx, "onesk_s", [128, 1])
        onesr_s = sb(octx, "onesr_s", [1, 128])
        ident_s = sb(octx, "ident_s", [128, 128])
        sc1 = sb(octx, "sc1", [128, NT]); sc2 = sb(octx, "sc2", [128, NT])
        sc3 = sb(octx, "sc3", [128, NT]); sc4 = sb(octx, "sc4", [128, NT])
        sc5 = sb(octx, "sc5", [128, NT])
        r1 = sb(octx, "r1", [1, NT]); r2 = sb(octx, "r2", [1, NT]); r3 = sb(octx, "r3", [1, NT])
        eps_s = sb(octx, "eps_s", [1, 4])
        bcA = sb(octx, "bcA", [128, NT]); bcB = sb(octx, "bcB", [128, NT])
        rs_b = sb(octx, "rs_b", [128, T]); mr_b = sb(octx, "mr_b", [128, T])
        rrms_b = sb(octx, "rrms_b", [128, T])
        wsl = [sb(octx, f"wsl{i}", [128, KC * 128], BF16) for i in range(2)]
        rtm = sb(octx, "rtm", [128, KC * 8])
        raff = sb(octx, "raff", [128, KC * 8])
        rsc1 = sb(octx, "rsc1", [128, KC * 8])
        rcol = sb(octx, "rcol", [128, 12])

        PP = {}
        _pp_next = [0]

        def pp_alloc(name, n):
            PP[name] = _pp_next[0]
            _pp_next[0] += n

        def pcol(name, i):
            o = PP[name] + i
            return ppar[:, o:o + 1]

        if 0 in layers:
            pp_alloc("bin", 48); pp_alloc("lnw", KC); pp_alloc("lnb", KC); pp_alloc("bout", KC)
        if 1 in layers:
            pp_alloc("pre1", KC); pp_alloc("pre2", KC); pp_alloc("post1", KC)
            pp_alloc("post2", KC); pp_alloc("post", KC); pp_alloc("rsc", KC)

        def dma_s(dst, src, sem):
            P.dma("sync", sem, lambda e, d=dst, s=src: e.dma_start(out=d, in_=s))

        def dma_g(dst, src, sem):
            P.dma("gpsimd", sem, lambda e, d=dst, s=src: e.dma_start(out=d, in_=s))

        def load_slab_b(dst_tile, src_ap, b):
            P.dma("scalar", sca[b],
                  lambda e, d=dst_tile[:], s2=src_ap: e.dma_start(out=d, in_=s2))
            return sca[b].n

        l0w_rdy = [0]

        def preconvert():
            specs = []
            if 0 in layers:
                specs += [("win_sl", "win_b", 48, KC * 128), ("wout_sl", "wout_b", KC, KC * 128)]
            if 1 in layers:
                specs += [("wq_sl", "wq_b", KC, KC * 128), ("wk_sl", "wk_b", KC, KC * 128),
                          ("wv_sl", "wv_b", KC, KC * 128), ("wo_sl", "wo_b", KC, KC * 128),
                          ("wg1_sl", "wg1_b", 64, KC * 128), ("wu1_sl", "wu1_b", 64, KC * 128),
                          ("wd1_sl", "wd1_b", KC, 64 * 128), ("wed_sl", "wed_b", KC, 64 * 128),
                          ("weg_sl", "weg_b", E * 8, KC * 128), ("weu_sl", "weu_b", E * 8, KC * 128)]
            CH = 4 * 1024 * 1024
            for si, (srcn, dstn, nm, F) in enumerate(specs):
                src = dtn[srcn]
                dst = dtn[dstn]
                sem = gdd[0] if srcn in ("win_sl", "wout_sl") else gdw[0]
                tot = nm * 128 * F
                for off in range(0, tot, CH):
                    w = min(CH, tot - off)
                    P.dma("gpsimd", sem,
                          lambda e, d=bass.AP(dst.tensor, off, [[1, 1], [1, w]]),
                          s2=bass.AP(src.tensor, off, [[1, 1], [1, w]]):
                          e.dma_start(out=d, in_=s2))
                if srcn == "wout_sl":
                    l0w_rdy[0] = gdd[0].n

        def chunk_ap(dram_ap, c, tau, rows=128, ncols=T, w=NT):
            return bass.AP(dram_ap.tensor, c * 128 * ncols + tau * w, [[ncols, rows], [1, w]])

        def chunk4_ap(dram_ap, q, tau):
            return bass.AP(dram_ap.tensor, q * 4 * 128 * T + tau * NT,
                           [[T, 128], [128 * T, 4], [1, NT]])

        _xb_marks = {}

        def stream4(src_dram, tau, consume, xb, extra=None):
            key = id(xb[0])
            marks = _xb_marks.setdefault(key, [(0, 0, 0), (0, 0, 0)])
            for q in range(4):
                b = q % 2
                mv, ma, mp = marks[b]
                P.wait("sync", sv, mv)
                P.wait("sync", sa, ma)
                P.wait("sync", sp, mp)
                if extra is not None:
                    extra()
                dma_s(xb[b][:].rearrange("p (c j) -> p c j", j=NT),
                      chunk4_ap(src_dram, q, tau), sdx[b])
                ld = sdx[b].n
                for j in range(4):
                    consume(4 * q + j, xb[b][:, j * NT:(j + 1) * NT], b, ld)
                marks[b] = (sv.n, sa.n, sp.n)

        def cm(tile, c, tau=None, w=NT):
            if tau is None:
                return tile[:, c * w:(c + 1) * w]
            return tile[:, (c * NTAU + tau) * w:(c * NTAU + tau + 1) * w]

        def bcast(row_ap, dst):
            P.wait("tensor", sv)
            P.pe(lambda t, r=row_ap: t.matmul(pbc[:], onesr_s[:], r, start=True, stop=True),
                 inc=(sp, 1))
            P.wait("vector", sp)
            P.vec(lambda v, d=dst: v.tensor_copy(out=d, in_=pbc[:]), inc=(sv, 1))

        mm_banks = [pa, pb, pc_, pd_]
        bank_free_at = [(0, 0)] * 4
        bank_ctr = [0]

        def mm_group(rhs_views, lhsT_views, evac, bank_pair=0, pre=None):
            g = bank_ctr[0]; bank_ctr[0] += 1
            bi = bank_pair * 2 + (g % 2)
            bank = mm_banks[bi]
            P.wait("tensor", sv, bank_free_at[bi][0])
            P.wait("tensor", sa, bank_free_at[bi][1])
            if pre is not None:
                pre()
            nk = len(rhs_views)
            for i in range(nk):
                P.pe(lambda t, rv=rhs_views[i], lv=lhsT_views[i], i=i, nk=nk, bank=bank:
                     t.matmul(bank[:], lv, rv, start=(i == 0), stop=(i == nk - 1)),
                     inc=(sp, 1) if i == nk - 1 else None)
            P.wait("vector", sp)
            evac(bank)
            bank_free_at[bi] = (sv.n, sa.n)

        def _stats_chunk(c, view, sq_ping, sq_done_at, do_sum):
            if do_sum:
                P.pe(lambda t, v=view, c=c: t.matmul(pr1[:], onesk_s[:], v,
                                                     start=(c == 0), stop=(c == KC - 1)),
                     inc=(sp, 1) if c == KC - 1 else None)
            sq = sq_ping[c % 2]
            P.wait("scalar", sp, sq_done_at[c % 2])
            P.act(lambda s, v=view, sq=sq: s.activation(out=sq[:], in_=v, func=AF.Square),
                  inc=(sa, 1))
            P.wait("tensor", sa)
            P.pe(lambda t, sq=sq, c=c: t.matmul(pr2[:], onesk_s[:], sq[:],
                                                start=(c == 0), stop=(c == KC - 1)),
                 inc=(sp, 1))
            sq_done_at[c % 2] = sp.n

        def stats_pass(src_dram, do_sum, per_chunk=None, xb=None):
            sq_ping = [sc1, sc2]
            sq_done_at = [0, 0]
            P.wait("tensor", sv, pr_free[0])
            if per_chunk is None:
                def consume(c, view, b, ld):
                    P.wait("tensor", sdx[b], ld)
                    P.wait("scalar", sdx[b], ld)
                    _stats_chunk(c, view, sq_ping, sq_done_at, do_sum)
                stream4(src_dram, stats_pass.tau, consume, xb)
            else:
                for c in range(KC):
                    view, ready = per_chunk(c)
                    if ready is not None:
                        ready()
                    _stats_chunk(c, view, sq_ping, sq_done_at, do_sum)
        stats_pass.tau = 0

        pr_free = [0]

        def rowmath_musd(eps, out_rs_in_r2, with_mu=True):
            P.wait("vector", sp)
            if with_mu:
                P.vec(lambda v: v.tensor_scalar(out=r1[:], in0=pr1[:], scalar1=1.0 / D,
                                                scalar2=None, op0=ALU.mult))
            P.vec(lambda v: v.tensor_scalar(out=r2[:], in0=pr2[:], scalar1=1.0 / D,
                                            scalar2=None, op0=ALU.mult), inc=(sv, 1))
            P.wait("scalar", sv)
            if with_mu:
                P.act(lambda s: s.activation(out=r3[:], in_=r1[:], func=AF.Square), inc=(sa, 1))
                P.wait("vector", sa)
                P.vec(lambda v: v.tensor_tensor(out=r2[:], in0=r2[:], in1=r3[:],
                                                op=ALU.subtract), inc=(sv, 1))
                P.wait("scalar", sv)
            P.act(lambda s, eps=eps: s.activation(out=r3[:], in_=r2[:], func=AF.Sqrt,
                                                  bias=eps_s[0:1, eps:eps + 1]), inc=(sa, 1))
            P.wait("vector", sa)
            P.vec(lambda v: v.reciprocal(out=r2[:], in_=r3[:]), inc=(sv, 1))
            pr_free[0] = sv.n

        def load_params():
            if 0 in layers:
                dma_s(ppar[:, PP["bin"]:PP["bin"] + 48], dtn["bin_pp"], smisc)
                dma_s(ppar[:, PP["lnw"]:PP["lnw"] + KC], dtn["lnw_pp"], smisc)
                dma_s(ppar[:, PP["lnb"]:PP["lnb"] + KC], dtn["lnb_pp"], smisc)
                dma_s(ppar[:, PP["bout"]:PP["bout"] + KC], dtn["bout_pp"], smisc)
            if 1 in layers:
                dma_s(ppar[:, PP["pre1"]:PP["pre1"] + KC], dtn["pre1_pp"], smisc)
                dma_s(ppar[:, PP["pre2"]:PP["pre2"] + KC], dtn["pre2_pp"], smisc)
                dma_s(ppar[:, PP["post1"]:PP["post1"] + KC], dtn["post1_pp"], smisc)
                dma_s(ppar[:, PP["post2"]:PP["post2"] + KC], dtn["post2_pp"], smisc)
                dma_s(ppar[:, PP["post"]:PP["post"] + KC], dtn["post_pp"], smisc)
                dma_s(ppar[:, PP["rsc"]:PP["rsc"] + KC], dtn["rsc_pp"], smisc)
            dma_s(eps_s[:], eps_c, smisc)
            dma_s(onesk_s[:], ones_k_c, smisc)
            dma_s(onesr_s[:], ones_r_c, smisc)
            dma_s(ident_s[:], ident_c, smisc)
            if 0 in layers:
                dma_s(r1[0:1, 0:16], dtn["lut_row"], smisc)
                P.wait("tensor", smisc)
                P.pe(lambda t: t.matmul(pbc[:, 0:16], onesr_s[:], r1[0:1, 0:16],
                                        start=True, stop=True), inc=(sp, 1))
                P.wait("vector", sp)
                P.vec(lambda v: v.tensor_copy(out=lutc[:], in_=pbc[:, 0:16]), inc=(sv, 1))
                P.wait("vector", sv)
            if 1 in layers:
                P.wait("sync", sp)
                dma_s(r1[0:1, 0:E], dtn["pes_row"], smisc)
                P.wait("tensor", smisc)
                P.wait("tensor", sv)
                P.pe(lambda t: t.matmul(pbc[:, 0:E], onesr_s[:], r1[0:1, 0:E],
                                        start=True, stop=True), inc=(sp, 1))
                P.wait("vector", sp)
                P.vec(lambda v: v.tensor_copy(out=pes_b[:], in_=pbc[:, 0:E]), inc=(sv, 1))
                P.wait("vector", sv)
            P.wait("vector", smisc)
            P.wait("scalar", smisc)
            P.wait("tensor", smisc)

        def ghost_mat(ctx):
            HF = MC0 * 128 // 2
            idxb = [sb(ctx, f"idxb{i}", [128, HF], I32) for i in range(2)]
            idbf = sb(ctx, "idbf", [128, HF], BF16)
            gacc = sb(ctx, "gacc0", [128, HF], BF16)
            gtmp = sb(ctx, "gtmp", [128, HF], BF16)
            work = []
            for si2 in range(11):
                src = dtn["gu_my"][si2]
                dst = bass.AP(dtn["gu_loc"].tensor, si2 * 128 * KC * 128,
                              [[KC * 128, 128], [1, KC * 128]])
                work.append((src, dst, KC * 128))
            for si2 in range(2):
                for h in range(2):
                    src = bass.AP(dtn["d_my"].tensor, si2 * 128 * MC0 * 128 + h * HF,
                                  [[MC0 * 128, 128], [1, HF]])
                    dst = bass.AP(dtn["d_loc"].tensor, si2 * 128 * MC0 * 128 + h * HF,
                                  [[MC0 * 128, 128], [1, HF]])
                    work.append((src, dst, HF))
            acc_store = [0]
            conv_done = {}
            for si, (src, dst, F) in enumerate(work):
                b = si % 2
                if si >= 2:
                    P.wait("scalar", sv, conv_done[si - 2])
                P.dma("scalar", sdw[b],
                      lambda e, d=idxb[b][:, 0:F], s2=src: e.dma_start(out=d, in_=s2))
                P.wait("vector", sdw[b])
                P.vec(lambda v, b=b, F=F: v.tensor_copy(out=idbf[:, 0:F],
                                                        in_=idxb[b][:, 0:F]), inc=(sv, 1))
                conv_done[si] = sv.n
                P.wait("vector", sdo[0], acc_store[0])
                P.vec(lambda v, F=F: v.tensor_scalar(
                    out=gacc[:, 0:F], in0=idbf[:, 0:F], scalar1=0.0,
                    scalar2=lutc[:, 0:1], op0=ALU.is_equal, op1=ALU.mult))
                for k in range(1, 16):
                    P.vec(lambda v, k=k, F=F: v.tensor_scalar(
                        out=gtmp[:, 0:F], in0=idbf[:, 0:F], scalar1=float(k),
                        scalar2=lutc[:, k:k + 1], op0=ALU.is_equal, op1=ALU.mult))
                    P.vec(lambda v, F=F: v.tensor_tensor(
                        out=gacc[:, 0:F], in0=gacc[:, 0:F], in1=gtmp[:, 0:F],
                        op=ALU.add), inc=(sv, 1) if k == 15 else None)
                P.wait("scalar", sv)
                P.dma("scalar", sdo[0],
                      lambda e, d=dst, s2=gacc[:, 0:F]: e.dma_start(out=d, in_=s2))
                acc_store[0] = sdo[0].n
            P.wait("gpsimd", sdo[0])
            P._add("gpsimd", lambda e: e.collective_compute(
                "AllGather", ALU.bypass, replica_groups=[list(range(8))],
                ins=[dtn["gu_loc"]], outs=[dtn["gu_all"]]), inc=(ccs, 1))
            P._add("gpsimd", lambda e: e.collective_compute(
                "AllGather", ALU.bypass, replica_groups=[list(range(8))],
                ins=[dtn["d_loc"]], outs=[dtn["d_all"]]), inc=(ccs, 1))

        def layer0_stats_h(ctx, big_a):
            xn32 = sb(ctx, "xn32", [128, KC * NT])
            xb0 = [sb(ctx, "xb0a", [128, 4 * NT]), sb(ctx, "xb0b", [128, 4 * NT])]
            for tau in range(NTAU):
                stats_pass.tau = tau
                stats_pass(x0, do_sum=True, xb=xb0)
                rowmath_musd(0, r2)
                bcast(r2[:], rs_b[:, tau * NT:(tau + 1) * NT])
                P.vec(lambda v: v.tensor_tensor(out=r3[:], in0=r1[:], in1=r2[:], op=ALU.mult),
                      inc=(sv, 1))
                bcast(r3[:], mr_b[:, tau * NT:(tau + 1) * NT])
            for tau in range(NTAU):
                rsb = rs_b[:, tau * NT:(tau + 1) * NT]
                mrb = mr_b[:, tau * NT:(tau + 1) * NT]
                sq_ping = [sc1, sc2]
                sq_done_at = [0, 0]
                P.wait("tensor", sv, pr_free[0])

                def consume(c, view, b, ld, tau=tau, rsb=rsb, mrb=mrb,
                            sq_ping=sq_ping, sq_done_at=sq_done_at):
                    P.wait("vector", sdx[b], ld)
                    P.vec(lambda v, c=c, view=view, rsb=rsb: v.tensor_tensor(
                        out=cm(xn32, c), in0=view, in1=rsb, op=ALU.mult))
                    P.vec(lambda v, c=c, mrb=mrb: v.tensor_tensor(
                        out=cm(xn32, c), in0=cm(xn32, c), in1=mrb, op=ALU.subtract),
                        inc=(sv, 1))
                    P.wait("tensor", sv)
                    P.wait("scalar", sv)
                    _stats_chunk(c, cm(xn32, c), sq_ping, sq_done_at, True)
                stream4(x0, tau, consume, xb0)
                rowmath_musd(0, r2)
                bcast(r1[:], bcA[:])
                bcast(r2[:], bcB[:])
                for c in range(KC):
                    P.vec(lambda v, c=c: v.tensor_tensor(out=sc1[:], in0=cm(xn32, c),
                                                         in1=bcA[:], op=ALU.subtract))
                    P.vec(lambda v: v.tensor_tensor(out=sc1[:], in0=sc1[:], in1=bcB[:],
                                                    op=ALU.mult))
                    P.vec(lambda v, c=c, tau=tau: v.tensor_scalar(
                        out=cm(big_a, c, tau), in0=sc1[:], scalar1=pcol("lnw", c),
                        scalar2=pcol("lnb", c), op0=ALU.mult, op1=ALU.add),
                        inc=(sv, 1) if c == KC - 1 else None)

        def layer0_attn(ctx, big_a, big_b):
            kraw = sb(ctx, "kraw", [128, KC * NT], BF16)
            costl = sc3
            sintl = sc4
            h_rdy = sv.n
            wsl_mark = [0, 0]
            slab_ctr = [0]
            P.wait("scalar", gdd[0], l0w_rdy[0])
            P.vec(lambda v: v.memset(acol[:], 0.0), inc=(sv, 1))

            def load_slab(src_ap):
                i = slab_ctr[0]; slab_ctr[0] += 1
                b = i % 2
                P.wait("scalar", sp, wsl_mark[b])
                return b, load_slab_b(wsl[b], src_ap, b)

            def rope_pass(tile, tau, with_norm):
                for c in range(4):
                    cA, cB = c, c + 4
                    P.wait("sync", sv)
                    dma_s(costl[:], bass.AP(dtn["cs_pp"].tensor, c * T + tau * NT,
                                            [[8 * T, 128], [1, NT]]), sdx[0])
                    dma_s(sintl[:], bass.AP(dtn["cs_pp"].tensor, (4 + c) * T + tau * NT,
                                            [[8 * T, 128], [1, NT]]), sdx[1])
                    P.wait("vector", sdx[0])
                    P.wait("vector", sdx[1])
                    P.vec(lambda v, tile=tile, cA=cA, tau=tau: v.tensor_tensor(
                        out=sc1[:], in0=cm(tile, cA, tau), in1=costl[:], op=ALU.mult))
                    P.vec(lambda v, tile=tile, cB=cB, tau=tau: v.tensor_tensor(
                        out=sc2[:], in0=cm(tile, cB, tau), in1=sintl[:], op=ALU.mult))
                    P.vec(lambda v: v.tensor_tensor(out=sc1[:], in0=sc1[:], in1=sc2[:],
                                                    op=ALU.subtract))
                    P.vec(lambda v, tile=tile, cB=cB, tau=tau: v.tensor_tensor(
                        out=sc2[:], in0=cm(tile, cB, tau), in1=costl[:], op=ALU.mult))
                    P.vec(lambda v, tile=tile, cA=cA, tau=tau: v.tensor_tensor(
                        out=sc5[:], in0=cm(tile, cA, tau), in1=sintl[:], op=ALU.mult))
                    P.vec(lambda v: v.tensor_tensor(out=sc2[:], in0=sc2[:], in1=sc5[:],
                                                    op=ALU.add))
                    P.vec(lambda v, tile=tile, cA=cA, tau=tau: v.tensor_copy(
                        out=cm(tile, cA, tau), in_=sc1[:]))
                    P.vec(lambda v, tile=tile, cB=cB, tau=tau: v.tensor_copy(
                        out=cm(tile, cB, tau), in_=sc2[:]), inc=(sv, 1))
                if with_norm:
                    def per_chunk(c, tile=tile, tau=tau):
                        def ready(c=c):
                            P.wait("scalar", sv)
                            P.wait("tensor", sv)
                        return cm(tile, c, tau), ready
                    stats_pass(None, do_sum=False, per_chunk=per_chunk)
                    P.wait("vector", sp)
                    P.vec(lambda v: v.tensor_copy(out=r3[:], in_=pr2[:]), inc=(sv, 1))
                    P.wait("scalar", sv)
                    P.act(lambda s: s.activation(out=r3[:], in_=r3[:], func=AF.Sqrt),
                          inc=(sa, 1))
                    P.wait("vector", sa)
                    P.vec(lambda v: v.reciprocal(out=r2[:], in_=r3[:]), inc=(sv, 1))
                    bcast(r2[:], bcA[:])

            for tau in range(NTAU):
                for mc in range(KC):
                    b, ld = load_slab(dtn["win_b"][16 + mc])

                    def pre(ld=ld, tau=tau, mc=mc, b=b):
                        P.wait("tensor", sca[b], ld)
                        if tau == 0 and mc == 0:
                            P.wait("tensor", sv, h_rdy)

                    def evk(bank, mc=mc, tau=tau):
                        P.vec(lambda v, bank=bank, mc=mc, tau=tau: v.tensor_scalar(
                            out=cm(kraw, mc), in0=bank[:], scalar1=pcol("bin", 16 + mc),
                            scalar2=None, op0=ALU.add), inc=(sv, 1))
                    mm_group([cm(big_a, kc, tau) for kc in range(KC)],
                             [wsl[b][:, kc * 128:(kc + 1) * 128] for kc in range(KC)],
                             evk, bank_pair=0, pre=pre)
                    wsl_mark[b] = sp.n
                for c in range(4):
                    cA, cB = c, c + 4
                    P.wait("sync", sv)
                    dma_s(costl[:], bass.AP(dtn["cs_pp"].tensor, c * T + tau * NT,
                                            [[8 * T, 128], [1, NT]]), sdx[0])
                    dma_s(sintl[:], bass.AP(dtn["cs_pp"].tensor, (4 + c) * T + tau * NT,
                                            [[8 * T, 128], [1, NT]]), sdx[1])
                    P.wait("vector", sdx[0])
                    P.wait("vector", sdx[1])
                    P.vec(lambda v, cA=cA: v.tensor_tensor(
                        out=sc1[:], in0=cm(kraw, cA), in1=costl[:], op=ALU.mult))
                    P.vec(lambda v, cB=cB: v.tensor_tensor(
                        out=sc2[:], in0=cm(kraw, cB), in1=sintl[:], op=ALU.mult))
                    P.vec(lambda v: v.tensor_tensor(out=sc1[:], in0=sc1[:], in1=sc2[:],
                                                    op=ALU.subtract))
                    P.vec(lambda v, cB=cB: v.tensor_tensor(
                        out=sc2[:], in0=cm(kraw, cB), in1=costl[:], op=ALU.mult))
                    P.vec(lambda v, cA=cA: v.tensor_tensor(
                        out=sc5[:], in0=cm(kraw, cA), in1=sintl[:], op=ALU.mult))
                    P.vec(lambda v: v.tensor_tensor(out=sc2[:], in0=sc2[:], in1=sc5[:],
                                                    op=ALU.add))
                    P.vec(lambda v, cA=cA: v.tensor_copy(out=cm(kraw, cA), in_=sc1[:]))
                    P.vec(lambda v, cB=cB: v.tensor_copy(out=cm(kraw, cB), in_=sc2[:]),
                          inc=(sv, 1))
                def per_chunk_k(c):
                    def ready():
                        P.wait("scalar", sv)
                        P.wait("tensor", sv)
                    return cm(kraw, c), ready
                stats_pass(None, do_sum=False, per_chunk=per_chunk_k)
                P.wait("vector", sp)
                P.vec(lambda v: v.tensor_copy(out=r3[:], in_=pr2[:]), inc=(sv, 1))
                P.wait("scalar", sv)
                P.act(lambda s: s.activation(out=r3[:], in_=r3[:], func=AF.Sqrt), inc=(sa, 1))
                P.wait("vector", sa)
                P.vec(lambda v: v.reciprocal(out=r2[:], in_=r3[:]), inc=(sv, 1))
                bcast(r2[:], bcA[:])

                for mc in range(KC):
                    b, ld = load_slab(dtn["win_b"][32 + mc])

                    def prev(ld=ld, b=b):
                        P.wait("tensor", sca[b], ld)

                    def evv(bank, mc=mc, tau=tau):
                        P.vec(lambda v, bank=bank, mc=mc: v.tensor_scalar(
                            out=sc1[:], in0=bank[:], scalar1=pcol("bin", 32 + mc),
                            scalar2=None, op0=ALU.add))
                        P.vec(lambda v, mc=mc: v.tensor_tensor(
                            out=sc1[:], in0=sc1[:], in1=cm(kraw, mc), op=ALU.mult))
                        P.vec(lambda v: v.tensor_tensor(out=sc1[:], in0=sc1[:], in1=bcA[:],
                                                        op=ALU.mult))
                        P.vec(lambda v: v.reduce_sum(out=sc2[:, 0:1], in_=sc1[:], axis=AX.X))
                        P.vec(lambda v, mc=mc: v.tensor_tensor(
                            out=acol[:, mc:mc + 1], in0=acol[:, mc:mc + 1],
                            in1=sc2[:, 0:1], op=ALU.add), inc=(sv, 1))
                    mm_group([cm(big_a, kc, tau) for kc in range(KC)],
                             [wsl[b][:, kc * 128:(kc + 1) * 128] for kc in range(KC)],
                             evv, bank_pair=0, pre=prev)
                    wsl_mark[b] = sp.n

                for mc in range(KC):
                    b, ld = load_slab(dtn["win_b"][mc])

                    def preq(ld=ld, b=b):
                        P.wait("tensor", sca[b], ld)

                    def evq(bank, mc=mc, tau=tau):
                        P.vec(lambda v, bank=bank, mc=mc, tau=tau: v.tensor_scalar(
                            out=cm(big_b, mc, tau), in0=bank[:], scalar1=pcol("bin", mc),
                            scalar2=None, op0=ALU.add), inc=(sv, 1))
                    mm_group([cm(big_a, kc, tau) for kc in range(KC)],
                             [wsl[b][:, kc * 128:(kc + 1) * 128] for kc in range(KC)],
                             evq, bank_pair=0, pre=preq)
                    wsl_mark[b] = sp.n
                rope_pass(big_b, tau, with_norm=False)

            P.wait("vector", sv)
            for c in range(KC):
                P.vec(lambda v, c=c: v.tensor_scalar(
                    out=big_b[:, c * T:(c + 1) * T], in0=big_b[:, c * T:(c + 1) * T],
                    scalar1=acol[:, c:c + 1], scalar2=None, op0=ALU.mult),
                    inc=(sv, 1) if c == KC - 1 else None)
            y_rdy = sv.n

            for mc in range(KC):
                b, ld = load_slab(dtn["wout_b"][mc])
                for tau in range(NTAU):
                    P.wait("sync", sv)
                    dma_s(sc4[:], chunk_ap(x0, mc, tau), sdx[1])
                    xld = sdx[1].n

                    def pred(ld=ld, tau=tau, b=b):
                        if tau == 0:
                            P.wait("tensor", sca[b], ld)
                            P.wait("tensor", sv, y_rdy)

                    def evd(bank, mc=mc, tau=tau, xld=xld):
                        P.wait("vector", sdx[1], xld)
                        rsb = rs_b[:, tau * NT:(tau + 1) * NT]
                        mrb = mr_b[:, tau * NT:(tau + 1) * NT]
                        P.vec(lambda v, rsb=rsb: v.tensor_tensor(
                            out=sc1[:], in0=sc4[:], in1=rsb, op=ALU.mult))
                        P.vec(lambda v, mrb=mrb: v.tensor_tensor(
                            out=sc1[:], in0=sc1[:], in1=mrb, op=ALU.subtract))
                        P.vec(lambda v: v.tensor_tensor(out=sc1[:], in0=sc1[:], in1=sc4[:],
                                                        op=ALU.add))
                        P.vec(lambda v, bank=bank, mc=mc: v.tensor_scalar(
                            out=sc2[:], in0=bank[:], scalar1=pcol("bout", mc),
                            scalar2=None, op0=ALU.add))
                        P.wait("vector", sdo[0])
                        P.vec(lambda v: v.tensor_tensor(out=sc3[:], in0=sc1[:], in1=sc2[:],
                                                        op=ALU.add), inc=(sv, 1))
                        P.wait("sync", sv)
                        dma_s(chunk_ap(dtn["x1_scr"], mc, tau), sc3[:], sdo[0])
                    mm_group([cm(big_b, kc, tau) for kc in range(KC)],
                             [wsl[b][:, kc * 128:(kc + 1) * 128] for kc in range(KC)],
                             evd, bank_pair=0, pre=pred)
                wsl_mark[b] = sp.n

        def layer0_ffn(ctx):
            xn_bf = sb(ctx, "xn_bf", [128, KC * NT], BF16)
            xb4 = [sb(ctx, "xb4a", [128, 4 * NT]), sb(ctx, "xb4b", [128, 4 * NT])]
            hss = sb(ctx, "hss", [128, MC0 * NT], BF16)
            wg2 = [sb(ctx, f"wg2_{i}", [128, KC * 128], BF16) for i in range(2)]
            wdl = [sb(ctx, f"wdl{i}", [128, MC0 * 128], BF16) for i in range(2)]
            gu_mark = [sp.n, sp.n]
            wd_mark = [0, 0]
            P.wait("sync", sdo[0])
            P.wait("sync", sdo[1])
            P.wait("sync", ccs)
            for tau in range(NTAU):
                rsb = rs_b[:, tau * NT:(tau + 1) * NT]
                mrb = mr_b[:, tau * NT:(tau + 1) * NT]
                for c in range(KC):
                    buf = [sc3, sc4][c % 2]
                    P.wait("sync", sv)
                    P.wait("sync", sdo[0])
                    dma_s(buf[:], chunk_ap(x0, c, tau), sdx[c % 2])
                    P.wait("vector", sdx[c % 2])
                    P.vec(lambda v, buf=buf, rsb=rsb: v.tensor_tensor(
                        out=sc1[:], in0=buf[:], in1=rsb, op=ALU.mult))
                    P.vec(lambda v, c=c, mrb=mrb: v.tensor_tensor(
                        out=cm(xn_bf, c), in0=sc1[:], in1=mrb, op=ALU.subtract),
                        inc=(sv, 1))
                xn_rdy = sv.n
                for mc in range(MC0):
                    bg = mc % 2
                    P.wait("sync", sp, gu_mark[bg])
                    dma_s(wsl[bg][:], bass.AP(dtn["gu_all"].tensor, mc * 128 * KC * 128,
                                              [[KC * 128, 128], [1, KC * 128]]), sdw[bg])
                    dma_s(wg2[bg][:], bass.AP(dtn["gu_all"].tensor, (43 + mc) * 128 * KC * 128,
                                              [[KC * 128, 128], [1, KC * 128]]), sdw[bg])
                    ld = sdw[bg].n

                    def preg(ld=ld, xn_rdy=xn_rdy, tau=tau, mc=mc, bg=bg):
                        P.wait("tensor", sdw[bg], ld)
                        if mc == 0:
                            P.wait("tensor", sv, xn_rdy)

                    def evg(bank):
                        P.wait("scalar", sp)
                        P.wait("scalar", sv)
                        P.act(lambda s, bank=bank: s.activation(out=sc2[:], in_=bank[:],
                                                                func=AF.Silu))

                    def evu(bank, mc=mc):
                        P.wait("vector", sa)
                        P.vec(lambda v, bank=bank, mc=mc: v.tensor_tensor(
                            out=cm(hss, mc), in0=sc2[:], in1=bank[:], op=ALU.mult),
                            inc=(sv, 1))
                    mm_group([cm(xn_bf, kc) for kc in range(KC)],
                             [wsl[bg][:, kc * 128:(kc + 1) * 128] for kc in range(KC)],
                             evg, bank_pair=0, pre=preg)
                    mm_group([cm(xn_bf, kc) for kc in range(KC)],
                             [wg2[bg][:, kc * 128:(kc + 1) * 128] for kc in range(KC)],
                             evu, bank_pair=1)
                    gu_mark[bg] = sp.n
                hss_rdy = sv.n
                for hm in range(KC):
                    b = hm % 2
                    P.wait("sync", sp, wd_mark[b])
                    dma_s(wdl[b][:], bass.AP(dtn["d_all"].tensor, hm * 128 * MC0 * 128,
                                             [[MC0 * 128, 128], [1, MC0 * 128]]), sdd[b])
                    ld = sdd[b].n
                    P.wait("sync", sv)
                    dma_s(sc4[:], chunk_ap(dtn["x1_scr"], hm, tau), sdx[1])
                    xld = sdx[1].n

                    def prew(ld=ld, hss_rdy=hss_rdy, hm=hm, b=b):
                        P.wait("tensor", sdd[b], ld)
                        if hm == 0:
                            P.wait("tensor", sv, hss_rdy)

                    def evdn(bank, hm=hm, tau=tau, xld=xld):
                        P.wait("vector", sdx[1], xld)
                        P.wait("vector", sdo[0])
                        P.vec(lambda v, bank=bank: v.tensor_tensor(
                            out=sc3[:], in0=bank[:], in1=sc4[:], op=ALU.add), inc=(sv, 1))
                        P.wait("sync", sv)
                        dma_s(chunk_ap(dtn["x1_scr"], hm, tau), sc3[:], sdo[0])
                    mm_group([cm(hss, kc) for kc in range(MC0)],
                             [wdl[b][:, kc * 128:(kc + 1) * 128] for kc in range(MC0)],
                             evdn, bank_pair=0, pre=prew)
                    wd_mark[b] = sp.n

        def phase_barrier(exclude=()):
            allsems = [sv, sa, sp, smisc, sdx[0], sdx[1], sdw[0], sdw[1],
                       gdw[0], gdw[1], sdo[0], sdo[1], sdd[0], sdd[1], gdd[0], gdd[1],
                       sca[0], sca[1], scp[0], scp[1], scp[2], scp[3],
                       sco[0], sco[1], sco[2], sco[3], ccs]
            for eng in ("vector", "scalar", "tensor", "sync", "gpsimd"):
                for sem in allsems:
                    if sem in exclude:
                        continue
                    P.wait(eng, sem)

        x1d = dtn["x1_scr"]

        xbL1 = [None]

        def build_xnt(xnt, tau):
            rsb = rs_b[:, tau * NT:(tau + 1) * NT]
            mrb = mr_b[:, tau * NT:(tau + 1) * NT]

            def consume(c, view, b, ld, rsb=rsb, mrb=mrb):
                P.wait("vector", sdx[b], ld)
                P.vec(lambda v, view=view, rsb=rsb: v.tensor_tensor(
                    out=sc1[:], in0=view, in1=rsb, op=ALU.mult))
                P.vec(lambda v, c=c, mrb=mrb: v.tensor_tensor(
                    out=xnt[:, c * NT:(c + 1) * NT], in0=sc1[:], in1=mrb,
                    op=ALU.subtract), inc=(sv, 1))
            stream4(x1d, tau, consume, xbL1[0],
                    extra=lambda: P.wait("sync", sdo[0]))

        def layer1_stats():
            for tau in range(NTAU):
                stats_pass.tau = tau
                stats_pass(x1d, do_sum=True, xb=xbL1[0])
                rowmath_musd(0, r2)
                bcast(r2[:], rs_b[:, tau * NT:(tau + 1) * NT])
                P.vec(lambda v: v.tensor_tensor(out=r3[:], in0=r1[:], in1=r2[:], op=ALU.mult))
                bcast(r3[:], mr_b[:, tau * NT:(tau + 1) * NT])
            for tau in range(NTAU):
                rsb = rs_b[:, tau * NT:(tau + 1) * NT]
                mrb = mr_b[:, tau * NT:(tau + 1) * NT]
                sq_ping = [sc1, sc2]
                sq_done_at = [0, 0]
                P.wait("tensor", sv, pr_free[0])

                def consume(c, view, b, ld, rsb=rsb, mrb=mrb,
                            sq_ping=sq_ping, sq_done_at=sq_done_at):
                    P.wait("vector", sdx[b], ld)
                    P.wait("vector", sa)
                    P.vec(lambda v, view=view, rsb=rsb: v.tensor_tensor(
                        out=sc5[:], in0=view, in1=rsb, op=ALU.mult))
                    P.vec(lambda v, mrb=mrb: v.tensor_tensor(
                        out=sc5[:], in0=sc5[:], in1=mrb, op=ALU.subtract), inc=(sv, 1))
                    P.wait("tensor", sv)
                    P.wait("scalar", sv)
                    _stats_chunk(c, sc5[:], sq_ping, sq_done_at, False)
                stream4(x1d, tau, consume, xbL1[0])
                rowmath_musd(1, r2, with_mu=False)
                bcast(r2[:], rrms_b[:, tau * NT:(tau + 1) * NT])

        def layer1_router(ctx):
            rwt = sb(ctx, "rwt", [128, KC * 8])
            rwb = sb(ctx, "rwb", [128, KC * 8], BF16)
            lgf = sb(ctx, "lgf", [8, T])
            xnt = sb(ctx, "xnt", [128, KC * NT], BF16)
            dma_s(rwt[:], dtn["rw_sl"][0], smisc)
            P.wait("vector", smisc)
            for kc in range(KC):
                P.vec(lambda v, kc=kc: v.tensor_scalar(
                    out=rwb[:, kc * 8:(kc + 1) * 8], in0=rwt[:, kc * 8:(kc + 1) * 8],
                    scalar1=pcol("rsc", kc), scalar2=None, op0=ALU.mult))
            rwb_rdy = sv.n
            for tau in range(NTAU):
                build_xnt(xnt, tau)
                P.wait("tensor", sv)
                for kc in range(KC):
                    P.pe(lambda t, kc=kc, tau=tau: t.matmul(
                        pbc[0:8, :], rwb[:, kc * 8:(kc + 1) * 8],
                        xnt[:, kc * NT:(kc + 1) * NT],
                        start=(kc == 0), stop=(kc == KC - 1)),
                        inc=(sp, 1) if kc == KC - 1 else None)
                P.wait("vector", sp)
                P.vec(lambda v, tau=tau: v.tensor_tensor(
                    out=lgf[0:8, tau * NT:(tau + 1) * NT], in0=pbc[0:8, :],
                    in1=rrms_b[0:8, tau * NT:(tau + 1) * NT], op=ALU.mult))
                P.vec(lambda v, tau=tau: v.tensor_scalar(
                    out=lgf[0:8, tau * NT:(tau + 1) * NT],
                    in0=lgf[0:8, tau * NT:(tau + 1) * NT],
                    scalar1=ROOT, scalar2=None, op0=ALU.mult), inc=(sv, 1))
            for tc in range(KC):
                P.wait("tensor", sv)
                P.pe(lambda t, tc=tc: t.transpose(
                    pt[:, 0:8], lgf[0:8, tc * 128:(tc + 1) * 128], ident_s[0:8, 0:8]),
                    inc=(sp, 1))
                P.wait("vector", sp)
                P.vec(lambda v, tc=tc: v.tensor_copy(out=rtm[:, tc * 8:(tc + 1) * 8],
                                                     in_=pt[:, 0:8]), inc=(sv, 1))
            for tc in range(KC):
                p8 = rtm[:, tc * 8:(tc + 1) * 8]
                q8 = raff[:, tc * 8:(tc + 1) * 8]
                s8 = rsc1[:, tc * 8:(tc + 1) * 8]
                P.vec(lambda v, p8=p8: v.reduce_max(out=rcol[:, 0:1], in_=p8, axis=AX.X))
                P.vec(lambda v, p8=p8, q8=q8: v.tensor_scalar(
                    out=q8, in0=p8, scalar1=rcol[:, 0:1], scalar2=None, op0=ALU.is_ge))
                P.vec(lambda v, p8=p8, q8=q8, s8=s8: v.scalar_tensor_tensor(
                    out=s8, in0=q8, scalar=-BIG, in1=p8, op0=ALU.mult, op1=ALU.add))
                P.vec(lambda v, s8=s8: v.reduce_max(out=rcol[:, 1:2], in_=s8, axis=AX.X))
                P.vec(lambda v, s8=s8: v.tensor_scalar(
                    out=s8, in0=s8, scalar1=rcol[:, 1:2], scalar2=None, op0=ALU.is_ge))
                P.vec(lambda v: v.tensor_tensor(out=rcol[:, 2:3], in0=rcol[:, 1:2],
                                                in1=rcol[:, 0:1], op=ALU.subtract))
                P.wait("scalar", sv)
                P.act(lambda s: s.activation(out=rcol[:, 3:4], in_=rcol[:, 2:3], func=AF.Exp))
                P.wait("vector", sa)
                P.vec(lambda v: v.tensor_scalar(out=rcol[:, 4:5], in0=rcol[:, 3:4],
                                                scalar1=1.0, scalar2=None, op0=ALU.add))
                P.vec(lambda v: v.reciprocal(out=rcol[:, 5:6], in_=rcol[:, 4:5]))
                P.vec(lambda v: v.tensor_scalar(out=rcol[:, 6:7], in0=rcol[:, 5:6],
                                                scalar1=-1.0, scalar2=1.0, op0=ALU.mult,
                                                op1=ALU.add))
                P.vec(lambda v, q8=q8: v.tensor_scalar(
                    out=q8, in0=q8, scalar1=rcol[:, 5:6], scalar2=None, op0=ALU.mult))
                P.vec(lambda v, s8=s8: v.tensor_scalar(
                    out=s8, in0=s8, scalar1=rcol[:, 6:7], scalar2=None, op0=ALU.mult))
                P.vec(lambda v, q8=q8, s8=s8: v.tensor_tensor(out=q8, in0=q8, in1=s8,
                                                              op=ALU.add))
                P.vec(lambda v, q8=q8: v.tensor_tensor(out=q8, in0=q8, in1=pes_b[:, 0:8],
                                                       op=ALU.mult), inc=(sv, 1))

        def layer1_attn(ctx):
            kbuf = sb(ctx, "kbuf", [128, KC * NT])
            vhb = sb(ctx, "vhb", [128, KC * NT], BF16)
            xnt = sb(ctx, "xnt_a", [128, KC * NT], BF16)
            carry = sb(ctx, "carry", [128, KC])
            wsl_mark = [0, 0]
            slab_ctr = [0]
            P.vec(lambda v: v.memset(carry[:], 0.0))

            def load_slab(src_ap):
                i = slab_ctr[0]; slab_ctr[0] += 1
                b = i % 2
                P.wait("scalar", sp, wsl_mark[b])
                return b, load_slab_b(wsl[b], src_ap, b)

            for tau in range(NTAU):
                build_xnt(xnt, tau)
                xn_rdy = sv.n
                for mc in range(KC):
                    b, ld = load_slab(dtn["wk_b"][mc])

                    def prk(ld=ld, b=b, xn_rdy=xn_rdy):
                        P.wait("tensor", sca[b], ld)
                        P.wait("tensor", sv, xn_rdy)

                    def evkk(bank, mc=mc):
                        P.vec(lambda v, bank=bank, mc=mc: v.tensor_copy(
                            out=cm(kbuf, mc), in_=bank[:]), inc=(sv, 1))
                    mm_group([xnt[:, kc * NT:(kc + 1) * NT] for kc in range(KC)],
                             [wsl[b][:, kc * 128:(kc + 1) * 128] for kc in range(KC)],
                             evkk, bank_pair=0, pre=prk)
                    wsl_mark[b] = sp.n
                for mc in range(KC):
                    b, ld = load_slab(dtn["wv_b"][mc])

                    def prv(ld=ld, b=b):
                        P.wait("tensor", sca[b], ld)

                    def evvv(bank, mc=mc):
                        P.vec(lambda v, bank=bank, mc=mc: v.tensor_tensor(
                            out=sc1[:], in0=bank[:], in1=cm(kbuf, mc), op=ALU.mult))
                        P.vec(lambda v, mc=mc: v.tensor_tensor_scan(
                            out=cm(kbuf, mc), data0=sc1[:], data1=sc1[:],
                            initial=carry[:, mc:mc + 1], op0=ALU.add, op1=ALU.bypass))
                        P.vec(lambda v, mc=mc: v.tensor_copy(
                            out=carry[:, mc:mc + 1], in_=cm(kbuf, mc)[:, NT - 1:NT]),
                            inc=(sv, 1))
                    mm_group([xnt[:, kc * NT:(kc + 1) * NT] for kc in range(KC)],
                             [wsl[b][:, kc * 128:(kc + 1) * 128] for kc in range(KC)],
                             evvv, bank_pair=0, pre=prv)
                    wsl_mark[b] = sp.n
                for mc in range(KC):
                    b, ld = load_slab(dtn["wq_b"][mc])

                    def prq(ld=ld, b=b):
                        P.wait("tensor", sca[b], ld)

                    def evqq(bank, mc=mc):
                        P.vec(lambda v, bank=bank, mc=mc: v.tensor_tensor(
                            out=sc1[:], in0=bank[:], in1=cm(kbuf, mc), op=ALU.mult))
                        P.vec(lambda v, mc=mc: v.tensor_scalar(
                            out=cm(vhb, mc), in0=sc1[:], scalar1=DENOM1, scalar2=None,
                            op0=ALU.mult), inc=(sv, 1))
                    mm_group([xnt[:, kc * NT:(kc + 1) * NT] for kc in range(KC)],
                             [wsl[b][:, kc * 128:(kc + 1) * 128] for kc in range(KC)],
                             evqq, bank_pair=0, pre=prq)
                    wsl_mark[b] = sp.n
                for mc in range(KC):
                    b, ld = load_slab(dtn["wo_b"][mc])
                    P.wait("sync", sv)
                    dma_s(sc4[:], chunk_ap(x1d, mc, tau), sdx[1])
                    xld = sdx[1].n

                    def pro(ld=ld, b=b):
                        P.wait("tensor", sca[b], ld)
                        P.wait("tensor", sv)

                    def evo(bank, mc=mc, tau=tau, xld=xld):
                        P.wait("vector", sdx[1], xld)
                        P.wait("vector", sdo[0])
                        P.vec(lambda v, bank=bank: v.tensor_tensor(
                            out=sc3[:], in0=bank[:], in1=sc4[:], op=ALU.add), inc=(sv, 1))
                        P.wait("sync", sv)
                        dma_s(chunk_ap(dtn["acc_scr"], mc, tau), sc3[:], sdo[0])
                    mm_group([cm(vhb, kc) for kc in range(KC)],
                             [wsl[b][:, kc * 128:(kc + 1) * 128] for kc in range(KC)],
                             evo, bank_pair=0, pre=pro)
                    wsl_mark[b] = sp.n

        def layer1_dense_gu(ctx, din):
            scb = sb(ctx, "scb", [128, NT], BF16)
            wga = [sb(ctx, f"wga{i}", [128, KC * 128], BF16) for i in range(2)]
            wgb = [sb(ctx, f"wgb{i}", [128, KC * 128], BF16) for i in range(2)]
            for tau in range(NTAU):
                rsb = rs_b[:, tau * NT:(tau + 1) * NT]
                mrb = mr_b[:, tau * NT:(tau + 1) * NT]
                rrb = rrms_b[:, tau * NT:(tau + 1) * NT]

                def consume(c, view, b, ld, tau=tau, rsb=rsb, mrb=mrb, rrb=rrb):
                    P.wait("vector", sdx[b], ld)
                    P.vec(lambda v, view=view, rsb=rsb: v.tensor_tensor(
                        out=sc1[:], in0=view, in1=rsb, op=ALU.mult))
                    P.vec(lambda v, mrb=mrb: v.tensor_tensor(
                        out=sc1[:], in0=sc1[:], in1=mrb, op=ALU.subtract))
                    P.vec(lambda v, rrb=rrb: v.tensor_tensor(
                        out=sc1[:], in0=sc1[:], in1=rrb, op=ALU.mult))
                    P.vec(lambda v, c=c, tau=tau: v.tensor_scalar(
                        out=cm(din, c, tau), in0=sc1[:], scalar1=pcol("pre1", c),
                        scalar2=None, op0=ALU.mult), inc=(sv, 1))
                stream4(x1d, tau, consume, xbL1[0],
                        extra=lambda: P.wait("sync", sdo[0]))
            din_rdy = sv.n
            gu_mark = [sp.n, sp.n]
            for mc in range(64):
                bg = mc % 2
                P.wait("scalar", sp, gu_mark[bg])
                load_slab_b(wga[bg], dtn["wg1_b"][mc], bg)
                ld = load_slab_b(wgb[bg], dtn["wu1_b"][mc], bg)
                for tau in range(NTAU):
                    def prg(ld=ld, bg=bg, mc=mc, tau=tau):
                        P.wait("tensor", sca[bg], ld)
                        if mc == 0 and tau == 0:
                            P.wait("tensor", sv, din_rdy)

                    def evg1(bank):
                        P.wait("scalar", sp)
                        P.wait("scalar", sv)
                        P.act(lambda s, bank=bank: s.activation(
                            out=sc2[:], in_=bank[:], func=AF.Gelu_apprx_tanh))

                    def evu1(bank, mc=mc, tau=tau):
                        P.wait("vector", sa)
                        P.wait("vector", sdo[1])
                        P.vec(lambda v, bank=bank: v.tensor_tensor(
                            out=scb[:], in0=sc2[:], in1=bank[:], op=ALU.mult), inc=(sv, 1))
                        P.wait("sync", sv)
                        dma_s(chunk_ap(dtn["h1_scr"], mc, tau), scb[:], sdo[1])
                    mm_group([cm(din, kc, tau) for kc in range(KC)],
                             [wga[bg][:, kc * 128:(kc + 1) * 128] for kc in range(KC)],
                             evg1, bank_pair=0, pre=prg)
                    mm_group([cm(din, kc, tau) for kc in range(KC)],
                             [wgb[bg][:, kc * 128:(kc + 1) * 128] for kc in range(KC)],
                             evu1, bank_pair=1)
                gu_mark[bg] = sp.n

        def layer1_dense_down(ctx, rdacc):
            h1t = sb(ctx, "h1t", [128, 64 * NT], BF16)
            wdl = [sb(ctx, f"wdl1_{i}", [128, 64 * 128], BF16) for i in range(2)]
            wd_mark = [0, 0]
            h1_mark = [0]
            P.vec(lambda v: v.memset(rdacc[:], 0.0), inc=(sv, 1))
            for tau in range(NTAU):
                P.wait("sync", sp, h1_mark[0])
                P.wait("sync", sdo[1])
                dma_s(h1t[:].rearrange("p (c j) -> p c j", j=NT),
                      bass.AP(dtn["h1_scr"].tensor, tau * NT,
                              [[T, 128], [128 * T, 64], [1, NT]]), sdx[0])
                h1ld = sdx[0].n
                for hm in range(KC):
                    b = hm % 2
                    P.wait("scalar", sp, wd_mark[b])
                    ld = load_slab_b(wdl[b], dtn["wd1_b"][hm], b)

                    def prd(ld=ld, b=b, h1ld=h1ld):
                        P.wait("tensor", sca[b], ld)
                        P.wait("tensor", sdx[0], h1ld)

                    def evd1(bank, hm=hm, tau=tau):
                        P.wait("vector", sdo[0])
                        P.vec(lambda v, bank=bank: v.tensor_copy(out=sc3[:], in_=bank[:]),
                              inc=(sv, 1))
                        P.wait("sync", sv)
                        dma_s(chunk_ap(dtn["dno_scr"], hm, tau), sc3[:], sdo[0])
                        P.wait("scalar", sv)
                        P.act(lambda s: s.activation(out=sc2[:], in_=sc3[:], func=AF.Square))
                        P.wait("tensor", sa)
                        P.wait("tensor", sv)
                        P.pe(lambda t: t.matmul(pr1[:], onesk_s[:], sc2[:],
                                                start=True, stop=True), inc=(sp, 1))
                        P.wait("vector", sp)
                        P.vec(lambda v, tau=tau: v.tensor_tensor(
                            out=rdacc[0:1, tau * NT:(tau + 1) * NT],
                            in0=rdacc[0:1, tau * NT:(tau + 1) * NT],
                            in1=pr1[:], op=ALU.add), inc=(sv, 1))
                    mm_group([h1t[:, kc * NT:(kc + 1) * NT] for kc in range(64)],
                             [wdl[b][:, kc * 128:(kc + 1) * 128] for kc in range(64)],
                             evd1, bank_pair=0, pre=prd)
                    wd_mark[b] = sp.n
                h1_mark[0] = sp.n

        def build_affb(e, affb):
            for tau in range(NTAU):
                for j in range(4):
                    tc = tau * 4 + j
                    P.wait("tensor", sv)
                    P.pe(lambda t, tc=tc, j=j, e=e: t.transpose(
                        pt[0:1, j * 128:(j + 1) * 128],
                        raff[:, tc * 8 + e:tc * 8 + e + 1], ident_s[:]),
                        inc=(sp, 1))
                P.wait("vector", sp)
                P.vec(lambda v: v.tensor_copy(out=r3[:], in_=pt[0:1, 0:NT]), inc=(sv, 1))
                bcast(r3[:], affb[:, tau * NT:(tau + 1) * NT])

        def layer1_moe_he(ctx, moein):
            scb = sb(ctx, "scb2", [128, NT], BF16)
            affb = sb(ctx, "affb", [128, NTAU * NT])
            wga = [sb(ctx, f"wea{i}", [128, KC * 128], BF16) for i in range(2)]
            wgb = [sb(ctx, f"web{i}", [128, KC * 128], BF16) for i in range(2)]
            for tau in range(NTAU):
                rsb = rs_b[:, tau * NT:(tau + 1) * NT]
                mrb = mr_b[:, tau * NT:(tau + 1) * NT]
                rrb = rrms_b[:, tau * NT:(tau + 1) * NT]

                def consume(c, view, b, ld, tau=tau, rsb=rsb, mrb=mrb, rrb=rrb):
                    P.wait("vector", sdx[b], ld)
                    P.vec(lambda v, view=view, rsb=rsb: v.tensor_tensor(
                        out=sc1[:], in0=view, in1=rsb, op=ALU.mult))
                    P.vec(lambda v, mrb=mrb: v.tensor_tensor(
                        out=sc1[:], in0=sc1[:], in1=mrb, op=ALU.subtract))
                    P.vec(lambda v, rrb=rrb: v.tensor_tensor(
                        out=sc1[:], in0=sc1[:], in1=rrb, op=ALU.mult))
                    P.vec(lambda v, c=c, tau=tau: v.tensor_scalar(
                        out=cm(moein, c, tau), in0=sc1[:], scalar1=pcol("pre2", c),
                        scalar2=None, op0=ALU.mult), inc=(sv, 1))
                stream4(x1d, tau, consume, xbL1[0],
                        extra=lambda: P.wait("sync", sdo[0]))
            mo_rdy = sv.n
            gu_mark = [sp.n, sp.n]
            for e in range(E):
                build_affb(e, affb)
                for mc in range(8):
                    bg = mc % 2
                    P.wait("scalar", sp, gu_mark[bg])
                    load_slab_b(wga[bg], dtn["weg_b"][e, mc], bg)
                    ld = load_slab_b(wgb[bg], dtn["weu_b"][e, mc], bg)
                    for tau in range(NTAU):
                        def prg(ld=ld, bg=bg, e=e, mc=mc, tau=tau):
                            P.wait("tensor", sca[bg], ld)
                            if e == 0 and mc == 0 and tau == 0:
                                P.wait("tensor", sv, mo_rdy)

                        def evg2(bank):
                            P.wait("scalar", sp)
                            P.wait("scalar", sv)
                            P.act(lambda s, bank=bank: s.activation(
                                out=sc2[:], in_=bank[:], func=AF.Gelu_apprx_tanh))

                        def evu2(bank, e=e, mc=mc, tau=tau):
                            P.wait("vector", sa)
                            P.wait("vector", sdo[1])
                            P.vec(lambda v, bank=bank: v.tensor_tensor(
                                out=sc1[:], in0=sc2[:], in1=bank[:], op=ALU.mult))
                            P.vec(lambda v, tau=tau: v.tensor_tensor(
                                out=scb[:], in0=sc1[:],
                                in1=affb[:, tau * NT:(tau + 1) * NT], op=ALU.mult),
                                inc=(sv, 1))
                            P.wait("sync", sv)
                            dma_s(chunk_ap(dtn["hew_scr"], e * 8 + mc, tau), scb[:], sdo[1])
                        mm_group([cm(moein, kc, tau) for kc in range(KC)],
                                 [wga[bg][:, kc * 128:(kc + 1) * 128] for kc in range(KC)],
                                 evg2, bank_pair=0, pre=prg)
                        mm_group([cm(moein, kc, tau) for kc in range(KC)],
                                 [wgb[bg][:, kc * 128:(kc + 1) * 128] for kc in range(KC)],
                                 evu2, bank_pair=1)
                    gu_mark[bg] = sp.n

        def layer1_moe_down(ctx, rdacc, rmacc, rsacc):
            hewt = sb(ctx, "hewt", [128, 64 * NT], BF16)
            wdl = [sb(ctx, f"wdl2_{i}", [128, 64 * 128], BF16) for i in range(2)]
            wd_mark = [0, 0]
            hw_mark = [0]
            P.vec(lambda v: v.memset(rmacc[:], 0.0))
            P.vec(lambda v: v.memset(rsacc[:], 0.0), inc=(sv, 1))
            for tau in range(NTAU):
                P.wait("sync", sp, hw_mark[0])
                P.wait("sync", sdo[1])
                dma_s(hewt[:].rearrange("p (c j) -> p c j", j=NT),
                      bass.AP(dtn["hew_scr"].tensor, tau * NT,
                              [[T, 128], [128 * T, 64], [1, NT]]), sdx[0])
                hld = sdx[0].n
                for hm in range(KC):
                    b = hm % 2
                    P.wait("scalar", sp, wd_mark[b])
                    ld = load_slab_b(wdl[b], dtn["wed_b"][hm], b)

                    def prd(ld=ld, b=b, hld=hld):
                        P.wait("tensor", sca[b], ld)
                        P.wait("tensor", sdx[0], hld)

                    def evd2(bank, hm=hm, tau=tau):
                        P.wait("vector", sdo[0])
                        P.vec(lambda v, bank=bank: v.tensor_copy(out=sc3[:], in_=bank[:]),
                              inc=(sv, 1))
                        P.wait("sync", sv)
                        dma_s(chunk_ap(dtn["mo_scr"], hm, tau), sc3[:], sdo[0])
                        P.wait("scalar", sv)
                        P.act(lambda s: s.activation(out=sc2[:], in_=sc3[:], func=AF.Square))
                        P.wait("tensor", sa)
                        P.wait("tensor", sv)
                        P.pe(lambda t: t.matmul(pr1[:], onesk_s[:], sc2[:],
                                                start=True, stop=True), inc=(sp, 1))
                        P.wait("vector", sp)
                        P.vec(lambda v, tau=tau: v.tensor_tensor(
                            out=rmacc[0:1, tau * NT:(tau + 1) * NT],
                            in0=rmacc[0:1, tau * NT:(tau + 1) * NT],
                            in1=pr1[:], op=ALU.add), inc=(sv, 1))
                    mm_group([hewt[:, kc * NT:(kc + 1) * NT] for kc in range(64)],
                             [wdl[b][:, kc * 128:(kc + 1) * 128] for kc in range(64)],
                             evd2, bank_pair=0, pre=prd)
                    wd_mark[b] = sp.n
                hw_mark[0] = sp.n
                if tau > 0:
                    layer1_final_tau(rdacc, rmacc, rsacc, tau - 1)
            layer1_final_tau(rdacc, rmacc, rsacc, NTAU - 1)

        def rms_row_bcast_def():
            def rms_row_bcast(accrow, tau, dst):
                P.vec(lambda v, accrow=accrow, tau=tau: v.tensor_scalar(
                    out=r2[:], in0=accrow[0:1, tau * NT:(tau + 1) * NT],
                    scalar1=1.0 / D, scalar2=None, op0=ALU.mult), inc=(sv, 1))
                P.wait("scalar", sv)
                P.act(lambda s: s.activation(out=r3[:], in_=r2[:], func=AF.Sqrt,
                                             bias=eps_s[0:1, 1:2]), inc=(sa, 1))
                P.wait("vector", sa)
                P.vec(lambda v: v.reciprocal(out=r2[:], in_=r3[:]), inc=(sv, 1))
                bcast(r2[:], dst)
            return rms_row_bcast

        def layer1_final_tau(rdacc, rmacc, rsacc, tau):
            rms_row_bcast = rms_row_bcast_def()
            xba, xbb = xbL1[0]
            if True:
                rms_row_bcast(rdacc, tau, bcA[:])
                rms_row_bcast(rmacc, tau, bcB[:])
                for q in range(4):
                    P.wait("sync", sv)
                    P.wait("sync", sa)
                    P.wait("sync", sp)
                    P.wait("sync", sdo[0])
                    dma_s(xba[:].rearrange("p (c j) -> p c j", j=NT),
                          chunk4_ap(dtn["dno_scr"], q, tau), sdx[0])
                    lda = sdx[0].n
                    dma_s(xbb[:].rearrange("p (c j) -> p c j", j=NT),
                          chunk4_ap(dtn["mo_scr"], q, tau), sdx[1])
                    ldb = sdx[1].n
                    for j in range(4):
                        c = 4 * q + j
                        va = xba[:, j * NT:(j + 1) * NT]
                        vb = xbb[:, j * NT:(j + 1) * NT]
                        P.wait("vector", sdx[0], lda)
                        P.wait("vector", sdx[1], ldb)
                        P.wait("vector", sa)
                        P.vec(lambda v, va=va: v.tensor_tensor(out=sc1[:], in0=va, in1=bcA[:],
                                                               op=ALU.mult))
                        P.vec(lambda v, c=c: v.tensor_scalar(
                            out=sc1[:], in0=sc1[:], scalar1=pcol("post1", c), scalar2=None,
                            op0=ALU.mult))
                        P.vec(lambda v, vb=vb: v.tensor_tensor(out=sc2[:], in0=vb, in1=bcB[:],
                                                               op=ALU.mult))
                        P.vec(lambda v, c=c: v.tensor_scalar(
                            out=sc2[:], in0=sc2[:], scalar1=pcol("post2", c), scalar2=None,
                            op0=ALU.mult))
                        P.wait("vector", sdo[0])
                        P.vec(lambda v: v.tensor_tensor(out=sc5[:], in0=sc1[:], in1=sc2[:],
                                                        op=ALU.add), inc=(sv, 1))
                        P.wait("sync", sv)
                        dma_s(chunk_ap(dtn["s_scr"], c, tau), sc5[:], sdo[0])
                        P.wait("scalar", sv)
                        P.act(lambda s: s.activation(out=sc2[:], in_=sc5[:], func=AF.Square))
                        P.wait("tensor", sa)
                        P.wait("tensor", sv)
                        P.pe(lambda t: t.matmul(pr1[:], onesk_s[:], sc2[:],
                                                start=True, stop=True), inc=(sp, 1))
                        P.wait("vector", sp)
                        P.vec(lambda v, tau=tau: v.tensor_tensor(
                            out=rsacc[0:1, tau * NT:(tau + 1) * NT],
                            in0=rsacc[0:1, tau * NT:(tau + 1) * NT],
                            in1=pr1[:], op=ALU.add), inc=(sv, 1))
            if True:
                rms_row_bcast(rsacc, tau, bcA[:])
                for q in range(4):
                    P.wait("sync", sv)
                    P.wait("sync", sp)
                    P.wait("sync", sdo[0])
                    dma_s(xba[:].rearrange("p (c j) -> p c j", j=NT),
                          chunk4_ap(dtn["s_scr"], q, tau), sdx[0])
                    lda = sdx[0].n
                    dma_s(xbb[:].rearrange("p (c j) -> p c j", j=NT),
                          chunk4_ap(dtn["acc_scr"], q, tau), sdx[1])
                    ldb = sdx[1].n
                    for j in range(4):
                        c = 4 * q + j
                        va = xba[:, j * NT:(j + 1) * NT]
                        vb = xbb[:, j * NT:(j + 1) * NT]
                        P.wait("vector", sdx[0], lda)
                        P.wait("vector", sdx[1], ldb)
                        P.vec(lambda v, va=va: v.tensor_tensor(out=sc1[:], in0=va, in1=bcA[:],
                                                               op=ALU.mult))
                        P.vec(lambda v, c=c: v.tensor_scalar(
                            out=sc1[:], in0=sc1[:], scalar1=pcol("post", c), scalar2=None,
                            op0=ALU.mult))
                        P.wait("vector", sdo[0])
                        P.vec(lambda v, vb=vb: v.tensor_tensor(out=sc5[:], in0=sc1[:], in1=vb,
                                                               op=ALU.add), inc=(sv, 1))
                        P.wait("sync", sv)
                        dma_s(chunk_ap(dtn["out"], c, tau), sc5[:], sdo[0])

        load_params()
        preconvert()
        if 0 in layers:
            with ExitStack() as c0:
                big_a = sb(c0, "big_a", [128, KC * T], BF16)
                with ExitStack() as mctx:
                    ghost_mat(mctx)
                    with ExitStack() as c1:
                        layer0_stats_h(c1, big_a)
                    for _s2 in (sv, sdw[0], sdw[1], sdo[0], ccs):
                        for _e in ("vector", "sync", "scalar", "tensor"):
                            P.wait(_e, _s2)
                with ExitStack() as c2:
                    big_b = sb(c2, "big_b", [128, KC * T], BF16)
                    with ExitStack() as c3:
                        layer0_attn(c3, big_a, big_b)
            with ExitStack() as c4:
                layer0_ffn(c4)
        elif 1 in layers:
            phase_barrier(exclude=(gdw[0],))
        if 1 in layers:
            phase_barrier()
            with ExitStack() as l1a:
                rdacc = sb(l1a, "rdacc", [1, T])
                rmacc = sb(l1a, "rmacc", [1, T])
                xbL1[0] = [sb(l1a, "xb1a", [128, 4 * NT]), sb(l1a, "xb1b", [128, 4 * NT])]
                layer1_stats()
                with ExitStack() as l1b:
                    layer1_router(l1b)
                phase_barrier()
                with ExitStack() as l1c:
                    layer1_attn(l1c)
                phase_barrier()
                with ExitStack() as l1d:
                    din = sb(l1d, "din", [128, KC * T], BF16)
                    layer1_dense_gu(l1d, din)
                phase_barrier()
                with ExitStack() as l1e:
                    layer1_dense_down(l1e, rdacc)
                phase_barrier()
                with ExitStack() as l1f:
                    moein = sb(l1f, "moein", [128, KC * T], BF16)
                    layer1_moe_he(l1f, moein)
                phase_barrier()
                with ExitStack() as l1g:
                    rsacc = sb(l1g, "rsacc", [1, T])
                    layer1_moe_down(l1g, rdacc, rmacc, rsacc)

        for _sem in [smisc, sdx[0], sdx[1], sdw[0], sdw[1], sdo[0], sdo[1], sdd[0], sdd[1]]:
            P.wait("sync", _sem)
        for _sem in [sca[0], sca[1], scp[0], scp[1], scp[2], scp[3],
                     sco[0], sco[1], sco[2], sco[3]]:
            P.wait("scalar", _sem)
        for _sem in [gdw[0], gdw[1], gdd[0], gdd[1], ccs]:
            P.wait("gpsimd", _sem)

        block = octx.enter_context(nc.Block())

        @block.sync
        def _(eng):
            for f in P.ops["sync"]:
                f(eng)

        @block.gpsimd
        def _(eng):
            for f in P.ops["gpsimd"]:
                f(eng)

        @block.vector
        def _(eng):
            for f in P.ops["vector"]:
                f(eng)

        @block.scalar
        def _(eng):
            for f in P.ops["scalar"]:
                f(eng)

        @block.tensor
        def _(eng):
            for f in P.ops["tensor"]:
                f(eng)

    return nc



def prep_inputs_l0(inputs):
    perm = np.concatenate([np.arange(0, 1024, 2), np.arange(1, 1024, 2),
                           np.arange(1024, 2048)])
    qkvperm = np.concatenate([perm, 2048 + perm, 4096 + perm])
    W_in = np.asarray(inputs["L0_attn_W_in"], dtype=np.float32)
    b_in = np.asarray(inputs["L0_attn_b_in"], dtype=np.float32)
    W_out = np.asarray(inputs["L0_attn_W_out"], dtype=np.float32)
    d = {}
    d["win_sl"] = to_slabs(np.ascontiguousarray(W_in[:, qkvperm]), KC, 48)
    d["bin_pp"] = col_pp(b_in[qkvperm], 48)
    d["lnw_pp"] = col_pp(inputs["L0_attn_ln_w"], KC)
    d["lnb_pp"] = col_pp(inputs["L0_attn_ln_b"], KC)
    d["wout_sl"] = to_slabs(np.ascontiguousarray(W_out[perm, :]), KC, KC)
    d["bout_pp"] = col_pp(inputs["L0_attn_b_out"], KC)
    d["lut_row"] = np.asarray(inputs["L0_ffn_lut"], np.float32).reshape(1, 16)
    gT = np.ascontiguousarray(np.asarray(inputs["L0_ffn_gate_idx"], np.int32).T)
    uT = np.ascontiguousarray(np.asarray(inputs["L0_ffn_up_idx"], np.int32).T)
    dT = np.ascontiguousarray(np.asarray(inputs["L0_ffn_down_idx"], np.int32).T)
    gu = np.concatenate([to_slabs(gT, KC, MC0), to_slabs(uT, KC, MC0)], axis=0)
    d["_gu88"] = np.concatenate([gu, gu[0:2]], axis=0)
    d["_d16"] = to_slabs(dT, MC0, KC)
    i = np.arange(0, 1024, 2, dtype=np.float64) / 1024.0
    inv_freq = 1.0 / (10000.0 ** i)
    s = np.arange(T, dtype=np.float64)
    ang = s[None, :] * inv_freq[:, None]
    cosT = np.cos(ang).astype(np.float32).reshape(4, 128, T)
    sinT = np.sin(ang).astype(np.float32).reshape(4, 128, T)
    cs = np.concatenate([cosT, sinT], axis=0)
    d["cs_pp"] = np.ascontiguousarray(cs.transpose(1, 0, 2))
    return d


def prep_inputs_l1(inputs):
    d = {}
    for nm, key in [("wq", "L1_attn_Wq"), ("wk", "L1_attn_Wk"), ("wv", "L1_attn_Wv"),
                    ("wo", "L1_attn_Wo")]:
        W = np.asarray(inputs[key], np.float32)
        d[nm + "_sl"] = to_slabs(np.ascontiguousarray(W.T), KC, KC)
    for nm, key in [("pre1", "L1_ffn_rn_pre1"), ("pre2", "L1_ffn_rn_pre2"),
                    ("post1", "L1_ffn_rn_post1"), ("post2", "L1_ffn_rn_post2"),
                    ("post", "L1_ffn_rn_post"), ("rsc", "L1_ffn_router_scale")]:
        d[nm + "_pp"] = col_pp(inputs[key], KC)
    rw = np.asarray(inputs["L1_ffn_router_w"], np.float32)
    d["rw_sl"] = np.ascontiguousarray(
        rw.reshape(KC, 128, 8).transpose(1, 0, 2).reshape(1, 128, KC * 8))
    d["pes_row"] = np.asarray(inputs["L1_ffn_per_expert_scale"], np.float32).reshape(1, E)
    d["wg1_sl"] = to_slabs(np.asarray(inputs["L1_ffn_w_gate"], np.float32), KC, 64)
    d["wu1_sl"] = to_slabs(np.asarray(inputs["L1_ffn_w_up"], np.float32), KC, 64)
    d["wd1_sl"] = to_slabs(np.asarray(inputs["L1_ffn_w_down"], np.float32), 64, KC)
    weg = np.asarray(inputs["L1_ffn_we_gate"], np.float32)
    weu = np.asarray(inputs["L1_ffn_we_up"], np.float32)
    wed = np.asarray(inputs["L1_ffn_we_down"], np.float32)
    d["weg_sl"] = np.stack([to_slabs(weg[e], KC, 8) for e in range(E)])
    d["weu_sl"] = np.stack([to_slabs(weu[e], KC, 8) for e in range(E)])
    wed_cat = np.concatenate([wed[e] for e in range(E)], axis=0)
    d["wed_sl"] = to_slabs(np.ascontiguousarray(wed_cat), 64, KC)
    return d


def kernel(**inputs):
    X = np.asarray(inputs["L0_attn_X"], np.float32)
    common = prep_inputs_l0(inputs)
    common.update(prep_inputs_l1(inputs))
    nc = build_program(layers=(0, 1))
    gu88 = common.pop("_gu88")
    d16 = common.pop("_d16")
    in_maps = []
    for b in range(8):
        m = dict(common)
        m["x0"] = np.ascontiguousarray(X[b].T)
        m["gu_my"] = np.ascontiguousarray(gu88[11 * b:11 * b + 11])
        m["d_my"] = np.ascontiguousarray(d16[2 * b:2 * b + 2])
        in_maps.append(m)
    res = run_bass_kernel_spmd(nc, in_maps, core_ids=list(range(8)))
    out = np.stack([np.ascontiguousarray(r["out"].T) for r in res.results])
    return out
